# version 05
# baseline (speedup 1.0000x reference)
import numpy as np
import ml_dtypes
from contextlib import ExitStack

import concourse.bass as bass
import concourse.mybir as mybir
from concourse.bass_utils import run_bass_kernel_spmd

F32 = mybir.dt.float32
BF16 = mybir.dt.bfloat16
I32 = mybir.dt.int32
I16 = mybir.dt.int16
AF = mybir.ActivationFunctionType
ALU = mybir.AluOpType

B, S, D = 4, 2048, 2048
NKV, DH, NPAIR = 8, 64, 4
MLP = 16384
NSL = 2048
NT = 16
DC = 16
TOWN = 1024
NCHUNK = 8
CT = 1024
W = 8
EXT = (8, 16)
LN_EPS = 1e-5
EXP_SHIFT = -16.0
GC2 = 2.0 * 0.7978845608028654
GC1 = GC2 * 0.044715
P126 = float(2.0 ** 126)

ENGINES = ("sync", "tensor", "vector", "scalar", "gpsimd")
DEBUG_HOOK = None
BUILD_PHASE = "full"
BUILD_DECODE = True
BUILD_DBG = False


class Sched:
    def __init__(self):
        self.ops = {e: [] for e in ENGINES}
        self.cnt = {}

    def op(self, eng, fn, waits=(), inc=None, amt=1):
        waits = [w for w in waits if w is not None]
        self.ops[eng].append((fn, tuple(waits), inc, amt))
        if inc is not None:
            self.cnt[inc] = self.cnt.get(inc, 0) + amt
            return self.cnt[inc]
        return None

    def dma(self, eng, fn, waits=()):
        sem = {"sync": "DL", "gpsimd": "DG"}[eng]
        return self.op(eng, fn, waits, inc=sem, amt=16)

    def n(self, sem):
        return self.cnt.get(sem, 0)


def build_program():
    nc = bass.Bass("TRN2", target_bir_lowering=False)

    def din(name, shape, dt):
        return nc.dram_tensor(name, shape, dt, kind="ExternalInput").ap()

    x_full = din("x_full", [S, D], F32)
    x_own = din("x_own", [TOWN, D], F32)
    wq_in = din("wq", [32, D, DH], F32)
    wk_in = din("wk", [NKV, D, DH], F32)
    wv_in = din("wv", [NKV, D, DH], F32)
    wo_in = din("wo", [NKV * DH, D], F32)
    gpk = din("gpk", [NSL, 1024], I32)
    gsc = din("gsc", [NSL, 64], F32)
    upk = din("upk", [NSL, 1024], I32)
    usc = din("usc", [NSL, 64], F32)
    dpk_in = din("dpk", [D, 1024], I32)
    dsc_in = din("dsc", [D, 64], F32)
    masks_in = din("masks", [16, 128, 512], BF16)
    ident_in = din("ident", [128, 128], BF16)
    y_out = nc.dram_tensor("y", [TOWN, D], F32, kind="ExternalOutput").ap()

    wg_raw = nc.dram_tensor("wg_raw", [NSL, D], BF16).ap()
    wu_raw = nc.dram_tensor("wu_raw", [NSL, D], BF16).ap()
    wd_raw = nc.dram_tensor("wd_raw", [D, NSL], BF16).ap()
    wg_dec = nc.dram_tensor("wg_dec", [NT, DC, 128, 128], BF16).ap()
    wu_dec = nc.dram_tensor("wu_dec", [NT, DC, 128, 128], BF16).ap()
    wd_dec = nc.dram_tensor("wd_dec", [NT, 128, D], BF16).ap()
    ag_in = [nc.dram_tensor(f"ag_in{j}", [D, 128], BF16).ap() for j in range(NCHUNK)]
    ag_out = [nc.dram_tensor(f"ag_out{j}", [W * D, 128], BF16).ap() for j in range(NCHUNK)]
    rs_in = [nc.dram_tensor(f"rs_in{j}", [CT, D], F32).ap() for j in range(NCHUNK)]
    rs_out = [nc.dram_tensor(f"rs_out{j}", [128, D], F32).ap() for j in range(NCHUNK)]
    sent_in = nc.dram_tensor("sent_in", [8, 8], F32).ap()
    sent_out = nc.dram_tensor("sent_out", [64, 8], F32).ap()
    sent_in2 = nc.dram_tensor("sent_in2", [8, 8], F32).ap()
    sent_out2 = nc.dram_tensor("sent_out2", [64, 8], F32).ap()
    dummy_d = nc.dram_tensor("dummy_d", [8, 8], F32).ap()
    xmid_sp = nc.dram_tensor("xmid_sp", [TOWN, D], F32).ap()
    fsum_sp = nc.dram_tensor("fsum_sp", [TOWN, D], F32).ap()

    E = Sched()
    stack = ExitStack()

    def sb(name, shape, dt):
        return stack.enter_context(nc.sbuf_tensor(name, shape, dt))

    def psum_t(name, shape, dt):
        return stack.enter_context(nc.psum_tensor(name, shape, dt))

    idn = sb("idn", [128, 128], BF16)
    AR2 = sb("ar2", [128, 30848], BF16)
    xtbuf = sb("xtbuf", [128, 16384], BF16)
    AR3 = sb("ar3", [128, 16384], BF16)
    wkvp = sb("wkvp", [128, 16384], BF16)
    estage = sb("estage", [128, 4096], F32)
    xbfst = sb("xbfst", [128, 4096], BF16)
    dpk_sb = sb("dpkb", [128, 2048], I32)
    dec2 = sb("dec2", [128, 4096], I16)
    dw_sb = sb("dw", [128, 2048], BF16)
    dscl = sb("dscl", [128, 128], F32)
    dsx = sb("dsx", [128, 128], F32)
    bc_sb = sb("bc", [64, 512], F32)
    rcp_sb = sb("rcp", [1, 512], F32)
    ones1 = sb("ones1", [1, 64], F32)
    biasS = sb("biasS", [128, 1], F32)
    stats = sb("stats", [128, 128], F32)

    masks_v = AR2.ap()[:, 0:8192]
    qt_v = AR2.ap()[:, 8192:12288]
    kt_v = AR2.ap()[:, 12288:20480]
    v_v = AR2.ap()[:, 20480:28800]
    pt_v = AR2.ap()[:, 28800:30848]
    wqp = AR3.ap()[:, 0:8192]
    zt_v = AR3.ap()[:, 8192:12288]
    xmb_v = AR3.ap()[:, 12288:16384]
    AR3f = AR3.ap().bitcast(F32)
    osq_scr = AR3f[:, 6144:8192]
    kp_v = wkvp.ap()[:, 0:8192]
    vp_v = wkvp.ap()[:, 8192:16384]
    eA = estage.ap()[:, 0:2048]
    eB = estage.ap()[:, 2048:4096]
    wst = [estage.ap()[:, 1024 * i:1024 * (i + 1)] for i in range(4)]
    wst3 = [w.rearrange("p (c k) -> p c k", k=64) for w in wst]
    xbff = xbfst.ap().bitcast(F32)
    dpkf = dpk_sb.ap().bitcast(F32)
    dec2bf = dec2.ap().bitcast(BF16)
    dec2f = dec2.ap().bitcast(F32)
    xm_v = xtbuf.ap()
    inter_v = wkvp.ap()
    wd_v = AR2.ap()[:, 0:16384]
    wgs_v = AR3.ap()[:, 0:4096]
    wus_v = AR3.ap()[:, 4096:8192]
    rsst_v = AR3f[:, 4096:8192]
    gg_v = xbff[:, 0:1024]
    g2_v = xbff[:, 1024:2048]
    t1_v = estage.ap()[:, 0:1024]
    p4a = dpkf[:, 0:2048]
    p4b = dec2f[:, 0:2048]

    PS = [psum_t(f"ps{i}", [128, 512], F32) for i in range(5)]
    PTB = [psum_t("pst0", [128, 1024], BF16), psum_t("pst1", [128, 1024], BF16)]
    BC = psum_t("psb", [64, 512], F32)

    sems = {}
    for s in ("DL", "DG", "CC", "PE", "AC", "DV", "PL", "FN", "FG"):
        sems[s] = stack.enter_context(nc.semaphore(s))

    def fence(eng="sync"):
        dsem = "DL" if eng == "sync" else "DG"
        fsem = "FN" if eng == "sync" else "FG"
        v = E.n(dsem)
        E.cnt[fsem] = E.cnt.get(fsem, 0) + 1
        fv = E.cnt[fsem]
        E.ops[eng].append((lambda e, h=sems[fsem]: e.sem_inc(h, 1), ((dsem, v),), None, 0))
        return fv

    E.op("vector", lambda e: e.memset(biasS[:, :], EXP_SHIFT), inc="DV")
    E.op("vector", lambda e: e.memset(ones1[:, :], 1.0), inc="DV")
    E.op("vector", lambda e: e.memset(
        v_v.rearrange("p (t g c) -> p t g c", t=16, g=8)[:, :, :, 64:65], 1.0), inc="DV")
    E.op("vector", lambda e: e.memset(stats.ap()[:, 8:16], 0.0), inc="DV")
    E.op("vector", lambda e: e.memset(stats.ap()[:, 72:80], 0.0), inc="DV")
    E.dma("sync", lambda e: e.dma_start(out=idn[:], in_=ident_in[:, :]))
    for i in range(16):
        E.dma("sync", lambda e, i=i: e.dma_start(
            out=masks_v[:, i * 512:(i + 1) * 512], in_=masks_in[i, :, :]))
    f_init = fence()

    dec_pl = {}
    dec_raw = {}
    dec_nib = {}
    dec_state = {"k": 0}
    dec_srcs = (
        [(gpk[128 * t:128 * (t + 1), :], gsc[128 * t:128 * (t + 1), :],
          wg_raw[128 * t:128 * (t + 1), :]) for t in range(16)]
        + [(upk[128 * t:128 * (t + 1), :], usc[128 * t:128 * (t + 1), :],
            wu_raw[128 * t:128 * (t + 1), :]) for t in range(16)]
        + [(dpk_in[128 * t:128 * (t + 1), :], dsc_in[128 * t:128 * (t + 1), :],
            wd_raw[128 * t:128 * (t + 1), :]) for t in range(16)]
    )

    def decode_next(n):
        if not BUILD_DECODE:
            return
        for _ in range(n):
            k = dec_state["k"]
            if k >= 48:
                return
            dec_state["k"] = k + 1
            pk_src, sc_src, raw_dst = dec_srcs[k]
            bi = k % 2
            pk_v = dpk_sb.ap()[:, bi * 1024:(bi + 1) * 1024]
            sc_v = dscl.ap()[:, bi * 64:(bi + 1) * 64]
            sx_v = dsx.ap()[:, bi * 64:(bi + 1) * 64]
            if k >= 1:
                _, _, prev_dst = dec_srcs[k - 1]
                dec_raw[k - 1] = E.dma("sync", lambda e, d=prev_dst: e.dma_start(
                    out=d, in_=dw_sb[:]), waits=[("PL", dec_pl[k - 1])])
            war = [("DV", dec_nib[k - 2])] if k >= 2 else []
            E.dma("sync", lambda e, o=pk_v, s=pk_src: e.dma_start(out=o, in_=s), waits=war)
            csc = E.dma("sync", lambda e, o=sc_v, s=sc_src: e.dma_start(out=o, in_=s))
            fdec = fence()
            w0 = [("FN", fdec)]
            if k >= 1:
                w0.append(("PL", dec_pl[k - 1]))
            cnib = E.op("vector", lambda e, i2=pk_v: e.tensor_copy(
                dec2[:, 2048:3072], i2), waits=w0, inc="DV")
            dec_nib[k] = cnib
            d1 = E.op("vector", lambda e: e.tensor_scalar(
                dec2[:, 0:1024], dec2[:, 2048:3072], 15, None, ALU.bitwise_and),
                waits=[("DV", cnib)], inc="DV")
            d2 = E.op("vector", lambda e: e.tensor_scalar(
                dec2[:, 1024:2048], dec2[:, 2048:3072], 4, None, ALU.logical_shift_right),
                inc="DV")
            d3 = E.op("vector", lambda e: e.tensor_scalar(
                dec2[:, 2048:4096], dec2[:, 0:2048], 8, 12, ALU.bitwise_and, ALU.logical_shift_left),
                waits=[("DV", d2)], inc="DV")
            d4 = E.op("vector", lambda e: e.tensor_scalar(
                dec2[:, 0:2048], dec2[:, 0:2048], 7, 6, ALU.bitwise_and, ALU.logical_shift_left),
                waits=[("DV", d3)], inc="DV")
            E.op("vector", lambda e, sv=sc_v, xv=sx_v: e.tensor_scalar(
                xv, sv, P126, None, ALU.mult))
            cdv = E.op("vector", lambda e: e.tensor_tensor(
                dec2[:, 0:2048], dec2[:, 0:2048], dec2[:, 2048:4096], ALU.bitwise_or),
                waits=[("DV", d4)], inc="DV")
            mb4 = dec2bf[:, 0:2048].rearrange("p (h a b) -> p h a b", h=2, b=16)
            wv4 = dw_sb.ap().rearrange("p (h a b) -> p h a b", h=2, b=16)
            sx4 = sx_v.unsqueeze(1).unsqueeze(-1).broadcast_to([128, 2, 64, 16])
            plw = [("DV", cdv), ("FN", fdec)]
            dec_pl[k] = E.op("gpsimd", lambda e, a=wv4, b=mb4, c=sx4: e.tensor_tensor(
                a, b, c, ALU.mult), waits=plw, inc="PL")

    def decode_flush():
        if not BUILD_DECODE:
            return
        decode_next(48 - dec_state["k"])
        k = 47
        _, _, dst = dec_srcs[k]
        dec_raw[k] = E.dma("sync", lambda e, d=dst: e.dma_start(
            out=d, in_=dw_sb[:]), waits=[("PL", dec_pl[k])])

    def strided_copy(dst4, src3, waits, inc=None):
        return E.op("vector", lambda e, d=dst4, s2=src3: e.tensor_copy(d, s2),
                    waits=waits, inc=inc)


    def wk_ap(a, dc):
        return kp_v[:, a * 2048 + dc * 128: a * 2048 + (dc + 1) * 128]

    def wv_ap(dc):
        return vp_v[:, dc * 512:(dc + 1) * 512]

    pt_state = {"i": 0}
    pt_hist = []

    def transpose_batch(items, pe_waits, ev_waits=()):
        b = pt_state["i"]
        pt_state["i"] += 1
        T = PTB[b % 2].ap()
        w = list(pe_waits)
        if b >= 2:
            w.append(("DV", pt_hist[b - 2]))
        cpes = []
        for si, (in_ap, out_ap) in enumerate(items):
            pv = T[:, si * 128:(si + 1) * 128]
            cpe = E.op("tensor", lambda e, p=pv, i2=in_ap: e.transpose(p, i2, idn[:]),
                       waits=w if si == 0 else (), inc="PE")
            cpes.append(cpe)
        last_dv = None
        for si, (in_ap, out_ap) in enumerate(items):
            pv = T[:, si * 128:(si + 1) * 128]
            last_dv = E.op("vector", lambda e, p=pv, o=out_ap: e.tensor_copy(o, p),
                           waits=[("PE", cpes[-1])] + (list(ev_waits) if si == 0 else []),
                           inc="DV")
        pt_hist.append(last_dv)
        return cpes[-1], last_dv

    xt_hist = []

    def build_xt_tile(src_rows, seq, xt_col0):
        bi = seq % 2
        xb = xbfst.ap()[:, bi * 2048:(bi + 1) * 2048]
        war = [("PE", xt_hist[-2])] if len(xt_hist) >= 2 else []
        cl = E.dma("gpsimd", lambda e, o=xb, s=src_rows: e.dma_start(out=o, in_=s),
                   waits=war)
        fxt = fence("gpsimd")
        last_pe = None
        last_dv = None
        for g in range(DC // 4):
            items = [(xb[:, (4 * g + u) * 128:(4 * g + u + 1) * 128],
                      xtbuf.ap()[:, (4 * g + u) * 1024 + xt_col0:
                                 (4 * g + u) * 1024 + xt_col0 + 128]) for u in range(4)]
            last_pe, last_dv = transpose_batch(
                items, [("FG", fxt), ("FN", f_init)] if g == 0 else [])
        xt_hist.append(last_pe)
        return last_dv

    bank_free = {i: None for i in range(6)}

    def mm(bank, lhsT, rhs, start, stop, waits=(), inc=False, out_ap=None):
        out = out_ap if out_ap is not None else PS[bank].ap()[:, :]
        return E.op("tensor",
                    lambda e, o=out, l=lhsT, r=rhs, st=start, sp=stop:
                    e.matmul(o, l, r, start=st, stop=sp, skip_group_check=True),
                    waits=waits, inc="PE" if inc else None)

    xt_seq = 0
    xtq_done = None
    for i in range(8):
        xtq_done = build_xt_tile(x_own[128 * i:128 * (i + 1), :], xt_seq, i * 128)
        xt_seq += 1
        decode_next(1)
    dv_prev = None
    decode_next(2)
    for a in range(NPAIR):
        srcs = [wq_in[8 * a + qh].rearrange("(c p) k -> p c k", p=128) for qh in range(8)]
        l0 = [None] * 8
        for qh in range(4):
            wts = []
            if qh == 0 and dv_prev:
                wts = [("DV", dv_prev)]
            l0[qh] = E.dma("sync", lambda e, s=srcs[qh], d=wst3[qh]: e.dma_start(out=d, in_=s),
                           waits=wts)
        fq1 = fence()
        c01 = E.op("vector", lambda e: e.tensor_tensor(wst[0], wst[0], wst[1], ALU.add),
                   waits=[("FN", fq1)], inc="DV")
        c23 = E.op("vector", lambda e: e.tensor_tensor(wst[2], wst[2], wst[3], ALU.add),
                   inc="DV")
        c02 = E.op("vector", lambda e: e.tensor_tensor(wst[0], wst[0], wst[2], ALU.add),
                   waits=[("DV", c23)], inc="DV")
        l4 = E.dma("sync", lambda e, s=srcs[4], d=wst3[1]: e.dma_start(out=d, in_=s),
                   waits=[("DV", c01)])
        l5 = E.dma("sync", lambda e, s=srcs[5], d=wst3[2]: e.dma_start(out=d, in_=s),
                   waits=[("DV", c02)])
        l6 = E.dma("sync", lambda e, s=srcs[6], d=wst3[3]: e.dma_start(out=d, in_=s))
        fq2 = fence()
        c45 = E.op("vector", lambda e: e.tensor_tensor(wst[1], wst[1], wst[2], ALU.add),
                   waits=[("FN", fq2)], inc="DV")
        l7 = E.dma("sync", lambda e, s=srcs[7], d=wst3[2]: e.dma_start(out=d, in_=s),
                   waits=[("DV", c45)])
        fq3 = fence()
        c46 = E.op("vector", lambda e: e.tensor_tensor(wst[1], wst[1], wst[3], ALU.add),
                   waits=[("DV", c45)], inc="DV")
        c47 = E.op("vector", lambda e: e.tensor_tensor(wst[1], wst[1], wst[2], ALU.add),
                   waits=[("FN", fq3), ("DV", c46)], inc="DV")
        dst = wqp.rearrange("p (a c two k) -> p a c two k", a=4, two=2, k=64)[:, a]
        strided_copy(dst[:, :, 0, :], wst3[0], [("DV", c02)])
        dv_prev = strided_copy(dst[:, :, 1, :], wst3[1], [("DV", c47)], inc="DV")
        decode_next(2)
    wq_done = dv_prev

    for a in range(NPAIR):
        lds = []
        for h in range(2):
            src = wk_in[2 * a + h].rearrange("(c p) k -> p c k", p=128)
            lds.append(E.dma("sync", lambda e, s=src, d=wst3[h]: e.dma_start(out=d, in_=s),
                             waits=[("DV", dv_prev)] if h == 0 else []))
        fkv = fence()
        dst = kp_v[:, a * 2048:(a + 1) * 2048].rearrange("p (c two k) -> p c two k", two=2, k=64)
        strided_copy(dst[:, :, 0, :], wst3[0], [("FN", fkv)])
        dv_prev = strided_copy(dst[:, :, 1, :], wst3[1], [], inc="DV")
    decode_next(2)
    for a in range(NPAIR):
        lds = []
        for h in range(2):
            src = wv_in[2 * a + h].rearrange("(c p) k -> p c k", p=128)
            lds.append(E.dma("sync", lambda e, s=src, d=wst3[h]: e.dma_start(out=d, in_=s),
                             waits=[("DV", dv_prev)] if h == 0 else []))
        fkv = fence()
        dst = vp_v.rearrange("p (c a two k) -> p c a two k", a=4, two=2, k=64)[:, :, a]
        strided_copy(dst[:, :, 0, :], wst3[0], [("FN", fkv)])
        dv_prev = strided_copy(dst[:, :, 1, :], wst3[1], [], inc="DV")
    wkv_done = dv_prev
    decode_next(2)
    q_last_ac = None
    for a in range(NPAIR):
        for sl in range(2):
            wts = []
            if a == 0 and sl == 0:
                wts = [("DV", xtq_done), ("DV", wq_done)]
            if bank_free[sl]:
                wts.append(bank_free[sl])
            for dcx in range(DC):
                c = mm(sl, wqp[:, a * 2048 + dcx * 128: a * 2048 + (dcx + 1) * 128],
                       xtbuf.ap()[:, dcx * 1024 + sl * 512: dcx * 1024 + sl * 512 + 512],
                       dcx == 0, dcx == DC - 1,
                       waits=wts if dcx == 0 else (), inc=(dcx == DC - 1))
            q_last_ac = E.op("scalar", lambda e, a=a, sl=sl, b=sl: e.activation(
                qt_v[:, a * 1024 + sl * 512: a * 1024 + sl * 512 + 512],
                PS[b].ap()[:, :], AF.Copy), waits=[("PE", c)], inc="AC")
            bank_free[sl] = ("AC", q_last_ac)
    decode_next(2)

    kv_last_ac = None
    for half in range(2):
        xt_done = None
        for i in range(8):
            t = 8 * half + i
            xt_done = build_xt_tile(x_full[128 * t:128 * (t + 1), :], xt_seq, i * 128)
            xt_seq += 1
            decode_next(1)
        for a in range(NPAIR):
            for kc2 in range(2):
                kc = 2 * half + kc2
                bank = kc2
                wts = [("DV", xt_done)]
                if half == 0 and a == 0 and kc2 == 0:
                    wts.append(("DV", wkv_done))
                if bank_free[bank]:
                    wts.append(bank_free[bank])
                for dcx in range(DC):
                    c = mm(bank, wk_ap(a, dcx),
                           xtbuf.ap()[:, dcx * 1024 + kc2 * 512: dcx * 1024 + kc2 * 512 + 512],
                           dcx == 0, dcx == DC - 1,
                           waits=wts if dcx == 0 else (), inc=(dcx == DC - 1))
                kv_last_ac = E.op("scalar", lambda e, a=a, kc=kc, b=bank: e.activation(
                    kt_v[:, a * 2048 + kc * 512: a * 2048 + kc * 512 + 512],
                    PS[b].ap()[:, :], AF.Copy), waits=[("PE", c)], inc="AC")
                bank_free[bank] = ("AC", kv_last_ac)
        for i in range(8):
            tt = 8 * half + i
            bank = 2 + (i % 2)
            wts = [("DV", xt_done)]
            if bank_free[bank]:
                wts.append(bank_free[bank])
            for dcx in range(DC):
                c = mm(bank, xtbuf.ap()[:, dcx * 1024 + i * 128: dcx * 1024 + i * 128 + 128],
                       wv_ap(dcx), dcx == 0, dcx == DC - 1,
                       waits=wts if dcx == 0 else (), inc=(dcx == DC - 1))
            vout = v_v.rearrange("p (t g c) -> p t g c", t=16, g=8)[:, tt, :, 0:64]
            vin = PS[bank].ap().rearrange("p (g c) -> p g c", g=8)
            kv_last_ac = E.op("scalar", lambda e, o=vout, i2=vin: e.activation(o, i2, AF.Copy),
                              waits=[("PE", c)], inc="AC")
            bank_free[bank] = ("AC", kv_last_ac)
        decode_next(4)

    wo_dv = None
    for a in range(NPAIR):
        wts = [("DV", wo_dv)] if wo_dv else [("AC", q_last_ac), ("DV", wkv_done)]
        c = E.dma("sync", lambda e, s=wo_in[128 * a:128 * (a + 1), :]:
                  e.dma_start(out=eA, in_=s), waits=wts)
        fwo = fence()
        wo_dv = E.op("vector", lambda e, a=a: e.tensor_copy(
            wqp[:, a * 2048:(a + 1) * 2048], eA), waits=[("FN", fwo)], inc="DV")
    wo_done = wo_dv
    decode_flush()

    xbar_hist = []
    pe_snapshot = E.n("PE")

    def emit_xbar(src_ap, dst_ap, wait_dl):
        k = len(xbar_hist)
        bi = k % 2
        tv = xbfst.ap()[:, bi * 2048:(bi + 1) * 2048]
        war = [("DL", xbar_hist[k - 2])] if k >= 2 else [("PE", pe_snapshot)]
        ct = E.dma("sync", lambda e, o=tv, s=src_ap: e.dma_start_transpose(o, s),
                   waits=[("DL", wait_dl)] + war)
        cw = E.dma("sync", lambda e, d=dst_ap, s=tv: e.dma_start(out=d, in_=s),
                   waits=[("DL", ct)])
        xbar_hist.append(cw)
        return cw

    if BUILD_DECODE:
        for dcx in range(DC):
            emit_xbar(wg_raw[:, 128 * dcx:128 * (dcx + 1)],
                      wg_dec.rearrange("nt dc d j -> dc d nt j")[dcx], dec_raw[15])
        for dcx in range(DC):
            emit_xbar(wu_raw[:, 128 * dcx:128 * (dcx + 1)],
                      wu_dec.rearrange("nt dc d j -> dc d nt j")[dcx], dec_raw[31])
    if BUILD_DECODE:
        for ntk in range(NT):
            emit_xbar(wd_raw[:, 128 * ntk:128 * (ntk + 1)], wd_dec[ntk], dec_raw[47])
        wg_ready = xbar_hist[DC - 1]
        wu_ready = xbar_hist[2 * DC - 1]
        wd_ready = xbar_hist[-1]
    else:
        wg_ready = wu_ready = wd_ready = 0

    pt_war = {0: None, 1: None, 2: None, 3: None}
    bc_cbc = None
    bc_cev = None
    zn_last = None
    for a in range(NPAIR):
        for sl in range(2):
            ext = EXT[sl]
            for h in range(2):
                last_av = None
                for kt in range(ext):
                    sbank = kt % 4
                    wts = []
                    if a == 0 and sl == 0 and kt == 0 and h == 0:
                        wts.append(("AC", kv_last_ac))
                        wts.append(("DV", 5))
                    if bank_free[sbank]:
                        wts.append(bank_free[sbank])
                    c = mm(sbank,
                           kt_v[64 * h:64 * (h + 1),
                                a * 2048 + kt * 128: a * 2048 + (kt + 1) * 128],
                           qt_v[64 * h:64 * (h + 1),
                                a * 1024 + sl * 512: a * 1024 + sl * 512 + 512],
                           True, True, waits=wts, inc=True)
                    slot = kt % 4
                    pv = pt_v[:, slot * 512:(slot + 1) * 512]
                    ew = [("PE", c)]
                    if pt_war[slot]:
                        ew.append(("PE", pt_war[slot]))
                    ce = E.op("scalar", lambda e, o=pv, b=sbank: e.activation(
                        o, PS[b].ap()[:, :], AF.Exp, scale=0.125, bias=biasS[:, :]),
                        waits=ew, inc="AC")
                    bank_free[sbank] = ("AC", ce)
                    if (sl == 0) or (kt >= 8):
                        mv = masks_v[:, kt * 512:(kt + 1) * 512]
                        cm = E.op("vector", lambda e, p=pv, m=mv: e.tensor_tensor(
                            p, p, m, ALU.mult),
                            waits=[("AC", ce), ("FN", f_init)], inc="DV")
                        pwait = ("DV", cm)
                    else:
                        pwait = ("AC", ce)
                    vslice = v_v[:, kt * 520 + (2 * a + h) * 65:
                                 kt * 520 + (2 * a + h) * 65 + 65]
                    avw = [pwait]
                    if kt == 0 and bank_free.get(4):
                        avw.append(bank_free[4])
                    cav = mm(None, vslice, pv, kt == 0, kt == ext - 1,
                             waits=avw, inc=True, out_ap=PS[4].ap()[0:65, :])
                    pt_war[slot] = cav
                    last_av = cav
                crw = [("PE", last_av)]
                if bc_cbc:
                    crw.append(("PE", bc_cbc))
                crc = E.op("vector", lambda e: e.reciprocal(
                    rcp_sb[:, :], PS[4].ap()[64:65, :]), waits=crw, inc="DV")
                bcw = [("DV", crc)]
                if bc_cev:
                    bcw.append(("AC", bc_cev))
                bc_cbc = E.op("tensor", lambda e: e.matmul(
                    BC[:, :], ones1[:, :], rcp_sb[:, :], start=True, stop=True,
                    skip_group_check=True), waits=bcw, inc="PE")
                bc_cev = E.op("scalar", lambda e: e.activation(bc_sb[:, :], BC[:, :], AF.Copy),
                              waits=[("PE", bc_cbc)] + ([("DV", zn_last)] if zn_last else []),
                              inc="AC")
                zn_last = E.op("vector", lambda e, a=a, sl=sl, h=h: e.tensor_tensor(
                    zt_v[64 * h:64 * (h + 1),
                         a * 1024 + sl * 512: a * 1024 + sl * 512 + 512],
                    PS[4].ap()[0:64, :], bc_sb[:, :], ALU.mult),
                    waits=[("AC", bc_cev)], inc="DV")
                bank_free[4] = ("DV", zn_last)
    z_done = zn_last

    sum_hist = []
    sq_hist = []
    spill_hist = []
    for i in range(8):
        bi = i % 2
        xo = eA if bi == 0 else eB
        war = [("DV", wo_done)]
        if i >= 2:
            war = [("DL", spill_hist[i - 2]), ("DV", sum_hist[i - 2]),
                   ("AC", sq_hist[i - 2])]
        cl = E.dma("sync", lambda e, o=xo, s=x_own[128 * i:128 * (i + 1), :]:
                   e.dma_start(out=o, in_=s), waits=war)
        fxo = fence()
        cadd_last = None
        for dq in range(4):
            wts = []
            if i == 0 and dq == 0:
                wts = [("DV", z_done), ("DV", wo_done)]
            if bank_free[dq]:
                wts.append(bank_free[dq])
            for a in range(NPAIR):
                c = mm(dq, zt_v[:, a * 1024 + i * 128: a * 1024 + (i + 1) * 128],
                       wqp[:, a * 2048 + dq * 512: a * 2048 + dq * 512 + 512],
                       a == 0, a == NPAIR - 1,
                       waits=wts if a == 0 else (), inc=(a == NPAIR - 1))
            cadd_last = E.op("vector", lambda e, o=xo, dq=dq, b=dq: e.tensor_tensor(
                o[:, dq * 512:(dq + 1) * 512], PS[b].ap()[:, :],
                o[:, dq * 512:(dq + 1) * 512], ALU.add),
                waits=[("PE", c), ("FN", fxo)], inc="DV")
            bank_free[dq] = ("DV", cadd_last)
        cs = E.op("vector", lambda e, x=xo, i=i: e.tensor_reduce(
            stats.ap()[:, i:i + 1], x, axis=mybir.AxisListType.X, op=ALU.add),
            waits=[("DV", cadd_last)], inc="DV")
        sum_hist.append(cs)
        cq = E.op("scalar", lambda e, x=xo, i=i: e.activation(
            osq_scr, x, AF.Square, accum_out=stats.ap()[:, 8 + i:9 + i]),
            waits=[("DV", cadd_last)], inc="AC")
        sq_hist.append(cq)
        spill_hist.append(E.dma("sync", lambda e, x=xo, i=i: e.dma_start(
            out=xmid_sp[128 * i:128 * (i + 1), :], in_=x), waits=[("DV", cs), ("AC", cq)]))

    if BUILD_PHASE == "oproj":
        outl = None
        for i in range(8):
            outl = E.dma("sync", lambda e, i=i: e.dma_start(
                out=y_out[128 * i:128 * (i + 1), :],
                in_=xmid_sp[128 * i:128 * (i + 1), :]),
                waits=[("DL", spill_hist[i])])
        E.ops["sync"].append((None, (("DL", outl),), None, 0))
        with nc.Block() as block:
            def make(engname):
                def emitter(eng):
                    last = {}
                    for fn, waits, inc, amt in E.ops[engname]:
                        for s2, v in waits:
                            if v is None or v == 0:
                                continue
                            if last.get(s2, -1) < v:
                                eng.wait_ge(sems[s2], v)
                                last[s2] = v
                        if fn is None:
                            continue
                        inst = fn(eng)
                        if inc is not None:
                            inst.then_inc(sems[inc], amt)
                return emitter
            block.sync(make("sync"))
            block.tensor(make("tensor"))
            block.vector(make("vector"))
            block.scalar(make("scalar"))
            block.gpsimd(make("gpsimd"))
        stack.close()
        return nc

    def ln_panels(c0, pw):
        st_ = stats.ap()
        w = pw
        x1 = E.op("vector", lambda e: e.tensor_scalar(
            st_[:, c0 + 16:c0 + 24], st_[:, c0 + 0:c0 + 8], 1.0 / D, None, ALU.mult),
            waits=w, inc="DV")
        x2 = E.op("vector", lambda e: e.tensor_scalar(
            st_[:, c0 + 24:c0 + 32], st_[:, c0 + 8:c0 + 16], 1.0 / D, None, ALU.mult),
            inc="DV")
        x3 = E.op("vector", lambda e: e.tensor_tensor(
            st_[:, c0 + 32:c0 + 40], st_[:, c0 + 16:c0 + 24], st_[:, c0 + 16:c0 + 24],
            ALU.mult), waits=[("DV", x1)], inc="DV")
        x4 = E.op("vector", lambda e: e.tensor_tensor(
            st_[:, c0 + 40:c0 + 48], st_[:, c0 + 24:c0 + 32], st_[:, c0 + 32:c0 + 40],
            ALU.subtract), waits=[("DV", x3)], inc="DV")
        x5 = E.op("vector", lambda e: e.tensor_scalar(
            st_[:, c0 + 40:c0 + 48], st_[:, c0 + 40:c0 + 48], LN_EPS, None, ALU.add),
            waits=[("DV", x4)], inc="DV")
        x6 = E.op("scalar", lambda e: e.activation(
            st_[:, c0 + 48:c0 + 56], st_[:, c0 + 40:c0 + 48], AF.Sqrt),
            waits=[("DV", x5)], inc="AC")
        x7 = E.op("vector", lambda e: e.reciprocal(
            st_[:, c0 + 32:c0 + 40], st_[:, c0 + 48:c0 + 56]),
            waits=[("AC", x6)], inc="DV")
        x8 = E.op("vector", lambda e: e.tensor_tensor(
            st_[:, c0 + 56:c0 + 64], st_[:, c0 + 16:c0 + 24], st_[:, c0 + 32:c0 + 40],
            ALU.mult), waits=[("DV", x7)], inc="DV")
        x9 = E.op("vector", lambda e: e.tensor_scalar(
            st_[:, c0 + 56:c0 + 64], st_[:, c0 + 56:c0 + 64], -1.0, None, ALU.mult),
            waits=[("DV", x8)], inc="DV")
        return x9

    ln1_dv = ln_panels(0, [("AC", sq_hist[-1]), ("DV", sum_hist[-1])])

    ag_cc = []
    ag_cag = []
    ap_hist = []
    for i in range(8):
        bi = i % 2
        xs = eA if bi == 0 else eB
        xb16 = xmb_v[:, bi * 2048:(bi + 1) * 2048]
        xmt = dec2bf[:, bi * 2048:(bi + 1) * 2048]
        if i < 2:
            war = [("DL", spill_hist[-1]), ("AC", sq_hist[-1]), ("DV", sum_hist[-1])]
        else:
            war = [("AC", ap_hist[i - 2][2]), ("DL", ap_hist[i - 2][3])]
        cl = E.dma("sync", lambda e, o=xs, s=xmid_sp[128 * i:128 * (i + 1), :]:
                   e.dma_start(out=o, in_=s), waits=war)
        fap = fence()
        aw = [("FN", fap), ("DV", ln1_dv)]
        if i >= 2:
            aw.append(("PE", ap_hist[i - 2][6]))
        cbf = E.op("scalar", lambda e, o=xb16, x=xs, i=i: e.activation(
            o, x, AF.Identity, scale=stats.ap()[:, 32 + i:33 + i],
            bias=stats.ap()[:, 56 + i:57 + i]), waits=aw, inc="AC")
        cf32 = E.op("scalar", lambda e, x=xs, i=i: e.activation(
            x, x, AF.Identity, scale=stats.ap()[:, 32 + i:33 + i],
            bias=stats.ap()[:, 56 + i:57 + i]), inc="AC")
        csp = E.dma("sync", lambda e, x=xs, i=i: e.dma_start(
            out=xmid_sp[128 * i:128 * (i + 1), :], in_=x), waits=[("AC", cf32)])
        last_pe = None
        last_dv = None
        for g in range(4):
            items = []
            for u in range(4):
                pc = 4 * g + u
                if pc < 8:
                    src = xb16[:, 256 * pc: 256 * pc + 256: 2]
                else:
                    src = xb16[:, 256 * (pc - 8) + 1: 256 * (pc - 8) + 256: 2]
                items.append((src, xmt[:, pc * 128:(pc + 1) * 128]))
            evw = []
            if g == 0:
                evw = [("PL", dec_pl[47])] if BUILD_DECODE else []
                if i >= 2:
                    evw.append(("FN", ap_hist[i - 2][5]))
            last_pe, last_dv = transpose_batch(
                items, [("AC", cbf)] if g == 0 else [], ev_waits=evw)
        for q4 in range(4):
            cag = E.dma("sync", lambda e, i=i, q4=q4, s=xmt: e.dma_start(
                out=ag_in[i].rearrange("(pc d) t -> d pc t", d=128)[:, 4 * q4:4 * (q4 + 1), :],
                in_=s.rearrange("p (pc t) -> p pc t", t=128)[:, 4 * q4:4 * (q4 + 1), :]),
                waits=[("DV", last_dv)] if q4 == 0 else [])
        ag_cag.append(fence())
        ap_hist.append((cl, cbf, cf32, csp, last_dv, ag_cag[-1], last_pe))

    if BUILD_PHASE == "attn":
        outl = None
        for i in range(8):
            outl = E.dma("sync", lambda e, i=i: e.dma_start(
                out=y_out[128 * i:128 * (i + 1), :],
                in_=xmid_sp[128 * i:128 * (i + 1), :]),
                waits=[("DL", ap_hist[i][3])])
        E.ops["sync"].append((None, (("DL", outl),), None, 0))
        with nc.Block() as block:
            def make(engname):
                def emitter(eng):
                    last = {}
                    for fn, waits, inc, amt in E.ops[engname]:
                        for s2, v in waits:
                            if v is None or v == 0:
                                continue
                            if last.get(s2, -1) < v:
                                eng.wait_ge(sems[s2], v)
                                last[s2] = v
                        if fn is None:
                            continue
                        inst = fn(eng)
                        if inc is not None:
                            inst.then_inc(sems[inc], amt)
                return emitter
            block.sync(make("sync"))
            block.tensor(make("tensor"))
            block.vector(make("vector"))
            block.scalar(make("scalar"))
            block.gpsimd(make("gpsimd"))
        stack.close()
        return nc

    if BUILD_PHASE == "full":
        for i in range(NCHUNK):
            ccw = [("FN", ag_cag[i])]
            ag_cc.append(E.op("gpsimd", lambda e, i=i: e.collective_compute(
                "AllGather", ALU.bypass, replica_groups=[list(range(W))],
                ins=[ag_in[i].opt()], outs=[ag_out[i].opt()]),
                waits=ccw, inc="CC"))
        sent1 = E.op("gpsimd", lambda e: e.collective_compute(
            "AllGather", ALU.bypass, replica_groups=[list(range(W))],
            ins=[sent_in.opt()], outs=[sent_out.opt()]),
            waits=[("CC", ag_cc[-1])], inc="CC")
        sent2 = E.op("gpsimd", lambda e: e.collective_compute(
            "AllGather", ALU.bypass, replica_groups=[list(range(W))],
            ins=[sent_in2.opt()], outs=[sent_out2.opt()]),
            waits=[("CC", sent1)], inc="CC")
    else:
        sent2 = 0

    p4_hist = []

    def emit_p4a(j):
        wts = [("CC", rs_cc[j])]
        if p4_hist:
            wts += [("DL", p4_hist[-1][2]), ("AC", p4_hist[-1][3])]
        c1 = E.dma("sync", lambda e, j=j: e.dma_start(out=p4a, in_=rs_out[j][:, :]),
                   waits=wts)
        c2 = E.dma("sync", lambda e, j=j: e.dma_start(
            out=p4b, in_=xmid_sp[128 * j:128 * (j + 1), :]),
            waits=([("DV", p4_hist[-1][4])] if p4_hist else []) + [("DL", ap_hist[j][3])])
        fp4 = fence()
        cadd = E.op("vector", lambda e: e.tensor_tensor(p4a, p4a, p4b, ALU.add),
                    waits=[("FN", fp4)], inc="DV")
        cs = E.op("vector", lambda e, j=j: e.tensor_reduce(
            stats.ap()[:, 64 + j:65 + j], p4a, axis=mybir.AxisListType.X, op=ALU.add),
            waits=[("DV", cadd)], inc="DV")
        cq = E.op("scalar", lambda e, j=j: e.activation(
            eB, p4a, AF.Square, accum_out=stats.ap()[:, 72 + j:73 + j]),
            waits=[("DV", cadd)], inc="AC")
        c3 = E.dma("sync", lambda e, j=j: e.dma_start(
            out=fsum_sp[128 * j:128 * (j + 1), :], in_=p4a),
            waits=[("DV", cs), ("AC", cq)])
        p4_hist.append((c1, c2, c3, cq, cs))

    gate_pe = {}
    up_pe = {}
    t1_pl = {}
    inter_dv = {}
    down_pe = {}
    rs_dl = {}
    rs_cc = []
    rs_fence = {}
    rs_chunk_fence = {}
    wg_dl = {}
    wu_dl = {}
    wd_dl = {}
    xm_dl = {}

    for j in range(NCHUNK):
        if j >= 2:
            emit_p4a(j - 2)
        for r in range(W):
            wts = []
            if r == 0:
                wts.append(("CC", ag_cc[j]))
                if j > 0:
                    wts.append(("PE", up_pe[(j * NT - 1, 1)]))
            src = ag_out[j].rearrange("(r dc d) t -> r d dc t", r=W, d=128)[r]
            dst = xm_v.rearrange("p (dc t) -> p dc t", t=1024)[:, :, 128 * r:128 * (r + 1)]
            for q4 in range(4):
                xm_dl[(j, r)] = E.dma("sync", lambda e, q4=q4, s=src, d=dst: e.dma_start(
                    out=d[:, 4 * q4:4 * (q4 + 1), :], in_=s[:, 4 * q4:4 * (q4 + 1), :]),
                    waits=wts if q4 == 0 else [])
        fxm = fence()
        fgu = {}
        fwd = {}

        def emit_gu_load(nt2):
            gi2 = j * NT + nt2
            bi2 = gi2 % 2
            wts = [("DL", wg_ready)] if j == 0 else []
            if gi2 >= 2:
                wts.append(("PE", gate_pe[(gi2 - 2, 1)]))
            wg_dl[gi2] = E.dma(
                "sync",
                lambda e, o=wgs_v[:, bi2 * 2048:(bi2 + 1) * 2048].rearrange("p (dc j2) -> p dc j2", j2=128),
                s=wg_dec[nt2].rearrange("dc d j2 -> d dc j2"):
                e.dma_start(out=o, in_=s), waits=wts)
            wts = [("DL", wu_ready)] if j == 0 else []
            if gi2 >= 2:
                wts.append(("PE", up_pe[(gi2 - 2, 1)]))
            wu_dl[gi2] = E.dma(
                "sync",
                lambda e, o=wus_v[:, bi2 * 2048:(bi2 + 1) * 2048].rearrange("p (dc j2) -> p dc j2", j2=128),
                s=wu_dec[nt2].rearrange("dc d j2 -> d dc j2"):
                e.dma_start(out=o, in_=s), waits=wts)
            fgu[gi2] = fence()

        def emit_wd_loads(hh):
            for ntk in range(NT):
                wts = [("DL", wd_ready)] if j == 0 else []
                if ntk == 0:
                    if hh == 1:
                        wts.append(("PE", down_pe[(j, 0, 7)]))
                    elif j > 0:
                        wts.append(("PE", down_pe[(j - 1, 1, 7)]))
                wd_dl[(j, hh, ntk)] = E.dma(
                    "sync", lambda e, o=wd_v[:, ntk * 1024:(ntk + 1) * 1024],
                    s=wd_dec[ntk][:, hh * 1024:(hh + 1) * 1024]:
                    e.dma_start(out=o, in_=s), waits=wts)
            fwd[hh] = fence()

        emit_gu_load(0)
        emit_gu_load(1)
        for nt in range(NT):
            gi = j * NT + nt
            bi = gi % 2
            for tq in range(2):
                gbank, ubank = tq, 2 + tq
                prev = (gi - 1, tq)
                wts = [("FN", fgu[gi])]
                if nt == 0 and tq == 0:
                    wts.append(("FN", fxm))
                if bank_free[gbank]:
                    wts.append(bank_free[gbank])
                for dcx in range(DC):
                    c = mm(gbank, wgs_v[:, bi * 2048 + dcx * 128: bi * 2048 + (dcx + 1) * 128],
                           xm_v[:, dcx * 1024 + tq * 512: dcx * 1024 + tq * 512 + 512],
                           dcx == 0, dcx == DC - 1,
                           waits=wts if dcx == 0 else (), inc=(dcx == DC - 1))
                gate_pe[(gi, tq)] = c
                ggv = gg_v[:, tq * 512:(tq + 1) * 512]
                g2v = g2_v[:, tq * 512:(tq + 1) * 512]
                tv = t1_v[:, tq * 512:(tq + 1) * 512]
                ggw = [("PE", c)]
                if prev in t1_pl:
                    ggw.append(("PL", t1_pl[prev]))
                cgg = E.op("scalar", lambda e, o=ggv, b=gbank: e.activation(
                    o, PS[b].ap()[:, :], AF.Copy), waits=ggw, inc="AC")
                bank_free[gbank] = ("AC", cgg)
                cg2 = E.op("scalar", lambda e, o=g2v, i2=ggv: e.activation(
                    o, i2, AF.Square), inc="AC")
                cw1 = E.op("gpsimd", lambda e, o=g2v: e.tensor_scalar(
                    o, o, GC1, GC2, ALU.mult, ALU.add), waits=[("AC", cg2)], inc="PL")
                car = E.op("gpsimd", lambda e, o=g2v, g=ggv: e.tensor_tensor(
                    o, g, o, ALU.mult), inc="PL")
                csg = E.op("scalar", lambda e, o=g2v: e.activation(
                    o, o, AF.Sigmoid), waits=[("PL", car)], inc="AC")
                t1w = [("AC", csg)]
                if j == 0 and nt == 0 and tq == 0 and BUILD_PHASE == "full":
                    t1w.append(("FN", ap_hist[7][5]))
                if prev in inter_dv:
                    t1w.append(("DV", inter_dv[prev]))
                ct1 = E.op("gpsimd", lambda e, o=tv, g=ggv, s2=g2v: e.tensor_tensor(
                    o, g, s2, ALU.mult), waits=t1w, inc="PL")
                t1_pl[(gi, tq)] = ct1
                wts = [("FN", fgu[gi])]
                if bank_free[ubank]:
                    wts.append(bank_free[ubank])
                for dcx in range(DC):
                    c2 = mm(ubank, wus_v[:, bi * 2048 + dcx * 128: bi * 2048 + (dcx + 1) * 128],
                            xm_v[:, dcx * 1024 + tq * 512: dcx * 1024 + tq * 512 + 512],
                            dcx == 0, dcx == DC - 1,
                            waits=wts if dcx == 0 else (), inc=(dcx == DC - 1))
                up_pe[(gi, tq)] = c2
                ivw = [("PL", ct1), ("PE", c2)]
                if j > 0 and nt == 0 and tq == 0:
                    ivw.append(("PE", down_pe[(j - 1, 1, 7)]))
                cin = E.op("vector", lambda e, nt=nt, tq=tq, b=ubank, t=tv: e.tensor_tensor(
                    inter_v[:, nt * 1024 + tq * 512: nt * 1024 + tq * 512 + 512],
                    t, PS[b].ap()[:, :], ALU.mult), waits=ivw, inc="DV")
                inter_dv[(gi, tq)] = cin
                bank_free[ubank] = ("DV", cin)
            if nt + 2 < NT:
                emit_gu_load(nt + 2)
            if nt == 12:
                emit_wd_loads(0)
        inter_done = inter_dv[(j * NT + NT - 1, 1)]
        for hh in range(2):
            if hh == 1:
                emit_wd_loads(1)
            for tt in range(8):
                bk = (tt % 2) * 2
                wts0 = [("DV", inter_done), ("FN", fwd[hh])]
                for hc in range(2):
                    if bank_free[bk + hc]:
                        wts0.append(bank_free[bk + hc])
                first = True
                c = None
                for ntk in range(NT):
                    lhs = inter_v[:, ntk * 1024 + tt * 128: ntk * 1024 + (tt + 1) * 128]
                    for hc in range(2):
                        rhs = wd_v[:, ntk * 1024 + hc * 512: ntk * 1024 + hc * 512 + 512]
                        c = mm(bk + hc, lhs, rhs, ntk == 0, ntk == NT - 1,
                               waits=wts0 if first else (), inc=(ntk == NT - 1))
                        first = False
                down_pe[(j, hh, tt)] = c
                step = hh * 8 + tt
                bi = step % 4
                rv = rsst_v[:, bi * 1024:(bi + 1) * 1024]
                evw = [("PE", c)]
                if rs_fence.get((j, hh, tt)):
                    evw.append(("FN", rs_fence[(j, hh, tt)]))
                E.op("scalar", lambda e, o=rv, b=bk: e.activation(
                    o[:, 0:512], PS[b].ap()[:, :], AF.Copy), waits=evw, inc="AC")
                ca2 = E.op("scalar", lambda e, o=rv, b=bk: e.activation(
                    o[:, 512:1024], PS[b + 1].ap()[:, :], AF.Copy), inc="AC")
                bank_free[bk] = ("AC", ca2)
                bank_free[bk + 1] = ("AC", ca2)
                rs_dl[(j, hh, tt)] = E.dma(
                    "sync", lambda e, j=j, hh=hh, tt=tt, s=rv:
                    e.dma_start(out=rs_in[j][128 * tt:128 * (tt + 1),
                                             hh * 1024:(hh + 1) * 1024], in_=s),
                    waits=[("AC", ca2)])
                if step % 4 == 3:
                    fr = fence()
                    for ahead in range(4):
                        st2 = step + 1 + ahead
                        jj, hh2, tt2 = (j, st2 // 8, st2 % 8) if st2 < 16 else                             (j + 1, (st2 - 16) // 8, (st2 - 16) % 8)
                        rs_fence[(jj, hh2, tt2)] = fr
                    if step == 15:
                        rs_chunk_fence[j] = fr
        rs_cc.append(E.op("gpsimd", lambda e, j=j: e.collective_compute(
            "ReduceScatter", ALU.add, replica_groups=[list(range(W))],
            ins=[rs_in[j].opt()], outs=[rs_out[j].opt()]),
            waits=[("FN", rs_chunk_fence[j])], inc="CC"))

    emit_p4a(NCHUNK - 2)
    emit_p4a(NCHUNK - 1)

    ln2_dv = ln_panels(64, [("AC", p4_hist[-1][3]), ("DV", p4_hist[-1][4])])

    app_prev = None
    out_last = None
    for j in range(NCHUNK):
        wts = [("DL", p4_hist[-1][2]), ("AC", p4_hist[-1][3])] if app_prev is None \
            else [("AC", app_prev[1])]
        c1 = E.dma("sync", lambda e, j=j: e.dma_start(
            out=p4a, in_=fsum_sp[128 * j:128 * (j + 1), :]), waits=wts)
        fp4b = fence()
        aw = [("FN", fp4b), ("DV", ln2_dv)]
        ca = E.op("scalar", lambda e, j=j: e.activation(
            p4b, p4a, AF.Identity, scale=stats.ap()[:, 96 + j:97 + j],
            bias=stats.ap()[:, 120 + j:121 + j]), waits=aw, inc="AC")
        out_last = E.dma("sync", lambda e, j=j: e.dma_start(
            out=y_out[128 * j:128 * (j + 1), :], in_=p4b), waits=[("AC", ca)])
        app_prev = (c1, ca, out_last)

    if BUILD_DBG:
        dbg1 = nc.dram_tensor("dbg_xmid", [TOWN, D], F32, kind="ExternalOutput").ap()
        dbg2 = nc.dram_tensor("dbg_rs", [TOWN, D], F32, kind="ExternalOutput").ap()
        dbg3 = nc.dram_tensor("dbg_fsum", [TOWN, D], F32, kind="ExternalOutput").ap()
        for j in range(NCHUNK):
            out_last = E.dma("sync", lambda e, j=j: e.dma_start(
                out=dbg1[128 * j:128 * (j + 1), :], in_=xmid_sp[128 * j:128 * (j + 1), :]),
                waits=[("DL", out_last)])
            out_last = E.dma("sync", lambda e, j=j: e.dma_start(
                out=dbg2[128 * j:128 * (j + 1), :], in_=rs_out[j][:, :]))
            out_last = E.dma("sync", lambda e, j=j: e.dma_start(
                out=dbg3[128 * j:128 * (j + 1), :], in_=fsum_sp[128 * j:128 * (j + 1), :]))
    E.ops["sync"].append((None, (("DL", out_last),), None, 0))

    if DEBUG_HOOK is not None:
        DEBUG_HOOK(dict(nc=nc, E=E, sems=sems, loc=locals()))

    with nc.Block() as block:
        def make(engname):
            def emitter(eng):
                last = {}
                for fn, waits, inc, amt in E.ops[engname]:
                    for s2, v in waits:
                        if v is None or v == 0:
                            continue
                        if last.get(s2, -1) < v:
                            eng.wait_ge(sems[s2], v)
                            last[s2] = v
                    if fn is None:
                        continue
                    inst = fn(eng)
                    if inc is not None:
                        inst.then_inc(sems[inc], amt)
            return emitter

        block.sync(make("sync"))
        block.tensor(make("tensor"))
        block.vector(make("vector"))
        block.scalar(make("scalar"))
        block.gpsimd(make("gpsimd"))

    stack.close()
    return nc


_PROG_CACHE = {}


def _get_prog():
    if "nc" not in _PROG_CACHE:
        _PROG_CACHE["nc"] = build_program()
    return _PROG_CACHE["nc"]


def _host_inputs(inputs):
    x = np.asarray(inputs["L0_attn_normalized_resid_pre"])
    wq = np.ascontiguousarray(np.asarray(inputs["L0_attn_W_Q"], dtype=np.float32))
    wk = np.ascontiguousarray(np.asarray(inputs["L0_attn_W_K"], dtype=np.float32))
    wv = np.ascontiguousarray(np.asarray(inputs["L0_attn_W_V"], dtype=np.float32))
    wo = np.ascontiguousarray(
        np.asarray(inputs["L0_attn_W_O"], dtype=np.float32).reshape(NKV * DH, D))
    gw = np.asarray(inputs["L0_ffn_gate_W_packed"], dtype=np.int32)
    gs = np.asarray(inputs["L0_ffn_gate_scales"], dtype=np.float32)
    uw = np.asarray(inputs["L0_ffn_up_W_packed"], dtype=np.int32)
    us = np.asarray(inputs["L0_ffn_up_scales"], dtype=np.float32)
    dw = np.asarray(inputs["L0_ffn_down_W_packed"], dtype=np.int32)
    ds = np.asarray(inputs["L0_ffn_down_scales"], dtype=np.float32)
    ident = np.eye(128, dtype=np.float32).astype(ml_dtypes.bfloat16)

    in_maps = []
    for c in range(W):
        b, p = c // 2, c % 2
        qcs = (0, 3) if p == 0 else (1, 2)
        x_full_c = np.ascontiguousarray(x[b], dtype=np.float32)
        x_own_c = np.ascontiguousarray(
            np.concatenate([x[b, 512 * qc:512 * (qc + 1)] for qc in qcs], axis=0),
            dtype=np.float32)
        sl = np.arange(NSL * c, NSL * (c + 1))
        rows = np.concatenate([sl[0::2], sl[1::2]])
        m = np.zeros((16, 128, 512), dtype=np.float32)
        kk = np.arange(128)[:, None]
        qq = np.arange(512)[None, :]
        for kt in range(8):
            m[kt] = (128 * kt + kk <= 512 * qcs[0] + qq)
        for kt in range(8, 16):
            m[kt] = (128 * kt + kk <= 512 * qcs[1] + qq)
        in_maps.append({
            "x_full": x_full_c, "x_own": x_own_c,
            "wq": wq, "wk": wk, "wv": wv, "wo": wo,
            "gpk": np.ascontiguousarray(gw[rows]),
            "gsc": np.ascontiguousarray(gs[rows]),
            "upk": np.ascontiguousarray(uw[rows]),
            "usc": np.ascontiguousarray(us[rows]),
            "dpk": np.ascontiguousarray(dw[:, 1024 * c:1024 * (c + 1)]),
            "dsc": np.ascontiguousarray(ds[:, 64 * c:64 * (c + 1)]),
            "masks": m.astype(ml_dtypes.bfloat16), "ident": ident,
        })
    return in_maps


def kernel(**inputs):
    nc = _get_prog()
    in_maps = _host_inputs(inputs)
    res = run_bass_kernel_spmd(nc, in_maps, core_ids=list(range(W)))
    out = np.empty((B, S, D), dtype=np.float32)
    for c in range(W):
        b, p = c // 2, c % 2
        qcs = (0, 3) if p == 0 else (1, 2)
        yc = res.results[c]["y"]
        out[b, 512 * qcs[0]:512 * (qcs[0] + 1)] = yc[0:512]
        out[b, 512 * qcs[1]:512 * (qcs[1] + 1)] = yc[512:1024]
    return out


if __name__ == "__main__":
    build_program()
    print("program built ok")


# version 06
# speedup vs baseline: 1.0239x; 1.0239x over previous
import numpy as np
import ml_dtypes
from contextlib import ExitStack

import concourse.bass as bass
import concourse.mybir as mybir
from concourse.bass_utils import run_bass_kernel_spmd

F32 = mybir.dt.float32
BF16 = mybir.dt.bfloat16
I32 = mybir.dt.int32
I16 = mybir.dt.int16
AF = mybir.ActivationFunctionType
ALU = mybir.AluOpType

B, S, D = 4, 2048, 2048
NKV, DH, NPAIR = 8, 64, 4
MLP = 16384
NSL = 2048
NT = 16
DC = 16
TOWN = 1024
NCHUNK = 8
CT = 1024
W = 8
EXT = (8, 16)
LN_EPS = 1e-5
EXP_SHIFT = -16.0
GC2 = 2.0 * 0.7978845608028654
GC1 = GC2 * 0.044715
P126 = float(2.0 ** 126)

ENGINES = ("sync", "tensor", "vector", "scalar", "gpsimd")
DEBUG_HOOK = None
BUILD_PHASE = "full"
BUILD_DECODE = True
BUILD_DBG = False


class Sched:
    def __init__(self):
        self.ops = {e: [] for e in ENGINES}
        self.cnt = {}

    def op(self, eng, fn, waits=(), inc=None, amt=1):
        waits = [w for w in waits if w is not None]
        self.ops[eng].append((fn, tuple(waits), inc, amt))
        if inc is not None:
            self.cnt[inc] = self.cnt.get(inc, 0) + amt
            return self.cnt[inc]
        return None

    def dma(self, eng, fn, waits=()):
        sem = {"sync": "DL", "gpsimd": "DG"}[eng]
        return self.op(eng, fn, waits, inc=sem, amt=16)

    def n(self, sem):
        return self.cnt.get(sem, 0)


def build_program():
    nc = bass.Bass("TRN2", target_bir_lowering=False)

    def din(name, shape, dt):
        return nc.dram_tensor(name, shape, dt, kind="ExternalInput").ap()

    x_full = din("x_full", [S, D], F32)
    x_own = din("x_own", [TOWN, D], F32)
    wq_in = din("wq", [32, D, DH], F32)
    wk_in = din("wk", [NKV, D, DH], F32)
    wv_in = din("wv", [NKV, D, DH], F32)
    wo_in = din("wo", [NKV * DH, D], F32)
    gpk = din("gpk", [NSL, 1024], I32)
    gsc = din("gsc", [NSL, 64], F32)
    upk = din("upk", [NSL, 1024], I32)
    usc = din("usc", [NSL, 64], F32)
    dpk_in = din("dpk", [D, 1024], I32)
    dsc_in = din("dsc", [D, 64], F32)
    masks_in = din("masks", [16, 128, 512], BF16)
    ident_in = din("ident", [128, 128], BF16)
    y_out = nc.dram_tensor("y", [TOWN, D], F32, kind="ExternalOutput").ap()

    wg_raw = nc.dram_tensor("wg_raw", [NSL, D], BF16).ap()
    wu_raw = nc.dram_tensor("wu_raw", [NSL, D], BF16).ap()
    wd_raw = nc.dram_tensor("wd_raw", [D, NSL], BF16).ap()
    wg_dec = nc.dram_tensor("wg_dec", [NT, DC, 128, 128], BF16).ap()
    wu_dec = nc.dram_tensor("wu_dec", [NT, DC, 128, 128], BF16).ap()
    wd_dec = nc.dram_tensor("wd_dec", [NT, 128, D], BF16).ap()
    ag_in = [nc.dram_tensor(f"ag_in{j}", [D, 128], BF16).ap() for j in range(NCHUNK)]
    ag_out = [nc.dram_tensor(f"ag_out{j}", [W * D, 128], BF16).ap() for j in range(NCHUNK)]
    rs_in = [nc.dram_tensor(f"rs_in{j}", [CT, D], F32).ap() for j in range(NCHUNK)]
    rs_out = [nc.dram_tensor(f"rs_out{j}", [128, D], F32).ap() for j in range(NCHUNK)]
    sent_in = nc.dram_tensor("sent_in", [8, 8], F32).ap()
    sent_out = nc.dram_tensor("sent_out", [64, 8], F32).ap()
    sent_in2 = nc.dram_tensor("sent_in2", [8, 8], F32).ap()
    sent_out2 = nc.dram_tensor("sent_out2", [64, 8], F32).ap()
    dummy_d = nc.dram_tensor("dummy_d", [8, 8], F32).ap()
    xmid_sp = nc.dram_tensor("xmid_sp", [TOWN, D], F32).ap()
    fsum_sp = nc.dram_tensor("fsum_sp", [TOWN, D], F32).ap()

    E = Sched()
    stack = ExitStack()

    def sb(name, shape, dt):
        return stack.enter_context(nc.sbuf_tensor(name, shape, dt))

    def psum_t(name, shape, dt):
        return stack.enter_context(nc.psum_tensor(name, shape, dt))

    idn = sb("idn", [128, 128], BF16)
    AR2 = sb("ar2", [128, 30848], BF16)
    xtbuf = sb("xtbuf", [128, 16384], BF16)
    AR3 = sb("ar3", [128, 16384], BF16)
    wkvp = sb("wkvp", [128, 16384], BF16)
    estage = sb("estage", [128, 4096], F32)
    xbfst = sb("xbfst", [128, 4096], BF16)
    dpk_sb = sb("dpkb", [128, 2048], I32)
    dec2 = sb("dec2", [128, 4096], I16)
    dw_sb = sb("dw", [128, 2048], BF16)
    dscl = sb("dscl", [128, 128], F32)
    dsx = sb("dsx", [128, 128], F32)
    bc_sb = sb("bc", [64, 512], F32)
    rcp_sb = sb("rcp", [1, 512], F32)
    ones1 = sb("ones1", [1, 64], F32)
    biasS = sb("biasS", [128, 1], F32)
    stats = sb("stats", [128, 128], F32)

    masks_v = AR2.ap()[:, 0:8192]
    qt_v = AR2.ap()[:, 8192:12288]
    kt_v = AR2.ap()[:, 12288:20480]
    v_v = AR2.ap()[:, 20480:28800]
    pt_v = AR2.ap()[:, 28800:30848]
    wqp = AR3.ap()[:, 0:8192]
    zt_v = AR3.ap()[:, 8192:12288]
    xmb_v = AR3.ap()[:, 12288:16384]
    AR3f = AR3.ap().bitcast(F32)
    osq_scr = AR3f[:, 6144:8192]
    kp_v = wkvp.ap()[:, 0:8192]
    vp_v = wkvp.ap()[:, 8192:16384]
    eA = estage.ap()[:, 0:2048]
    eB = estage.ap()[:, 2048:4096]
    wst = [estage.ap()[:, 1024 * i:1024 * (i + 1)] for i in range(4)]
    wst3 = [w.rearrange("p (c k) -> p c k", k=64) for w in wst]
    xbff = xbfst.ap().bitcast(F32)
    dpkf = dpk_sb.ap().bitcast(F32)
    dec2bf = dec2.ap().bitcast(BF16)
    dec2f = dec2.ap().bitcast(F32)
    xm_v = xtbuf.ap()
    inter_v = wkvp.ap()
    wd_v = AR2.ap()[:, 0:16384]
    wgs_v = AR3.ap()[:, 0:4096]
    wus_v = AR3.ap()[:, 4096:8192]
    rsst_v = AR3f[:, 4096:8192]
    gg_v = xbff[:, 0:1024]
    g2_v = xbff[:, 1024:2048]
    t1_v = estage.ap()[:, 0:1024]
    p4a = dpkf[:, 0:2048]
    p4b = dec2f[:, 0:2048]

    PS = [psum_t(f"ps{i}", [128, 512], F32) for i in range(5)]
    PTB = [psum_t("pst0", [128, 1024], BF16), psum_t("pst1", [128, 1024], BF16)]
    BC = psum_t("psb", [64, 512], F32)

    sems = {}
    for s in ("DL", "DG", "CC", "PE", "AC", "DV", "PL", "FN", "FG"):
        sems[s] = stack.enter_context(nc.semaphore(s))

    def fence(eng="sync"):
        dsem = "DL" if eng == "sync" else "DG"
        fsem = "FN" if eng == "sync" else "FG"
        v = E.n(dsem)
        E.cnt[fsem] = E.cnt.get(fsem, 0) + 1
        fv = E.cnt[fsem]
        E.ops[eng].append((lambda e, h=sems[fsem]: e.sem_inc(h, 1), ((dsem, v),), None, 0))
        return fv

    E.op("vector", lambda e: e.memset(biasS[:, :], EXP_SHIFT), inc="DV")
    E.op("vector", lambda e: e.memset(ones1[:, :], 1.0), inc="DV")
    E.op("vector", lambda e: e.memset(
        v_v.rearrange("p (t g c) -> p t g c", t=16, g=8)[:, :, :, 64:65], 1.0), inc="DV")
    E.op("vector", lambda e: e.memset(stats.ap()[:, 8:16], 0.0), inc="DV")
    E.op("vector", lambda e: e.memset(stats.ap()[:, 72:80], 0.0), inc="DV")
    E.dma("sync", lambda e: e.dma_start(out=idn[:], in_=ident_in[:, :]))
    for i in range(16):
        E.dma("sync", lambda e, i=i: e.dma_start(
            out=masks_v[:, i * 512:(i + 1) * 512], in_=masks_in[i, :, :]))
    f_init = fence()

    dec_pl = {}
    dec_raw = {}
    dec_nib = {}
    dec_state = {"k": 0}
    dec_srcs = (
        [(gpk[128 * t:128 * (t + 1), :], gsc[128 * t:128 * (t + 1), :],
          wg_raw[128 * t:128 * (t + 1), :]) for t in range(16)]
        + [(upk[128 * t:128 * (t + 1), :], usc[128 * t:128 * (t + 1), :],
            wu_raw[128 * t:128 * (t + 1), :]) for t in range(16)]
        + [(dpk_in[128 * t:128 * (t + 1), :], dsc_in[128 * t:128 * (t + 1), :],
            wd_raw[128 * t:128 * (t + 1), :]) for t in range(16)]
    )

    def decode_next(n):
        if not BUILD_DECODE:
            return
        for _ in range(n):
            k = dec_state["k"]
            if k >= 48:
                return
            dec_state["k"] = k + 1
            pk_src, sc_src, raw_dst = dec_srcs[k]
            bi = k % 2
            pk_v = dpk_sb.ap()[:, bi * 1024:(bi + 1) * 1024]
            sc_v = dscl.ap()[:, bi * 64:(bi + 1) * 64]
            sx_v = dsx.ap()[:, bi * 64:(bi + 1) * 64]
            if k >= 1:
                _, _, prev_dst = dec_srcs[k - 1]
                dec_raw[k - 1] = E.dma("sync", lambda e, d=prev_dst: e.dma_start(
                    out=d, in_=dw_sb[:]), waits=[("PL", dec_pl[k - 1])])
            war = [("DV", dec_nib[k - 2])] if k >= 2 else []
            E.dma("sync", lambda e, o=pk_v, s=pk_src: e.dma_start(out=o, in_=s), waits=war)
            csc = E.dma("sync", lambda e, o=sc_v, s=sc_src: e.dma_start(out=o, in_=s))
            fdec = fence()
            w0 = [("FN", fdec)]
            if k >= 1:
                w0.append(("PL", dec_pl[k - 1]))
            cnib = E.op("vector", lambda e, i2=pk_v: e.tensor_copy(
                dec2[:, 2048:3072], i2), waits=w0, inc="DV")
            dec_nib[k] = cnib
            d1 = E.op("vector", lambda e: e.tensor_scalar(
                dec2[:, 0:1024], dec2[:, 2048:3072], 15, None, ALU.bitwise_and),
                waits=[("DV", cnib)], inc="DV")
            d2 = E.op("vector", lambda e: e.tensor_scalar(
                dec2[:, 1024:2048], dec2[:, 2048:3072], 4, None, ALU.logical_shift_right),
                inc="DV")
            d3 = E.op("vector", lambda e: e.tensor_scalar(
                dec2[:, 2048:4096], dec2[:, 0:2048], 8, 12, ALU.bitwise_and, ALU.logical_shift_left),
                waits=[("DV", d2)], inc="DV")
            d4 = E.op("vector", lambda e: e.tensor_scalar(
                dec2[:, 0:2048], dec2[:, 0:2048], 7, 6, ALU.bitwise_and, ALU.logical_shift_left),
                waits=[("DV", d3)], inc="DV")
            E.op("vector", lambda e, sv=sc_v, xv=sx_v: e.tensor_scalar(
                xv, sv, P126, None, ALU.mult))
            cdv = E.op("vector", lambda e: e.tensor_tensor(
                dec2[:, 0:2048], dec2[:, 0:2048], dec2[:, 2048:4096], ALU.bitwise_or),
                waits=[("DV", d4)], inc="DV")
            mb4 = dec2bf[:, 0:2048].rearrange("p (h a b) -> p h a b", h=2, b=16)
            wv4 = dw_sb.ap().rearrange("p (h a b) -> p h a b", h=2, b=16)
            sx4 = sx_v.unsqueeze(1).unsqueeze(-1).broadcast_to([128, 2, 64, 16])
            plw = [("DV", cdv), ("FN", fdec)]
            dec_pl[k] = E.op("gpsimd", lambda e, a=wv4, b=mb4, c=sx4: e.tensor_tensor(
                a, b, c, ALU.mult), waits=plw, inc="PL")

    def decode_flush():
        if not BUILD_DECODE:
            return
        decode_next(48 - dec_state["k"])
        k = 47
        _, _, dst = dec_srcs[k]
        dec_raw[k] = E.dma("sync", lambda e, d=dst: e.dma_start(
            out=d, in_=dw_sb[:]), waits=[("PL", dec_pl[k])])

    def strided_copy(dst4, src3, waits, inc=None):
        return E.op("vector", lambda e, d=dst4, s2=src3: e.tensor_copy(d, s2),
                    waits=waits, inc=inc)


    def wk_ap(a, dc):
        return kp_v[:, a * 2048 + dc * 128: a * 2048 + (dc + 1) * 128]

    def wv_ap(dc):
        return vp_v[:, dc * 512:(dc + 1) * 512]

    dv_prev = None
    decode_next(2)
    for a in range(NPAIR):
        srcs = [wq_in[8 * a + qh].rearrange("(c p) k -> p c k", p=128) for qh in range(8)]
        l0 = [None] * 8
        for qh in range(4):
            wts = []
            if qh == 0 and dv_prev:
                wts = [("DV", dv_prev)]
            l0[qh] = E.dma("sync", lambda e, s=srcs[qh], d=wst3[qh]: e.dma_start(out=d, in_=s),
                           waits=wts)
        fq1 = fence()
        c01 = E.op("vector", lambda e: e.tensor_tensor(wst[0], wst[0], wst[1], ALU.add),
                   waits=[("FN", fq1)], inc="DV")
        c23 = E.op("vector", lambda e: e.tensor_tensor(wst[2], wst[2], wst[3], ALU.add),
                   inc="DV")
        c02 = E.op("vector", lambda e: e.tensor_tensor(wst[0], wst[0], wst[2], ALU.add),
                   waits=[("DV", c23)], inc="DV")
        l4 = E.dma("sync", lambda e, s=srcs[4], d=wst3[1]: e.dma_start(out=d, in_=s),
                   waits=[("DV", c01)])
        l5 = E.dma("sync", lambda e, s=srcs[5], d=wst3[2]: e.dma_start(out=d, in_=s),
                   waits=[("DV", c02)])
        l6 = E.dma("sync", lambda e, s=srcs[6], d=wst3[3]: e.dma_start(out=d, in_=s))
        fq2 = fence()
        c45 = E.op("vector", lambda e: e.tensor_tensor(wst[1], wst[1], wst[2], ALU.add),
                   waits=[("FN", fq2)], inc="DV")
        l7 = E.dma("sync", lambda e, s=srcs[7], d=wst3[2]: e.dma_start(out=d, in_=s),
                   waits=[("DV", c45)])
        fq3 = fence()
        c46 = E.op("vector", lambda e: e.tensor_tensor(wst[1], wst[1], wst[3], ALU.add),
                   waits=[("DV", c45)], inc="DV")
        c47 = E.op("vector", lambda e: e.tensor_tensor(wst[1], wst[1], wst[2], ALU.add),
                   waits=[("FN", fq3), ("DV", c46)], inc="DV")
        dst = wqp.rearrange("p (a c two k) -> p a c two k", a=4, two=2, k=64)[:, a]
        strided_copy(dst[:, :, 0, :], wst3[0], [("DV", c02)])
        dv_prev = strided_copy(dst[:, :, 1, :], wst3[1], [("DV", c47)], inc="DV")
        decode_next(2)
    wq_done = dv_prev

    for a in range(NPAIR):
        lds = []
        for h in range(2):
            src = wk_in[2 * a + h].rearrange("(c p) k -> p c k", p=128)
            lds.append(E.dma("sync", lambda e, s=src, d=wst3[h]: e.dma_start(out=d, in_=s),
                             waits=[("DV", dv_prev)] if h == 0 else []))
        fkv = fence()
        dst = kp_v[:, a * 2048:(a + 1) * 2048].rearrange("p (c two k) -> p c two k", two=2, k=64)
        strided_copy(dst[:, :, 0, :], wst3[0], [("FN", fkv)])
        dv_prev = strided_copy(dst[:, :, 1, :], wst3[1], [], inc="DV")
    decode_next(2)
    for a in range(NPAIR):
        lds = []
        for h in range(2):
            src = wv_in[2 * a + h].rearrange("(c p) k -> p c k", p=128)
            lds.append(E.dma("sync", lambda e, s=src, d=wst3[h]: e.dma_start(out=d, in_=s),
                             waits=[("DV", dv_prev)] if h == 0 else []))
        fkv = fence()
        dst = vp_v.rearrange("p (c a two k) -> p c a two k", a=4, two=2, k=64)[:, :, a]
        strided_copy(dst[:, :, 0, :], wst3[0], [("FN", fkv)])
        dv_prev = strided_copy(dst[:, :, 1, :], wst3[1], [], inc="DV")
    wkv_done = dv_prev
    decode_next(2)

    pt_state = {"i": 0}
    pt_hist = []

    def transpose_batch(items, pe_waits, ev_waits=()):
        b = pt_state["i"]
        pt_state["i"] += 1
        T = PTB[b % 2].ap()
        w = list(pe_waits)
        if b >= 2:
            w.append(pt_hist[b - 2])
        cpes = []
        for si, (in_ap, out_ap) in enumerate(items):
            pv = T[:, si * 128:(si + 1) * 128]
            cpe = E.op("tensor", lambda e, p=pv, i2=in_ap: e.transpose(p, i2, idn[:]),
                       waits=w if si == 0 else (), inc="PE")
            cpes.append(cpe)
        last_ev = None
        for si, (in_ap, out_ap) in enumerate(items):
            pv = T[:, si * 128:(si + 1) * 128]
            last_ev = E.op("scalar", lambda e, p=pv, o=out_ap: e.activation(o, p, AF.Copy),
                           waits=[("PE", cpes[-1])] + (list(ev_waits) if si == 0 else []),
                           inc="AC")
        pt_hist.append(("AC", last_ev))
        return cpes[-1], ("AC", last_ev)

    xt_hist = []

    def build_xt_tile(src_rows, seq, xt_col0):
        bi = seq % 2
        xb = xbfst.ap()[:, bi * 2048:(bi + 1) * 2048]
        war = [("PE", xt_hist[-2])] if len(xt_hist) >= 2 else []
        cl = E.dma("gpsimd", lambda e, o=xb, s=src_rows: e.dma_start(out=o, in_=s),
                   waits=war)
        fxt = fence("gpsimd")
        last_pe = None
        last_dv = None
        for g in range(DC // 4):
            items = [(xb[:, (4 * g + u) * 128:(4 * g + u + 1) * 128],
                      xtbuf.ap()[:, (4 * g + u) * 1024 + xt_col0:
                                 (4 * g + u) * 1024 + xt_col0 + 128]) for u in range(4)]
            last_pe, last_dv = transpose_batch(
                items, [("FG", fxt), ("FN", f_init)] if g == 0 else [])
        xt_hist.append(last_pe)
        return last_dv

    bank_free = {i: None for i in range(6)}

    def mm(bank, lhsT, rhs, start, stop, waits=(), inc=False, out_ap=None):
        out = out_ap if out_ap is not None else PS[bank].ap()[:, :]
        return E.op("tensor",
                    lambda e, o=out, l=lhsT, r=rhs, st=start, sp=stop:
                    e.matmul(o, l, r, start=st, stop=sp, skip_group_check=True),
                    waits=waits, inc="PE" if inc else None)

    xt_seq = 0
    xtq_done = None
    for i in range(8):
        xtq_done = build_xt_tile(x_own[128 * i:128 * (i + 1), :], xt_seq, i * 128)
        xt_seq += 1
        decode_next(1)
    q_last_ac = None
    for a in range(NPAIR):
        for sl in range(2):
            wts = []
            if a == 0 and sl == 0:
                wts = [xtq_done, ("DV", wq_done)]
            if bank_free[sl]:
                wts.append(bank_free[sl])
            for dcx in range(DC):
                c = mm(sl, wqp[:, a * 2048 + dcx * 128: a * 2048 + (dcx + 1) * 128],
                       xtbuf.ap()[:, dcx * 1024 + sl * 512: dcx * 1024 + sl * 512 + 512],
                       dcx == 0, dcx == DC - 1,
                       waits=wts if dcx == 0 else (), inc=(dcx == DC - 1))
            q_last_ac = E.op("scalar", lambda e, a=a, sl=sl, b=sl: e.activation(
                qt_v[:, a * 1024 + sl * 512: a * 1024 + sl * 512 + 512],
                PS[b].ap()[:, :], AF.Copy), waits=[("PE", c)], inc="AC")
            bank_free[sl] = ("AC", q_last_ac)
    decode_next(2)

    kv_last_ac = None
    for half in range(2):
        xt_done = None
        for i in range(8):
            t = 8 * half + i
            xt_done = build_xt_tile(x_full[128 * t:128 * (t + 1), :], xt_seq, i * 128)
            xt_seq += 1
            decode_next(1)
        for a in range(NPAIR):
            for kc2 in range(2):
                kc = 2 * half + kc2
                bank = kc2
                wts = [xt_done]
                if half == 0 and a == 0 and kc2 == 0:
                    wts.append(("DV", wkv_done))
                if bank_free[bank]:
                    wts.append(bank_free[bank])
                for dcx in range(DC):
                    c = mm(bank, wk_ap(a, dcx),
                           xtbuf.ap()[:, dcx * 1024 + kc2 * 512: dcx * 1024 + kc2 * 512 + 512],
                           dcx == 0, dcx == DC - 1,
                           waits=wts if dcx == 0 else (), inc=(dcx == DC - 1))
                kv_last_ac = E.op("scalar", lambda e, a=a, kc=kc, b=bank: e.activation(
                    kt_v[:, a * 2048 + kc * 512: a * 2048 + kc * 512 + 512],
                    PS[b].ap()[:, :], AF.Copy), waits=[("PE", c)], inc="AC")
                bank_free[bank] = ("AC", kv_last_ac)
        for i in range(8):
            tt = 8 * half + i
            bank = 2 + (i % 2)
            wts = [xt_done]
            if bank_free[bank]:
                wts.append(bank_free[bank])
            for dcx in range(DC):
                c = mm(bank, xtbuf.ap()[:, dcx * 1024 + i * 128: dcx * 1024 + i * 128 + 128],
                       wv_ap(dcx), dcx == 0, dcx == DC - 1,
                       waits=wts if dcx == 0 else (), inc=(dcx == DC - 1))
            vout = v_v.rearrange("p (t g c) -> p t g c", t=16, g=8)[:, tt, :, 0:64]
            vin = PS[bank].ap().rearrange("p (g c) -> p g c", g=8)
            kv_last_ac = E.op("scalar", lambda e, o=vout, i2=vin: e.activation(o, i2, AF.Copy),
                              waits=[("PE", c)], inc="AC")
            bank_free[bank] = ("AC", kv_last_ac)
        decode_next(4)

    wo_dv = None
    for a in range(NPAIR):
        wts = [("DV", wo_dv)] if wo_dv else [("AC", q_last_ac), ("DV", wkv_done)]
        c = E.dma("sync", lambda e, s=wo_in[128 * a:128 * (a + 1), :]:
                  e.dma_start(out=eA, in_=s), waits=wts)
        fwo = fence()
        wo_dv = E.op("vector", lambda e, a=a: e.tensor_copy(
            wqp[:, a * 2048:(a + 1) * 2048], eA), waits=[("FN", fwo)], inc="DV")
    wo_done = wo_dv
    decode_flush()

    xbar_hist = []
    pe_snapshot = E.n("PE")

    def emit_xbar(src_ap, dst_ap, wait_dl):
        k = len(xbar_hist)
        bi = k % 2
        tv = xbfst.ap()[:, bi * 2048:(bi + 1) * 2048]
        war = [("DL", xbar_hist[k - 2])] if k >= 2 else [("PE", pe_snapshot)]
        ct = E.dma("sync", lambda e, o=tv, s=src_ap: e.dma_start_transpose(o, s),
                   waits=[("DL", wait_dl)] + war)
        cw = E.dma("sync", lambda e, d=dst_ap, s=tv: e.dma_start(out=d, in_=s),
                   waits=[("DL", ct)])
        xbar_hist.append(cw)
        return cw

    if BUILD_DECODE:
        for dcx in range(DC):
            emit_xbar(wg_raw[:, 128 * dcx:128 * (dcx + 1)],
                      wg_dec.rearrange("nt dc d j -> dc d nt j")[dcx], dec_raw[15])
        for dcx in range(DC):
            emit_xbar(wu_raw[:, 128 * dcx:128 * (dcx + 1)],
                      wu_dec.rearrange("nt dc d j -> dc d nt j")[dcx], dec_raw[31])
    if BUILD_DECODE:
        for ntk in range(NT):
            emit_xbar(wd_raw[:, 128 * ntk:128 * (ntk + 1)], wd_dec[ntk], dec_raw[47])
        wg_ready = xbar_hist[DC - 1]
        wu_ready = xbar_hist[2 * DC - 1]
        wd_ready = xbar_hist[-1]
    else:
        wg_ready = wu_ready = wd_ready = 0

    pt_war = {0: None, 1: None, 2: None, 3: None}
    bc_cbc = None
    bc_cev = None
    zn_last = None
    for a in range(NPAIR):
        for sl in range(2):
            ext = EXT[sl]
            for h in range(2):
                last_av = None
                for kt in range(ext):
                    sbank = kt % 4
                    wts = []
                    if a == 0 and sl == 0 and kt == 0 and h == 0:
                        wts.append(("AC", kv_last_ac))
                        wts.append(("DV", 5))
                    if bank_free[sbank]:
                        wts.append(bank_free[sbank])
                    c = mm(sbank,
                           kt_v[64 * h:64 * (h + 1),
                                a * 2048 + kt * 128: a * 2048 + (kt + 1) * 128],
                           qt_v[64 * h:64 * (h + 1),
                                a * 1024 + sl * 512: a * 1024 + sl * 512 + 512],
                           True, True, waits=wts, inc=True)
                    slot = kt % 4
                    pv = pt_v[:, slot * 512:(slot + 1) * 512]
                    ew = [("PE", c)]
                    if pt_war[slot]:
                        ew.append(("PE", pt_war[slot]))
                    ce = E.op("scalar", lambda e, o=pv, b=sbank: e.activation(
                        o, PS[b].ap()[:, :], AF.Exp, scale=0.125, bias=biasS[:, :]),
                        waits=ew, inc="AC")
                    bank_free[sbank] = ("AC", ce)
                    if (sl == 0) or (kt >= 8):
                        mv = masks_v[:, kt * 512:(kt + 1) * 512]
                        cm = E.op("vector", lambda e, p=pv, m=mv: e.tensor_tensor(
                            p, p, m, ALU.mult),
                            waits=[("AC", ce), ("FN", f_init)], inc="DV")
                        pwait = ("DV", cm)
                    else:
                        pwait = ("AC", ce)
                    vslice = v_v[:, kt * 520 + (2 * a + h) * 65:
                                 kt * 520 + (2 * a + h) * 65 + 65]
                    avw = [pwait]
                    if kt == 0 and bank_free.get(4):
                        avw.append(bank_free[4])
                    cav = mm(None, vslice, pv, kt == 0, kt == ext - 1,
                             waits=avw, inc=True, out_ap=PS[4].ap()[0:65, :])
                    pt_war[slot] = cav
                    last_av = cav
                crw = [("PE", last_av)]
                if bc_cbc:
                    crw.append(("PE", bc_cbc))
                crc = E.op("vector", lambda e: e.reciprocal(
                    rcp_sb[:, :], PS[4].ap()[64:65, :]), waits=crw, inc="DV")
                bcw = [("DV", crc)]
                if bc_cev:
                    bcw.append(("AC", bc_cev))
                bc_cbc = E.op("tensor", lambda e: e.matmul(
                    BC[:, :], ones1[:, :], rcp_sb[:, :], start=True, stop=True,
                    skip_group_check=True), waits=bcw, inc="PE")
                bc_cev = E.op("scalar", lambda e: e.activation(bc_sb[:, :], BC[:, :], AF.Copy),
                              waits=[("PE", bc_cbc)] + ([("DV", zn_last)] if zn_last else []),
                              inc="AC")
                zn_last = E.op("vector", lambda e, a=a, sl=sl, h=h: e.tensor_tensor(
                    zt_v[64 * h:64 * (h + 1),
                         a * 1024 + sl * 512: a * 1024 + sl * 512 + 512],
                    PS[4].ap()[0:64, :], bc_sb[:, :], ALU.mult),
                    waits=[("AC", bc_cev)], inc="DV")
                bank_free[4] = ("DV", zn_last)
    z_done = zn_last

    sum_hist = []
    sq_hist = []
    spill_hist = []
    ag_cc = []
    ag_cag = []
    ap_hist = []
    st_ = stats.ap()
    for i in range(8):
        bi = i % 2
        xo = eA if bi == 0 else eB
        xb16 = xmb_v[:, bi * 2048:(bi + 1) * 2048]
        xmt = dec2bf[:, bi * 2048:(bi + 1) * 2048]
        war = [("DV", wo_done)]
        if i >= 2:
            war = [("DL", spill_hist[i - 2]), ("AC", ap_hist[i - 2][0])]
        cl = E.dma("sync", lambda e, o=xo, s=x_own[128 * i:128 * (i + 1), :]:
                   e.dma_start(out=o, in_=s), waits=war)
        fxo = fence()
        cadd_last = None
        for dq in range(4):
            wts = []
            if i == 0 and dq == 0:
                wts = [("DV", z_done), ("DV", wo_done)]
            if bank_free[dq]:
                wts.append(bank_free[dq])
            for a in range(NPAIR):
                c = mm(dq, zt_v[:, a * 1024 + i * 128: a * 1024 + (i + 1) * 128],
                       wqp[:, a * 2048 + dq * 512: a * 2048 + dq * 512 + 512],
                       a == 0, a == NPAIR - 1,
                       waits=wts if a == 0 else (), inc=(a == NPAIR - 1))
            cadd_last = E.op("vector", lambda e, o=xo, dq=dq, b=dq: e.tensor_tensor(
                o[:, dq * 512:(dq + 1) * 512], PS[b].ap()[:, :],
                o[:, dq * 512:(dq + 1) * 512], ALU.add),
                waits=[("PE", c), ("FN", fxo)], inc="DV")
            bank_free[dq] = ("DV", cadd_last)
        cs = E.op("vector", lambda e, x=xo, i=i: e.tensor_reduce(
            st_[:, i:i + 1], x, axis=mybir.AxisListType.X, op=ALU.add),
            waits=[("DV", cadd_last)], inc="DV")
        sum_hist.append(cs)
        cq = E.op("scalar", lambda e, x=xo, i=i: e.activation(
            osq_scr, x, AF.Square, accum_out=st_[:, 8 + i:9 + i]),
            waits=[("DV", cadd_last)], inc="AC")
        sq_hist.append(cq)
        p1 = E.op("vector", lambda e, i=i: e.tensor_scalar(
            st_[:, 16 + i:17 + i], st_[:, i:i + 1], 1.0 / D, None, ALU.mult),
            waits=[("DV", cs), ("AC", cq)], inc="DV")
        p3 = E.op("vector", lambda e, i=i: e.tensor_tensor(
            st_[:, 32 + i:33 + i], st_[:, 16 + i:17 + i], st_[:, 16 + i:17 + i],
            ALU.mult), waits=[("DV", p1)], inc="DV")
        p4 = E.op("vector", lambda e, i=i: e.scalar_tensor_tensor(
            st_[:, 40 + i:41 + i], st_[:, 8 + i:9 + i], 1.0 / D,
            st_[:, 32 + i:33 + i], ALU.mult, ALU.subtract),
            waits=[("DV", p3)], inc="DV")
        p5 = E.op("vector", lambda e, i=i: e.tensor_scalar(
            st_[:, 40 + i:41 + i], st_[:, 40 + i:41 + i], LN_EPS, None, ALU.add),
            waits=[("DV", p4)], inc="DV")
        p6 = E.op("scalar", lambda e, i=i: e.activation(
            st_[:, 48 + i:49 + i], st_[:, 40 + i:41 + i], AF.Sqrt),
            waits=[("DV", p5)], inc="AC")
        p7 = E.op("vector", lambda e, i=i: e.reciprocal(
            st_[:, 32 + i:33 + i], st_[:, 48 + i:49 + i]), waits=[("AC", p6)], inc="DV")
        p8 = E.op("vector", lambda e, i=i: e.tensor_tensor(
            st_[:, 56 + i:57 + i], st_[:, 16 + i:17 + i], st_[:, 32 + i:33 + i],
            ALU.mult), waits=[("DV", p7)], inc="DV")
        p9 = E.op("vector", lambda e, i=i: e.tensor_scalar(
            st_[:, 56 + i:57 + i], st_[:, 56 + i:57 + i], -1.0, None, ALU.mult),
            waits=[("DV", p8)], inc="DV")
        aw = [("DV", p9)]
        if i >= 2:
            aw.append(("PE", ap_hist[i - 2][3]))
        cbf = E.op("scalar", lambda e, o=xb16, x=xo, i=i: e.activation(
            o, x, AF.Identity, scale=st_[:, 32 + i:33 + i],
            bias=st_[:, 56 + i:57 + i]), waits=aw, inc="AC")
        cf32 = E.op("scalar", lambda e, x=xo, i=i: e.activation(
            x, x, AF.Identity, scale=st_[:, 32 + i:33 + i],
            bias=st_[:, 56 + i:57 + i]), inc="AC")
        csp = E.dma("sync", lambda e, x=xo, i=i: e.dma_start(
            out=xmid_sp[128 * i:128 * (i + 1), :], in_=x), waits=[("AC", cf32)])
        spill_hist.append(csp)
        last_pe = None
        last_ev = None
        for g in range(4):
            items = []
            for u in range(4):
                pc = 4 * g + u
                if pc < 8:
                    src = xb16[:, 256 * pc: 256 * pc + 256: 2]
                else:
                    src = xb16[:, 256 * (pc - 8) + 1: 256 * (pc - 8) + 256: 2]
                items.append((src, xmt[:, pc * 128:(pc + 1) * 128]))
            evw = []
            if g == 0:
                evw = [("PL", dec_pl[47])] if (BUILD_DECODE and i == 0) else []
                if i >= 2:
                    evw.append(("FN", ap_hist[i - 2][2]))
            last_pe, last_ev = transpose_batch(
                items, [("AC", cbf)] if g == 0 else [], ev_waits=evw)
        for q4 in range(4):
            E.dma("sync", lambda e, i=i, q4=q4, s=xmt: e.dma_start(
                out=ag_in[i].rearrange("(pc d) t -> d pc t", d=128)[:, 4 * q4:4 * (q4 + 1), :],
                in_=s.rearrange("p (pc t) -> p pc t", t=128)[:, 4 * q4:4 * (q4 + 1), :]),
                waits=[last_ev] if q4 == 0 else [])
        fcag = fence()
        ag_cag.append(fcag)
        if BUILD_PHASE == "full":
            ag_cc.append(E.op("gpsimd", lambda e, i=i: e.collective_compute(
                "AllGather", ALU.bypass, replica_groups=[list(range(W))],
                ins=[ag_in[i].opt()], outs=[ag_out[i].opt()]),
                waits=[("FN", fcag)], inc="CC"))
        ap_hist.append((cf32, csp, fcag, last_pe))

    if BUILD_PHASE == "attn":
        outl = None
        for i in range(8):
            outl = E.dma("sync", lambda e, i=i: e.dma_start(
                out=y_out[128 * i:128 * (i + 1), :],
                in_=xmid_sp[128 * i:128 * (i + 1), :]),
                waits=[("DL", ap_hist[i][1])])
        E.ops["sync"].append((None, (("DL", outl),), None, 0))
        with nc.Block() as block:
            def make(engname):
                def emitter(eng):
                    last = {}
                    for fn, waits, inc, amt in E.ops[engname]:
                        for s2, v in waits:
                            if v is None or v == 0:
                                continue
                            if last.get(s2, -1) < v:
                                eng.wait_ge(sems[s2], v)
                                last[s2] = v
                        if fn is None:
                            continue
                        inst = fn(eng)
                        if inc is not None:
                            inst.then_inc(sems[inc], amt)
                return emitter
            block.sync(make("sync"))
            block.tensor(make("tensor"))
            block.vector(make("vector"))
            block.scalar(make("scalar"))
            block.gpsimd(make("gpsimd"))
        stack.close()
        return nc


    def ln_panels(c0, pw):
        x1 = E.op("vector", lambda e: e.tensor_scalar(
            st_[:, c0 + 16:c0 + 24], st_[:, c0 + 0:c0 + 8], 1.0 / D, None, ALU.mult),
            waits=pw, inc="DV")
        x2 = E.op("vector", lambda e: e.tensor_scalar(
            st_[:, c0 + 24:c0 + 32], st_[:, c0 + 8:c0 + 16], 1.0 / D, None, ALU.mult),
            inc="DV")
        x3 = E.op("vector", lambda e: e.tensor_tensor(
            st_[:, c0 + 32:c0 + 40], st_[:, c0 + 16:c0 + 24], st_[:, c0 + 16:c0 + 24],
            ALU.mult), waits=[("DV", x1)], inc="DV")
        x4 = E.op("vector", lambda e: e.tensor_tensor(
            st_[:, c0 + 40:c0 + 48], st_[:, c0 + 24:c0 + 32], st_[:, c0 + 32:c0 + 40],
            ALU.subtract), waits=[("DV", x3), ("DV", x2)], inc="DV")
        x5 = E.op("vector", lambda e: e.tensor_scalar(
            st_[:, c0 + 40:c0 + 48], st_[:, c0 + 40:c0 + 48], LN_EPS, None, ALU.add),
            waits=[("DV", x4)], inc="DV")
        x6 = E.op("scalar", lambda e: e.activation(
            st_[:, c0 + 48:c0 + 56], st_[:, c0 + 40:c0 + 48], AF.Sqrt),
            waits=[("DV", x5)], inc="AC")
        x7 = E.op("vector", lambda e: e.reciprocal(
            st_[:, c0 + 32:c0 + 40], st_[:, c0 + 48:c0 + 56]),
            waits=[("AC", x6)], inc="DV")
        x8 = E.op("vector", lambda e: e.tensor_tensor(
            st_[:, c0 + 56:c0 + 64], st_[:, c0 + 16:c0 + 24], st_[:, c0 + 32:c0 + 40],
            ALU.mult), waits=[("DV", x7)], inc="DV")
        x9 = E.op("vector", lambda e: e.tensor_scalar(
            st_[:, c0 + 56:c0 + 64], st_[:, c0 + 56:c0 + 64], -1.0, None, ALU.mult),
            waits=[("DV", x8)], inc="DV")
        return x9

    p4_hist = []

    def emit_p4a(j):
        wts = [("CC", rs_cc[j])]
        if p4_hist:
            wts += [("DL", p4_hist[-1][2]), ("AC", p4_hist[-1][3])]
        c1 = E.dma("sync", lambda e, j=j: e.dma_start(out=p4a, in_=rs_out[j][:, :]),
                   waits=wts)
        c2 = E.dma("sync", lambda e, j=j: e.dma_start(
            out=p4b, in_=xmid_sp[128 * j:128 * (j + 1), :]),
            waits=([("DV", p4_hist[-1][4])] if p4_hist else []) + [("DL", ap_hist[j][1])])
        fp4 = fence()
        cadd = E.op("vector", lambda e: e.tensor_tensor(p4a, p4a, p4b, ALU.add),
                    waits=[("FN", fp4)], inc="DV")
        cs = E.op("vector", lambda e, j=j: e.tensor_reduce(
            stats.ap()[:, 64 + j:65 + j], p4a, axis=mybir.AxisListType.X, op=ALU.add),
            waits=[("DV", cadd)], inc="DV")
        cq = E.op("scalar", lambda e, j=j: e.activation(
            eB, p4a, AF.Square, accum_out=stats.ap()[:, 72 + j:73 + j]),
            waits=[("DV", cadd)], inc="AC")
        c3 = E.dma("sync", lambda e, j=j: e.dma_start(
            out=fsum_sp[128 * j:128 * (j + 1), :], in_=p4a),
            waits=[("DV", cs), ("AC", cq)])
        p4_hist.append((c1, c2, c3, cq, cs))

    gate_pe = {}
    up_pe = {}
    t1_pl = {}
    inter_dv = {}
    down_pe = {}
    rs_dl = {}
    rs_cc = []
    rs_fence = {}
    rs_chunk_fence = {}
    wg_dl = {}
    wu_dl = {}
    wd_dl = {}
    xm_dl = {}

    for j in range(NCHUNK):
        if j >= 2:
            emit_p4a(j - 2)
        for r in range(W):
            wts = []
            if r == 0:
                wts.append(("CC", ag_cc[j]))
                if j > 0:
                    wts.append(("PE", up_pe[(j * NT - 1, 1)]))
            src = ag_out[j].rearrange("(r dc d) t -> r d dc t", r=W, d=128)[r]
            dst = xm_v.rearrange("p (dc t) -> p dc t", t=1024)[:, :, 128 * r:128 * (r + 1)]
            for q4 in range(4):
                xm_dl[(j, r)] = E.dma("sync", lambda e, q4=q4, s=src, d=dst: e.dma_start(
                    out=d[:, 4 * q4:4 * (q4 + 1), :], in_=s[:, 4 * q4:4 * (q4 + 1), :]),
                    waits=wts if q4 == 0 else [])
        fxm = fence()
        fgu = {}
        fwd = {}

        def emit_gu_load(nt2):
            gi2 = j * NT + nt2
            bi2 = gi2 % 2
            wts = [("DL", wg_ready)] if j == 0 else []
            if gi2 >= 2:
                wts.append(("PE", gate_pe[(gi2 - 2, 1)]))
            wg_dl[gi2] = E.dma(
                "sync",
                lambda e, o=wgs_v[:, bi2 * 2048:(bi2 + 1) * 2048].rearrange("p (dc j2) -> p dc j2", j2=128),
                s=wg_dec[nt2].rearrange("dc d j2 -> d dc j2"):
                e.dma_start(out=o, in_=s), waits=wts)
            wts = [("DL", wu_ready)] if j == 0 else []
            if gi2 >= 2:
                wts.append(("PE", up_pe[(gi2 - 2, 1)]))
            wu_dl[gi2] = E.dma(
                "sync",
                lambda e, o=wus_v[:, bi2 * 2048:(bi2 + 1) * 2048].rearrange("p (dc j2) -> p dc j2", j2=128),
                s=wu_dec[nt2].rearrange("dc d j2 -> d dc j2"):
                e.dma_start(out=o, in_=s), waits=wts)
            fgu[gi2] = fence()

        def emit_wd_loads(hh):
            for ntk in range(NT):
                wts = [("DL", wd_ready)] if j == 0 else []
                if ntk == 0:
                    if hh == 1:
                        wts.append(("PE", down_pe[(j, 0, 7)]))
                    elif j > 0:
                        wts.append(("PE", down_pe[(j - 1, 1, 7)]))
                wd_dl[(j, hh, ntk)] = E.dma(
                    "sync", lambda e, o=wd_v[:, ntk * 1024:(ntk + 1) * 1024],
                    s=wd_dec[ntk][:, hh * 1024:(hh + 1) * 1024]:
                    e.dma_start(out=o, in_=s), waits=wts)
            fwd[hh] = fence()

        emit_gu_load(0)
        emit_gu_load(1)
        for nt in range(NT):
            gi = j * NT + nt
            bi = gi % 2
            for tq in range(2):
                gbank, ubank = tq, 2 + tq
                prev = (gi - 1, tq)
                wts = [("FN", fgu[gi])]
                if nt == 0 and tq == 0:
                    wts.append(("FN", fxm))
                if bank_free[gbank]:
                    wts.append(bank_free[gbank])
                for dcx in range(DC):
                    c = mm(gbank, wgs_v[:, bi * 2048 + dcx * 128: bi * 2048 + (dcx + 1) * 128],
                           xm_v[:, dcx * 1024 + tq * 512: dcx * 1024 + tq * 512 + 512],
                           dcx == 0, dcx == DC - 1,
                           waits=wts if dcx == 0 else (), inc=(dcx == DC - 1))
                gate_pe[(gi, tq)] = c
                ggv = gg_v[:, tq * 512:(tq + 1) * 512]
                g2v = g2_v[:, tq * 512:(tq + 1) * 512]
                tv = t1_v[:, tq * 512:(tq + 1) * 512]
                ggw = [("PE", c)]
                if prev in t1_pl:
                    ggw.append(("PL", t1_pl[prev]))
                cgg = E.op("scalar", lambda e, o=ggv, b=gbank: e.activation(
                    o, PS[b].ap()[:, :], AF.Copy), waits=ggw, inc="AC")
                bank_free[gbank] = ("AC", cgg)
                cg2 = E.op("scalar", lambda e, o=g2v, i2=ggv: e.activation(
                    o, i2, AF.Square), inc="AC")
                cw1 = E.op("gpsimd", lambda e, o=g2v: e.tensor_scalar(
                    o, o, GC1, GC2, ALU.mult, ALU.add), waits=[("AC", cg2)], inc="PL")
                car = E.op("gpsimd", lambda e, o=g2v, g=ggv: e.tensor_tensor(
                    o, g, o, ALU.mult), inc="PL")
                csg = E.op("scalar", lambda e, o=g2v: e.activation(
                    o, o, AF.Sigmoid), waits=[("PL", car)], inc="AC")
                t1w = [("AC", csg)]
                if j == 0 and nt == 0 and tq == 0 and BUILD_PHASE == "full":
                    t1w.append(("FN", ap_hist[7][2]))
                if prev in inter_dv:
                    t1w.append(("DV", inter_dv[prev]))
                ct1 = E.op("gpsimd", lambda e, o=tv, g=ggv, s2=g2v: e.tensor_tensor(
                    o, g, s2, ALU.mult), waits=t1w, inc="PL")
                t1_pl[(gi, tq)] = ct1
                wts = [("FN", fgu[gi])]
                if bank_free[ubank]:
                    wts.append(bank_free[ubank])
                for dcx in range(DC):
                    c2 = mm(ubank, wus_v[:, bi * 2048 + dcx * 128: bi * 2048 + (dcx + 1) * 128],
                            xm_v[:, dcx * 1024 + tq * 512: dcx * 1024 + tq * 512 + 512],
                            dcx == 0, dcx == DC - 1,
                            waits=wts if dcx == 0 else (), inc=(dcx == DC - 1))
                up_pe[(gi, tq)] = c2
                ivw = [("PL", ct1), ("PE", c2)]
                if j > 0 and nt == 0 and tq == 0:
                    ivw.append(("PE", down_pe[(j - 1, 1, 7)]))
                cin = E.op("vector", lambda e, nt=nt, tq=tq, b=ubank, t=tv: e.tensor_tensor(
                    inter_v[:, nt * 1024 + tq * 512: nt * 1024 + tq * 512 + 512],
                    t, PS[b].ap()[:, :], ALU.mult), waits=ivw, inc="DV")
                inter_dv[(gi, tq)] = cin
                bank_free[ubank] = ("DV", cin)
            if nt + 2 < NT:
                emit_gu_load(nt + 2)
            if nt == 12:
                emit_wd_loads(0)
        inter_done = inter_dv[(j * NT + NT - 1, 1)]
        for hh in range(2):
            if hh == 1:
                emit_wd_loads(1)
            for tt in range(8):
                bk = (tt % 2) * 2
                wts0 = [("DV", inter_done), ("FN", fwd[hh])]
                for hc in range(2):
                    if bank_free[bk + hc]:
                        wts0.append(bank_free[bk + hc])
                first = True
                c = None
                for ntk in range(NT):
                    lhs = inter_v[:, ntk * 1024 + tt * 128: ntk * 1024 + (tt + 1) * 128]
                    for hc in range(2):
                        rhs = wd_v[:, ntk * 1024 + hc * 512: ntk * 1024 + hc * 512 + 512]
                        c = mm(bk + hc, lhs, rhs, ntk == 0, ntk == NT - 1,
                               waits=wts0 if first else (), inc=(ntk == NT - 1))
                        first = False
                down_pe[(j, hh, tt)] = c
                step = hh * 8 + tt
                bi = step % 4
                rv = rsst_v[:, bi * 1024:(bi + 1) * 1024]
                evw = [("PE", c)]
                if rs_fence.get((j, hh, tt)):
                    evw.append(("FN", rs_fence[(j, hh, tt)]))
                E.op("scalar", lambda e, o=rv, b=bk: e.activation(
                    o[:, 0:512], PS[b].ap()[:, :], AF.Copy), waits=evw, inc="AC")
                ca2 = E.op("scalar", lambda e, o=rv, b=bk: e.activation(
                    o[:, 512:1024], PS[b + 1].ap()[:, :], AF.Copy), inc="AC")
                bank_free[bk] = ("AC", ca2)
                bank_free[bk + 1] = ("AC", ca2)
                rs_dl[(j, hh, tt)] = E.dma(
                    "sync", lambda e, j=j, hh=hh, tt=tt, s=rv:
                    e.dma_start(out=rs_in[j][128 * tt:128 * (tt + 1),
                                             hh * 1024:(hh + 1) * 1024], in_=s),
                    waits=[("AC", ca2)])
                if step % 4 == 3:
                    fr = fence()
                    for ahead in range(4):
                        st2 = step + 1 + ahead
                        jj, hh2, tt2 = (j, st2 // 8, st2 % 8) if st2 < 16 else                             (j + 1, (st2 - 16) // 8, (st2 - 16) % 8)
                        rs_fence[(jj, hh2, tt2)] = fr
                    if step == 15:
                        rs_chunk_fence[j] = fr
        rs_cc.append(E.op("gpsimd", lambda e, j=j: e.collective_compute(
            "ReduceScatter", ALU.add, replica_groups=[list(range(W))],
            ins=[rs_in[j].opt()], outs=[rs_out[j].opt()]),
            waits=[("FN", rs_chunk_fence[j])], inc="CC"))

    emit_p4a(NCHUNK - 2)
    emit_p4a(NCHUNK - 1)

    ln2_dv = ln_panels(64, [("AC", p4_hist[-1][3]), ("DV", p4_hist[-1][4])])

    app_prev = None
    out_last = None
    for j in range(NCHUNK):
        wts = [("DL", p4_hist[-1][2]), ("AC", p4_hist[-1][3])] if app_prev is None \
            else [("AC", app_prev[1])]
        c1 = E.dma("sync", lambda e, j=j: e.dma_start(
            out=p4a, in_=fsum_sp[128 * j:128 * (j + 1), :]), waits=wts)
        fp4b = fence()
        aw = [("FN", fp4b), ("DV", ln2_dv)]
        ca = E.op("scalar", lambda e, j=j: e.activation(
            p4b, p4a, AF.Identity, scale=stats.ap()[:, 96 + j:97 + j],
            bias=stats.ap()[:, 120 + j:121 + j]), waits=aw, inc="AC")
        out_last = E.dma("sync", lambda e, j=j: e.dma_start(
            out=y_out[128 * j:128 * (j + 1), :], in_=p4b), waits=[("AC", ca)])
        app_prev = (c1, ca, out_last)

    if BUILD_DBG:
        dbg1 = nc.dram_tensor("dbg_xmid", [TOWN, D], F32, kind="ExternalOutput").ap()
        dbg2 = nc.dram_tensor("dbg_rs", [TOWN, D], F32, kind="ExternalOutput").ap()
        dbg3 = nc.dram_tensor("dbg_fsum", [TOWN, D], F32, kind="ExternalOutput").ap()
        for j in range(NCHUNK):
            out_last = E.dma("sync", lambda e, j=j: e.dma_start(
                out=dbg1[128 * j:128 * (j + 1), :], in_=xmid_sp[128 * j:128 * (j + 1), :]),
                waits=[("DL", out_last)])
            out_last = E.dma("sync", lambda e, j=j: e.dma_start(
                out=dbg2[128 * j:128 * (j + 1), :], in_=rs_out[j][:, :]))
            out_last = E.dma("sync", lambda e, j=j: e.dma_start(
                out=dbg3[128 * j:128 * (j + 1), :], in_=fsum_sp[128 * j:128 * (j + 1), :]))
    E.ops["sync"].append((None, (("DL", out_last),), None, 0))

    if DEBUG_HOOK is not None:
        DEBUG_HOOK(dict(nc=nc, E=E, sems=sems, loc=locals()))

    with nc.Block() as block:
        def make(engname):
            def emitter(eng):
                last = {}
                for fn, waits, inc, amt in E.ops[engname]:
                    for s2, v in waits:
                        if v is None or v == 0:
                            continue
                        if last.get(s2, -1) < v:
                            eng.wait_ge(sems[s2], v)
                            last[s2] = v
                    if fn is None:
                        continue
                    inst = fn(eng)
                    if inc is not None:
                        inst.then_inc(sems[inc], amt)
            return emitter

        block.sync(make("sync"))
        block.tensor(make("tensor"))
        block.vector(make("vector"))
        block.scalar(make("scalar"))
        block.gpsimd(make("gpsimd"))

    stack.close()
    return nc


_PROG_CACHE = {}


def _get_prog():
    if "nc" not in _PROG_CACHE:
        _PROG_CACHE["nc"] = build_program()
    return _PROG_CACHE["nc"]


def _host_inputs(inputs):
    x = np.asarray(inputs["L0_attn_normalized_resid_pre"])
    wq = np.ascontiguousarray(np.asarray(inputs["L0_attn_W_Q"], dtype=np.float32))
    wk = np.ascontiguousarray(np.asarray(inputs["L0_attn_W_K"], dtype=np.float32))
    wv = np.ascontiguousarray(np.asarray(inputs["L0_attn_W_V"], dtype=np.float32))
    wo = np.ascontiguousarray(
        np.asarray(inputs["L0_attn_W_O"], dtype=np.float32).reshape(NKV * DH, D))
    gw = np.asarray(inputs["L0_ffn_gate_W_packed"], dtype=np.int32)
    gs = np.asarray(inputs["L0_ffn_gate_scales"], dtype=np.float32)
    uw = np.asarray(inputs["L0_ffn_up_W_packed"], dtype=np.int32)
    us = np.asarray(inputs["L0_ffn_up_scales"], dtype=np.float32)
    dw = np.asarray(inputs["L0_ffn_down_W_packed"], dtype=np.int32)
    ds = np.asarray(inputs["L0_ffn_down_scales"], dtype=np.float32)
    ident = np.eye(128, dtype=np.float32).astype(ml_dtypes.bfloat16)

    in_maps = []
    for c in range(W):
        b, p = c // 2, c % 2
        qcs = (0, 3) if p == 0 else (1, 2)
        x_full_c = np.ascontiguousarray(x[b], dtype=np.float32)
        x_own_c = np.ascontiguousarray(
            np.concatenate([x[b, 512 * qc:512 * (qc + 1)] for qc in qcs], axis=0),
            dtype=np.float32)
        sl = np.arange(NSL * c, NSL * (c + 1))
        rows = np.concatenate([sl[0::2], sl[1::2]])
        m = np.zeros((16, 128, 512), dtype=np.float32)
        kk = np.arange(128)[:, None]
        qq = np.arange(512)[None, :]
        for kt in range(8):
            m[kt] = (128 * kt + kk <= 512 * qcs[0] + qq)
        for kt in range(8, 16):
            m[kt] = (128 * kt + kk <= 512 * qcs[1] + qq)
        in_maps.append({
            "x_full": x_full_c, "x_own": x_own_c,
            "wq": wq, "wk": wk, "wv": wv, "wo": wo,
            "gpk": np.ascontiguousarray(gw[rows]),
            "gsc": np.ascontiguousarray(gs[rows]),
            "upk": np.ascontiguousarray(uw[rows]),
            "usc": np.ascontiguousarray(us[rows]),
            "dpk": np.ascontiguousarray(dw[:, 1024 * c:1024 * (c + 1)]),
            "dsc": np.ascontiguousarray(ds[:, 64 * c:64 * (c + 1)]),
            "masks": m.astype(ml_dtypes.bfloat16), "ident": ident,
        })
    return in_maps


def kernel(**inputs):
    nc = _get_prog()
    in_maps = _host_inputs(inputs)
    res = run_bass_kernel_spmd(nc, in_maps, core_ids=list(range(W)))
    out = np.empty((B, S, D), dtype=np.float32)
    for c in range(W):
        b, p = c // 2, c % 2
        qcs = (0, 3) if p == 0 else (1, 2)
        yc = res.results[c]["y"]
        out[b, 512 * qcs[0]:512 * (qcs[0] + 1)] = yc[0:512]
        out[b, 512 * qcs[1]:512 * (qcs[1] + 1)] = yc[512:1024]
    return out


if __name__ == "__main__":
    build_program()
    print("program built ok")


# version 07
# speedup vs baseline: 1.0336x; 1.0095x over previous
import numpy as np
import ml_dtypes
from contextlib import ExitStack

import concourse.bass as bass
import concourse.mybir as mybir
from concourse.bass_utils import run_bass_kernel_spmd

F32 = mybir.dt.float32
BF16 = mybir.dt.bfloat16
I32 = mybir.dt.int32
I16 = mybir.dt.int16
AF = mybir.ActivationFunctionType
ALU = mybir.AluOpType

B, S, D = 4, 2048, 2048
NKV, DH, NPAIR = 8, 64, 4
MLP = 16384
NSL = 2048
NT = 16
DC = 16
TOWN = 1024
NCHUNK = 8
CT = 1024
W = 8
EXT = (8, 16)
LN_EPS = 1e-5
EXP_SHIFT = -16.0
GC2 = 2.0 * 0.7978845608028654
GC1 = GC2 * 0.044715
P126 = float(2.0 ** 126)

ENGINES = ("sync", "tensor", "vector", "scalar", "gpsimd")
DEBUG_HOOK = None
BUILD_PHASE = "full"
BUILD_DECODE = True
BUILD_DBG = False


class Sched:
    def __init__(self):
        self.ops = {e: [] for e in ENGINES}
        self.cnt = {}

    def op(self, eng, fn, waits=(), inc=None, amt=1):
        waits = [w for w in waits if w is not None]
        self.ops[eng].append((fn, tuple(waits), inc, amt))
        if inc is not None:
            self.cnt[inc] = self.cnt.get(inc, 0) + amt
            return self.cnt[inc]
        return None

    def dma(self, eng, fn, waits=()):
        sem = {"sync": "DL", "gpsimd": "DG"}[eng]
        return self.op(eng, fn, waits, inc=sem, amt=16)

    def n(self, sem):
        return self.cnt.get(sem, 0)


def build_program():
    nc = bass.Bass("TRN2", target_bir_lowering=False)

    def din(name, shape, dt):
        return nc.dram_tensor(name, shape, dt, kind="ExternalInput").ap()

    x_full = din("x_full", [S, D], F32)
    x_own = din("x_own", [TOWN, D], F32)
    wq_in = din("wq", [32, D, DH], F32)
    wk_in = din("wk", [NKV, D, DH], F32)
    wv_in = din("wv", [NKV, D, DH], F32)
    wo_in = din("wo", [NKV * DH, D], F32)
    gpk = din("gpk", [NSL, 1024], I32)
    gsc = din("gsc", [NSL, 64], F32)
    upk = din("upk", [NSL, 1024], I32)
    usc = din("usc", [NSL, 64], F32)
    dpk_in = din("dpk", [D, 1024], I32)
    dsc_in = din("dsc", [D, 64], F32)
    masks_in = din("masks", [16, 128, 512], BF16)
    ident_in = din("ident", [128, 128], BF16)
    y_out = nc.dram_tensor("y", [TOWN, D], F32, kind="ExternalOutput").ap()

    wg_raw = nc.dram_tensor("wg_raw", [NSL, D], BF16).ap()
    wu_raw = nc.dram_tensor("wu_raw", [NSL, D], BF16).ap()
    wd_raw = nc.dram_tensor("wd_raw", [D, NSL], BF16).ap()
    wg_dec = nc.dram_tensor("wg_dec", [NT, DC, 128, 128], BF16).ap()
    wu_dec = nc.dram_tensor("wu_dec", [NT, DC, 128, 128], BF16).ap()
    wd_dec = nc.dram_tensor("wd_dec", [NT, 128, D], BF16).ap()
    ag_in = [nc.dram_tensor(f"ag_in{j}", [D, 128], BF16).ap() for j in range(NCHUNK)]
    ag_out = [nc.dram_tensor(f"ag_out{j}", [W * D, 128], BF16,
                             addr_space="Shared").ap() for j in range(NCHUNK)]
    rs_in = [nc.dram_tensor(f"rs_in{j}", [CT, D], F32).ap() for j in range(NCHUNK)]
    rs_out = [nc.dram_tensor(f"rs_out{j}", [128, D], F32).ap() for j in range(NCHUNK)]
    sent_in = nc.dram_tensor("sent_in", [8, 8], F32).ap()
    sent_out = nc.dram_tensor("sent_out", [64, 8], F32).ap()
    sent_in2 = nc.dram_tensor("sent_in2", [8, 8], F32).ap()
    sent_out2 = nc.dram_tensor("sent_out2", [64, 8], F32).ap()
    dummy_d = nc.dram_tensor("dummy_d", [8, 8], F32).ap()
    xmid_sp = nc.dram_tensor("xmid_sp", [TOWN, D], F32).ap()
    fsum_sp = nc.dram_tensor("fsum_sp", [TOWN, D], F32).ap()

    E = Sched()
    stack = ExitStack()

    def sb(name, shape, dt):
        return stack.enter_context(nc.sbuf_tensor(name, shape, dt))

    def psum_t(name, shape, dt):
        return stack.enter_context(nc.psum_tensor(name, shape, dt))

    idn = sb("idn", [128, 128], BF16)
    AR2 = sb("ar2", [128, 30848], BF16)
    xtbuf = sb("xtbuf", [128, 16384], BF16)
    AR3 = sb("ar3", [128, 16384], BF16)
    wkvp = sb("wkvp", [128, 16384], BF16)
    estage = sb("estage", [128, 4096], F32)
    xbfst = sb("xbfst", [128, 4096], BF16)
    dpk_sb = sb("dpkb", [128, 2048], I32)
    dec2 = sb("dec2", [128, 4096], I16)
    dw_sb = sb("dw", [128, 2048], BF16)
    dscl = sb("dscl", [128, 128], F32)
    dsx = sb("dsx", [128, 128], F32)
    bc_sb = sb("bc", [64, 512], F32)
    rcp_sb = sb("rcp", [1, 512], F32)
    ones1 = sb("ones1", [1, 64], F32)
    biasS = sb("biasS", [128, 1], F32)
    stats = sb("stats", [128, 128], F32)

    masks_v = AR2.ap()[:, 0:8192]
    qt_v = AR2.ap()[:, 8192:12288]
    kt_v = AR2.ap()[:, 12288:20480]
    v_v = AR2.ap()[:, 20480:28800]
    pt_v = AR2.ap()[:, 28800:30848]
    wqp = AR3.ap()[:, 0:8192]
    zt_v = AR3.ap()[:, 8192:12288]
    xmb_v = AR3.ap()[:, 12288:16384]
    AR3f = AR3.ap().bitcast(F32)
    osq_scr = AR3f[:, 6144:8192]
    kp_v = wkvp.ap()[:, 0:8192]
    vp_v = wkvp.ap()[:, 8192:16384]
    eA = estage.ap()[:, 0:2048]
    eB = estage.ap()[:, 2048:4096]
    wst = [estage.ap()[:, 1024 * i:1024 * (i + 1)] for i in range(4)]
    wst3 = [w.rearrange("p (c k) -> p c k", k=64) for w in wst]
    xbff = xbfst.ap().bitcast(F32)
    dpkf = dpk_sb.ap().bitcast(F32)
    dec2bf = dec2.ap().bitcast(BF16)
    dec2f = dec2.ap().bitcast(F32)
    xm_v = xtbuf.ap()
    inter_v = wkvp.ap()
    wd_v = AR2.ap()[:, 0:16384]
    wgs_v = AR3.ap()[:, 0:4096]
    wus_v = AR3.ap()[:, 4096:8192]
    rsst_v = AR3f[:, 4096:8192]
    gg_v = xbff[:, 0:1024]
    g2_v = xbff[:, 1024:2048]
    t1_v = estage.ap()[:, 0:1024]
    p4a = dpkf[:, 0:2048]
    p4b = dec2f[:, 0:2048]

    PS = [psum_t(f"ps{i}", [128, 512], F32) for i in range(5)]
    PTB = [psum_t("pst0", [128, 1024], BF16), psum_t("pst1", [128, 1024], BF16)]
    BC = psum_t("psb", [64, 512], F32)

    sems = {}
    for s in ("DL", "DG", "CC", "PE", "AC", "DV", "PL", "FN", "FG"):
        sems[s] = stack.enter_context(nc.semaphore(s))

    def fence(eng="sync"):
        dsem = "DL" if eng == "sync" else "DG"
        fsem = "FN" if eng == "sync" else "FG"
        v = E.n(dsem)
        E.cnt[fsem] = E.cnt.get(fsem, 0) + 1
        fv = E.cnt[fsem]
        E.ops[eng].append((lambda e, h=sems[fsem]: e.sem_inc(h, 1), ((dsem, v),), None, 0))
        return fv

    E.op("vector", lambda e: e.memset(biasS[:, :], EXP_SHIFT), inc="DV")
    E.op("vector", lambda e: e.memset(ones1[:, :], 1.0), inc="DV")
    E.op("vector", lambda e: e.memset(
        v_v.rearrange("p (t g c) -> p t g c", t=16, g=8)[:, :, :, 64:65], 1.0), inc="DV")
    E.op("vector", lambda e: e.memset(stats.ap()[:, 8:16], 0.0), inc="DV")
    E.op("vector", lambda e: e.memset(stats.ap()[:, 72:80], 0.0), inc="DV")
    E.dma("sync", lambda e: e.dma_start(out=idn[:], in_=ident_in[:, :]))
    for i in range(16):
        E.dma("sync", lambda e, i=i: e.dma_start(
            out=masks_v[:, i * 512:(i + 1) * 512], in_=masks_in[i, :, :]))
    f_init = fence()

    dec_pl = {}
    dec_raw = {}
    dec_nib = {}
    dec_state = {"k": 0}
    dec_srcs = (
        [(gpk[128 * t:128 * (t + 1), :], gsc[128 * t:128 * (t + 1), :],
          wg_raw[128 * t:128 * (t + 1), :]) for t in range(16)]
        + [(upk[128 * t:128 * (t + 1), :], usc[128 * t:128 * (t + 1), :],
            wu_raw[128 * t:128 * (t + 1), :]) for t in range(16)]
        + [(dpk_in[128 * t:128 * (t + 1), :], dsc_in[128 * t:128 * (t + 1), :],
            wd_raw[128 * t:128 * (t + 1), :]) for t in range(16)]
    )

    def decode_next(n):
        if not BUILD_DECODE:
            return
        for _ in range(n):
            k = dec_state["k"]
            if k >= 48:
                return
            dec_state["k"] = k + 1
            pk_src, sc_src, raw_dst = dec_srcs[k]
            bi = k % 2
            pk_v = dpk_sb.ap()[:, bi * 1024:(bi + 1) * 1024]
            sc_v = dscl.ap()[:, bi * 64:(bi + 1) * 64]
            sx_v = dsx.ap()[:, bi * 64:(bi + 1) * 64]
            if k >= 1:
                _, _, prev_dst = dec_srcs[k - 1]
                dec_raw[k - 1] = E.dma("sync", lambda e, d=prev_dst: e.dma_start(
                    out=d, in_=dw_sb[:]), waits=[("PL", dec_pl[k - 1])])
            war = [("DV", dec_nib[k - 2])] if k >= 2 else []
            E.dma("sync", lambda e, o=pk_v, s=pk_src: e.dma_start(out=o, in_=s), waits=war)
            csc = E.dma("sync", lambda e, o=sc_v, s=sc_src: e.dma_start(out=o, in_=s))
            fdec = fence()
            w0 = [("FN", fdec)]
            if k >= 1:
                w0.append(("PL", dec_pl[k - 1]))
            cnib = E.op("vector", lambda e, i2=pk_v: e.tensor_copy(
                dec2[:, 2048:3072], i2), waits=w0, inc="DV")
            dec_nib[k] = cnib
            d1 = E.op("vector", lambda e: e.tensor_scalar(
                dec2[:, 0:1024], dec2[:, 2048:3072], 15, None, ALU.bitwise_and),
                waits=[("DV", cnib)], inc="DV")
            d2 = E.op("vector", lambda e: e.tensor_scalar(
                dec2[:, 1024:2048], dec2[:, 2048:3072], 4, None, ALU.logical_shift_right),
                inc="DV")
            d3 = E.op("vector", lambda e: e.tensor_scalar(
                dec2[:, 2048:4096], dec2[:, 0:2048], 8, 12, ALU.bitwise_and, ALU.logical_shift_left),
                waits=[("DV", d2)], inc="DV")
            d4 = E.op("vector", lambda e: e.tensor_scalar(
                dec2[:, 0:2048], dec2[:, 0:2048], 7, 6, ALU.bitwise_and, ALU.logical_shift_left),
                waits=[("DV", d3)], inc="DV")
            E.op("vector", lambda e, sv=sc_v, xv=sx_v: e.tensor_scalar(
                xv, sv, P126, None, ALU.mult))
            cdv = E.op("vector", lambda e: e.tensor_tensor(
                dec2[:, 0:2048], dec2[:, 0:2048], dec2[:, 2048:4096], ALU.bitwise_or),
                waits=[("DV", d4)], inc="DV")
            mb4 = dec2bf[:, 0:2048].rearrange("p (h a b) -> p h a b", h=2, b=16)
            wv4 = dw_sb.ap().rearrange("p (h a b) -> p h a b", h=2, b=16)
            sx4 = sx_v.unsqueeze(1).unsqueeze(-1).broadcast_to([128, 2, 64, 16])
            plw = [("DV", cdv), ("FN", fdec)]
            dec_pl[k] = E.op("gpsimd", lambda e, a=wv4, b=mb4, c=sx4: e.tensor_tensor(
                a, b, c, ALU.mult), waits=plw, inc="PL")

    def decode_flush():
        if not BUILD_DECODE:
            return
        decode_next(48 - dec_state["k"])
        k = 47
        _, _, dst = dec_srcs[k]
        dec_raw[k] = E.dma("sync", lambda e, d=dst: e.dma_start(
            out=d, in_=dw_sb[:]), waits=[("PL", dec_pl[k])])

    def strided_copy(dst4, src3, waits, inc=None):
        return E.op("vector", lambda e, d=dst4, s2=src3: e.tensor_copy(d, s2),
                    waits=waits, inc=inc)


    def wk_ap(a, dc):
        return kp_v[:, a * 2048 + dc * 128: a * 2048 + (dc + 1) * 128]

    def wv_ap(dc):
        return vp_v[:, dc * 512:(dc + 1) * 512]

    dv_prev = None
    decode_next(2)
    for a in range(NPAIR):
        srcs = [wq_in[8 * a + qh].rearrange("(c p) k -> p c k", p=128) for qh in range(8)]
        l0 = [None] * 8
        for qh in range(4):
            wts = []
            if qh == 0 and dv_prev:
                wts = [("DV", dv_prev)]
            l0[qh] = E.dma("sync", lambda e, s=srcs[qh], d=wst3[qh]: e.dma_start(out=d, in_=s),
                           waits=wts)
        fq1 = fence()
        c01 = E.op("vector", lambda e: e.tensor_tensor(wst[0], wst[0], wst[1], ALU.add),
                   waits=[("FN", fq1)], inc="DV")
        c23 = E.op("vector", lambda e: e.tensor_tensor(wst[2], wst[2], wst[3], ALU.add),
                   inc="DV")
        c02 = E.op("vector", lambda e: e.tensor_tensor(wst[0], wst[0], wst[2], ALU.add),
                   waits=[("DV", c23)], inc="DV")
        l4 = E.dma("sync", lambda e, s=srcs[4], d=wst3[1]: e.dma_start(out=d, in_=s),
                   waits=[("DV", c01)])
        l5 = E.dma("sync", lambda e, s=srcs[5], d=wst3[2]: e.dma_start(out=d, in_=s),
                   waits=[("DV", c02)])
        l6 = E.dma("sync", lambda e, s=srcs[6], d=wst3[3]: e.dma_start(out=d, in_=s))
        fq2 = fence()
        c45 = E.op("vector", lambda e: e.tensor_tensor(wst[1], wst[1], wst[2], ALU.add),
                   waits=[("FN", fq2)], inc="DV")
        l7 = E.dma("sync", lambda e, s=srcs[7], d=wst3[2]: e.dma_start(out=d, in_=s),
                   waits=[("DV", c45)])
        fq3 = fence()
        c46 = E.op("vector", lambda e: e.tensor_tensor(wst[1], wst[1], wst[3], ALU.add),
                   waits=[("DV", c45)], inc="DV")
        c47 = E.op("vector", lambda e: e.tensor_tensor(wst[1], wst[1], wst[2], ALU.add),
                   waits=[("FN", fq3), ("DV", c46)], inc="DV")
        dst = wqp.rearrange("p (a c two k) -> p a c two k", a=4, two=2, k=64)[:, a]
        strided_copy(dst[:, :, 0, :], wst3[0], [("DV", c02)])
        dv_prev = strided_copy(dst[:, :, 1, :], wst3[1], [("DV", c47)], inc="DV")
        decode_next(2)
    wq_done = dv_prev

    for a in range(NPAIR):
        lds = []
        for h in range(2):
            src = wk_in[2 * a + h].rearrange("(c p) k -> p c k", p=128)
            lds.append(E.dma("sync", lambda e, s=src, d=wst3[h]: e.dma_start(out=d, in_=s),
                             waits=[("DV", dv_prev)] if h == 0 else []))
        fkv = fence()
        dst = kp_v[:, a * 2048:(a + 1) * 2048].rearrange("p (c two k) -> p c two k", two=2, k=64)
        strided_copy(dst[:, :, 0, :], wst3[0], [("FN", fkv)])
        dv_prev = strided_copy(dst[:, :, 1, :], wst3[1], [], inc="DV")
    decode_next(2)
    for a in range(NPAIR):
        lds = []
        for h in range(2):
            src = wv_in[2 * a + h].rearrange("(c p) k -> p c k", p=128)
            lds.append(E.dma("sync", lambda e, s=src, d=wst3[h]: e.dma_start(out=d, in_=s),
                             waits=[("DV", dv_prev)] if h == 0 else []))
        fkv = fence()
        dst = vp_v.rearrange("p (c a two k) -> p c a two k", a=4, two=2, k=64)[:, :, a]
        strided_copy(dst[:, :, 0, :], wst3[0], [("FN", fkv)])
        dv_prev = strided_copy(dst[:, :, 1, :], wst3[1], [], inc="DV")
    wkv_done = dv_prev
    decode_next(2)

    pt_state = {"i": 0}
    pt_hist = []

    def transpose_batch(items, pe_waits, ev_waits=()):
        b = pt_state["i"]
        pt_state["i"] += 1
        T = PTB[b % 2].ap()
        w = list(pe_waits)
        if b >= 2:
            w.append(pt_hist[b - 2])
        cpes = []
        for si, (in_ap, out_ap) in enumerate(items):
            pv = T[:, si * 128:(si + 1) * 128]
            cpe = E.op("tensor", lambda e, p=pv, i2=in_ap: e.transpose(p, i2, idn[:]),
                       waits=w if si == 0 else (), inc="PE")
            cpes.append(cpe)
        last_ev = None
        for si, (in_ap, out_ap) in enumerate(items):
            pv = T[:, si * 128:(si + 1) * 128]
            last_ev = E.op("scalar", lambda e, p=pv, o=out_ap: e.activation(o, p, AF.Copy),
                           waits=[("PE", cpes[-1])] + (list(ev_waits) if si == 0 else []),
                           inc="AC")
        pt_hist.append(("AC", last_ev))
        return cpes[-1], ("AC", last_ev)

    xt_hist = []

    def build_xt_tile(src_rows, seq, xt_col0):
        bi = seq % 2
        xb = xbfst.ap()[:, bi * 2048:(bi + 1) * 2048]
        war = [("PE", xt_hist[-2])] if len(xt_hist) >= 2 else []
        cl = E.dma("gpsimd", lambda e, o=xb, s=src_rows: e.dma_start(out=o, in_=s),
                   waits=war)
        fxt = fence("gpsimd")
        last_pe = None
        last_dv = None
        for g in range(DC // 4):
            items = [(xb[:, (4 * g + u) * 128:(4 * g + u + 1) * 128],
                      xtbuf.ap()[:, (4 * g + u) * 1024 + xt_col0:
                                 (4 * g + u) * 1024 + xt_col0 + 128]) for u in range(4)]
            last_pe, last_dv = transpose_batch(
                items, [("FG", fxt), ("FN", f_init)] if g == 0 else [])
        xt_hist.append(last_pe)
        return last_dv

    bank_free = {i: None for i in range(6)}

    def mm(bank, lhsT, rhs, start, stop, waits=(), inc=False, out_ap=None):
        out = out_ap if out_ap is not None else PS[bank].ap()[:, :]
        return E.op("tensor",
                    lambda e, o=out, l=lhsT, r=rhs, st=start, sp=stop:
                    e.matmul(o, l, r, start=st, stop=sp, skip_group_check=True),
                    waits=waits, inc="PE" if inc else None)

    xt_seq = 0
    xtq_done = None
    for i in range(8):
        xtq_done = build_xt_tile(x_own[128 * i:128 * (i + 1), :], xt_seq, i * 128)
        xt_seq += 1
        decode_next(1)
    q_last_ac = None
    for a in range(NPAIR):
        for sl in range(2):
            wts = []
            if a == 0 and sl == 0:
                wts = [xtq_done, ("DV", wq_done)]
            if bank_free[sl]:
                wts.append(bank_free[sl])
            for dcx in range(DC):
                c = mm(sl, wqp[:, a * 2048 + dcx * 128: a * 2048 + (dcx + 1) * 128],
                       xtbuf.ap()[:, dcx * 1024 + sl * 512: dcx * 1024 + sl * 512 + 512],
                       dcx == 0, dcx == DC - 1,
                       waits=wts if dcx == 0 else (), inc=(dcx == DC - 1))
            q_last_ac = E.op("scalar", lambda e, a=a, sl=sl, b=sl: e.activation(
                qt_v[:, a * 1024 + sl * 512: a * 1024 + sl * 512 + 512],
                PS[b].ap()[:, :], AF.Copy), waits=[("PE", c)], inc="AC")
            bank_free[sl] = ("AC", q_last_ac)
    decode_next(2)

    kv_last_ac = None
    for half in range(2):
        xt_done = None
        for i in range(8):
            t = 8 * half + i
            xt_done = build_xt_tile(x_full[128 * t:128 * (t + 1), :], xt_seq, i * 128)
            xt_seq += 1
            decode_next(1)
        for a in range(NPAIR):
            for kc2 in range(2):
                kc = 2 * half + kc2
                bank = kc2
                wts = [xt_done]
                if half == 0 and a == 0 and kc2 == 0:
                    wts.append(("DV", wkv_done))
                if bank_free[bank]:
                    wts.append(bank_free[bank])
                for dcx in range(DC):
                    c = mm(bank, wk_ap(a, dcx),
                           xtbuf.ap()[:, dcx * 1024 + kc2 * 512: dcx * 1024 + kc2 * 512 + 512],
                           dcx == 0, dcx == DC - 1,
                           waits=wts if dcx == 0 else (), inc=(dcx == DC - 1))
                kv_last_ac = E.op("scalar", lambda e, a=a, kc=kc, b=bank: e.activation(
                    kt_v[:, a * 2048 + kc * 512: a * 2048 + kc * 512 + 512],
                    PS[b].ap()[:, :], AF.Copy), waits=[("PE", c)], inc="AC")
                bank_free[bank] = ("AC", kv_last_ac)
        for i in range(8):
            tt = 8 * half + i
            bank = 2 + (i % 2)
            wts = [xt_done]
            if bank_free[bank]:
                wts.append(bank_free[bank])
            for dcx in range(DC):
                c = mm(bank, xtbuf.ap()[:, dcx * 1024 + i * 128: dcx * 1024 + i * 128 + 128],
                       wv_ap(dcx), dcx == 0, dcx == DC - 1,
                       waits=wts if dcx == 0 else (), inc=(dcx == DC - 1))
            vout = v_v.rearrange("p (t g c) -> p t g c", t=16, g=8)[:, tt, :, 0:64]
            vin = PS[bank].ap().rearrange("p (g c) -> p g c", g=8)
            kv_last_ac = E.op("scalar", lambda e, o=vout, i2=vin: e.activation(o, i2, AF.Copy),
                              waits=[("PE", c)], inc="AC")
            bank_free[bank] = ("AC", kv_last_ac)
        decode_next(4)

    wo_dv = None
    for a in range(NPAIR):
        wts = [("DV", wo_dv)] if wo_dv else [("AC", q_last_ac), ("DV", wkv_done)]
        c = E.dma("sync", lambda e, s=wo_in[128 * a:128 * (a + 1), :]:
                  e.dma_start(out=eA, in_=s), waits=wts)
        fwo = fence()
        wo_dv = E.op("vector", lambda e, a=a: e.tensor_copy(
            wqp[:, a * 2048:(a + 1) * 2048], eA), waits=[("FN", fwo)], inc="DV")
    wo_done = wo_dv
    decode_flush()

    xbar_hist = []
    pe_snapshot = E.n("PE")

    def emit_xbar(src_ap, dst_ap, wait_dl):
        k = len(xbar_hist)
        bi = k % 2
        tv = xbfst.ap()[:, bi * 2048:(bi + 1) * 2048]
        war = [("DL", xbar_hist[k - 2])] if k >= 2 else [("PE", pe_snapshot)]
        ct = E.dma("sync", lambda e, o=tv, s=src_ap: e.dma_start_transpose(o, s),
                   waits=[("DL", wait_dl)] + war)
        cw = E.dma("sync", lambda e, d=dst_ap, s=tv: e.dma_start(out=d, in_=s),
                   waits=[("DL", ct)])
        xbar_hist.append(cw)
        return cw

    if BUILD_DECODE:
        for dcx in range(DC):
            emit_xbar(wg_raw[:, 128 * dcx:128 * (dcx + 1)],
                      wg_dec.rearrange("nt dc d j -> dc d nt j")[dcx], dec_raw[15])
        for dcx in range(DC):
            emit_xbar(wu_raw[:, 128 * dcx:128 * (dcx + 1)],
                      wu_dec.rearrange("nt dc d j -> dc d nt j")[dcx], dec_raw[31])
    if BUILD_DECODE:
        for ntk in range(NT):
            emit_xbar(wd_raw[:, 128 * ntk:128 * (ntk + 1)], wd_dec[ntk], dec_raw[47])
        wg_ready = xbar_hist[DC - 1]
        wu_ready = xbar_hist[2 * DC - 1]
        wd_ready = xbar_hist[-1]
    else:
        wg_ready = wu_ready = wd_ready = 0

    pt_war = {0: None, 1: None, 2: None, 3: None}
    bc_cbc = None
    bc_cev = None
    zn_last = None
    for a in range(NPAIR):
        for sl in range(2):
            ext = EXT[sl]
            for h in range(2):
                last_av = None
                for kt in range(ext):
                    sbank = kt % 4
                    wts = []
                    if a == 0 and sl == 0 and kt == 0 and h == 0:
                        wts.append(("AC", kv_last_ac))
                        wts.append(("DV", 5))
                    if bank_free[sbank]:
                        wts.append(bank_free[sbank])
                    c = mm(sbank,
                           kt_v[64 * h:64 * (h + 1),
                                a * 2048 + kt * 128: a * 2048 + (kt + 1) * 128],
                           qt_v[64 * h:64 * (h + 1),
                                a * 1024 + sl * 512: a * 1024 + sl * 512 + 512],
                           True, True, waits=wts, inc=True)
                    slot = kt % 4
                    pv = pt_v[:, slot * 512:(slot + 1) * 512]
                    ew = [("PE", c)]
                    if pt_war[slot]:
                        ew.append(("PE", pt_war[slot]))
                    ce = E.op("scalar", lambda e, o=pv, b=sbank: e.activation(
                        o, PS[b].ap()[:, :], AF.Exp, scale=0.125, bias=biasS[:, :]),
                        waits=ew, inc="AC")
                    bank_free[sbank] = ("AC", ce)
                    if (sl == 0) or (kt >= 8):
                        mv = masks_v[:, kt * 512:(kt + 1) * 512]
                        cm = E.op("vector", lambda e, p=pv, m=mv: e.tensor_tensor(
                            p, p, m, ALU.mult),
                            waits=[("AC", ce), ("FN", f_init)], inc="DV")
                        pwait = ("DV", cm)
                    else:
                        pwait = ("AC", ce)
                    vslice = v_v[:, kt * 520 + (2 * a + h) * 65:
                                 kt * 520 + (2 * a + h) * 65 + 65]
                    avw = [pwait]
                    if kt == 0 and bank_free.get(4):
                        avw.append(bank_free[4])
                    cav = mm(None, vslice, pv, kt == 0, kt == ext - 1,
                             waits=avw, inc=True, out_ap=PS[4].ap()[0:65, :])
                    pt_war[slot] = cav
                    last_av = cav
                crw = [("PE", last_av)]
                if bc_cbc:
                    crw.append(("PE", bc_cbc))
                crc = E.op("vector", lambda e: e.reciprocal(
                    rcp_sb[:, :], PS[4].ap()[64:65, :]), waits=crw, inc="DV")
                bcw = [("DV", crc)]
                if bc_cev:
                    bcw.append(("AC", bc_cev))
                bc_cbc = E.op("tensor", lambda e: e.matmul(
                    BC[:, :], ones1[:, :], rcp_sb[:, :], start=True, stop=True,
                    skip_group_check=True), waits=bcw, inc="PE")
                bc_cev = E.op("scalar", lambda e: e.activation(bc_sb[:, :], BC[:, :], AF.Copy),
                              waits=[("PE", bc_cbc)] + ([("DV", zn_last)] if zn_last else []),
                              inc="AC")
                zn_last = E.op("vector", lambda e, a=a, sl=sl, h=h: e.tensor_tensor(
                    zt_v[64 * h:64 * (h + 1),
                         a * 1024 + sl * 512: a * 1024 + sl * 512 + 512],
                    PS[4].ap()[0:64, :], bc_sb[:, :], ALU.mult),
                    waits=[("AC", bc_cev)], inc="DV")
                bank_free[4] = ("DV", zn_last)
    z_done = zn_last

    sum_hist = []
    sq_hist = []
    spill_hist = []
    ag_cc = []
    ag_cag = []
    ap_hist = []
    st_ = stats.ap()
    for i in range(8):
        bi = i % 2
        xo = eA if bi == 0 else eB
        xb16 = xmb_v[:, bi * 2048:(bi + 1) * 2048]
        xmt = dec2bf[:, bi * 2048:(bi + 1) * 2048]
        war = [("DV", wo_done)]
        if i >= 2:
            war = [("DL", spill_hist[i - 2]), ("AC", ap_hist[i - 2][0])]
        cl = E.dma("sync", lambda e, o=xo, s=x_own[128 * i:128 * (i + 1), :]:
                   e.dma_start(out=o, in_=s), waits=war)
        fxo = fence()
        cadd_last = None
        for dq in range(4):
            wts = []
            if i == 0 and dq == 0:
                wts = [("DV", z_done), ("DV", wo_done)]
            if bank_free[dq]:
                wts.append(bank_free[dq])
            for a in range(NPAIR):
                c = mm(dq, zt_v[:, a * 1024 + i * 128: a * 1024 + (i + 1) * 128],
                       wqp[:, a * 2048 + dq * 512: a * 2048 + dq * 512 + 512],
                       a == 0, a == NPAIR - 1,
                       waits=wts if a == 0 else (), inc=(a == NPAIR - 1))
            cadd_last = E.op("vector", lambda e, o=xo, dq=dq, b=dq: e.tensor_tensor(
                o[:, dq * 512:(dq + 1) * 512], PS[b].ap()[:, :],
                o[:, dq * 512:(dq + 1) * 512], ALU.add),
                waits=[("PE", c), ("FN", fxo)], inc="DV")
            bank_free[dq] = ("DV", cadd_last)
        cs = E.op("vector", lambda e, x=xo, i=i: e.tensor_reduce(
            st_[:, i:i + 1], x, axis=mybir.AxisListType.X, op=ALU.add),
            waits=[("DV", cadd_last)], inc="DV")
        sum_hist.append(cs)
        cq = E.op("scalar", lambda e, x=xo, i=i: e.activation(
            osq_scr, x, AF.Square, accum_out=st_[:, 8 + i:9 + i]),
            waits=[("DV", cadd_last)], inc="AC")
        sq_hist.append(cq)
        p1 = E.op("vector", lambda e, i=i: e.tensor_scalar(
            st_[:, 16 + i:17 + i], st_[:, i:i + 1], 1.0 / D, None, ALU.mult),
            waits=[("DV", cs), ("AC", cq)], inc="DV")
        p3 = E.op("vector", lambda e, i=i: e.tensor_tensor(
            st_[:, 32 + i:33 + i], st_[:, 16 + i:17 + i], st_[:, 16 + i:17 + i],
            ALU.mult), waits=[("DV", p1)], inc="DV")
        p4 = E.op("vector", lambda e, i=i: e.scalar_tensor_tensor(
            st_[:, 40 + i:41 + i], st_[:, 8 + i:9 + i], 1.0 / D,
            st_[:, 32 + i:33 + i], ALU.mult, ALU.subtract),
            waits=[("DV", p3)], inc="DV")
        p5 = E.op("vector", lambda e, i=i: e.tensor_scalar(
            st_[:, 40 + i:41 + i], st_[:, 40 + i:41 + i], LN_EPS, None, ALU.add),
            waits=[("DV", p4)], inc="DV")
        p6 = E.op("scalar", lambda e, i=i: e.activation(
            st_[:, 48 + i:49 + i], st_[:, 40 + i:41 + i], AF.Sqrt),
            waits=[("DV", p5)], inc="AC")
        p7 = E.op("vector", lambda e, i=i: e.reciprocal(
            st_[:, 32 + i:33 + i], st_[:, 48 + i:49 + i]), waits=[("AC", p6)], inc="DV")
        p8 = E.op("vector", lambda e, i=i: e.tensor_tensor(
            st_[:, 56 + i:57 + i], st_[:, 16 + i:17 + i], st_[:, 32 + i:33 + i],
            ALU.mult), waits=[("DV", p7)], inc="DV")
        p9 = E.op("vector", lambda e, i=i: e.tensor_scalar(
            st_[:, 56 + i:57 + i], st_[:, 56 + i:57 + i], -1.0, None, ALU.mult),
            waits=[("DV", p8)], inc="DV")
        aw = [("DV", p9)]
        if i >= 2:
            aw.append(("PE", ap_hist[i - 2][3]))
        cbf = E.op("scalar", lambda e, o=xb16, x=xo, i=i: e.activation(
            o, x, AF.Identity, scale=st_[:, 32 + i:33 + i],
            bias=st_[:, 56 + i:57 + i]), waits=aw, inc="AC")
        cf32 = E.op("scalar", lambda e, x=xo, i=i: e.activation(
            x, x, AF.Identity, scale=st_[:, 32 + i:33 + i],
            bias=st_[:, 56 + i:57 + i]), inc="AC")
        csp = E.dma("sync", lambda e, x=xo, i=i: e.dma_start(
            out=xmid_sp[128 * i:128 * (i + 1), :], in_=x), waits=[("AC", cf32)])
        spill_hist.append(csp)
        last_pe = None
        last_ev = None
        for g in range(4):
            items = []
            for u in range(4):
                pc = 4 * g + u
                if pc < 8:
                    src = xb16[:, 256 * pc: 256 * pc + 256: 2]
                else:
                    src = xb16[:, 256 * (pc - 8) + 1: 256 * (pc - 8) + 256: 2]
                items.append((src, xmt[:, pc * 128:(pc + 1) * 128]))
            evw = []
            if g == 0:
                evw = [("PL", dec_pl[47])] if (BUILD_DECODE and i == 0) else []
                if i >= 2:
                    evw.append(("FN", ap_hist[i - 2][2]))
            last_pe, last_ev = transpose_batch(
                items, [("AC", cbf)] if g == 0 else [], ev_waits=evw)
        for q4 in range(4):
            E.dma("sync", lambda e, i=i, q4=q4, s=xmt: e.dma_start(
                out=ag_in[i].rearrange("(pc d) t -> d pc t", d=128)[:, 4 * q4:4 * (q4 + 1), :],
                in_=s.rearrange("p (pc t) -> p pc t", t=128)[:, 4 * q4:4 * (q4 + 1), :]),
                waits=[last_ev] if q4 == 0 else [])
        fcag = fence()
        ag_cag.append(fcag)
        if BUILD_PHASE == "full":
            ag_cc.append(E.op("gpsimd", lambda e, i=i: e.collective_compute(
                "AllGather", ALU.bypass, replica_groups=[list(range(W))],
                ins=[ag_in[i].opt()], outs=[ag_out[i].opt()]),
                waits=[("FN", fcag)], inc="CC"))
        ap_hist.append((cf32, csp, fcag, last_pe))

    if BUILD_PHASE == "attn":
        outl = None
        for i in range(8):
            outl = E.dma("sync", lambda e, i=i: e.dma_start(
                out=y_out[128 * i:128 * (i + 1), :],
                in_=xmid_sp[128 * i:128 * (i + 1), :]),
                waits=[("DL", ap_hist[i][1])])
        E.ops["sync"].append((None, (("DL", outl),), None, 0))
        with nc.Block() as block:
            def make(engname):
                def emitter(eng):
                    last = {}
                    for fn, waits, inc, amt in E.ops[engname]:
                        for s2, v in waits:
                            if v is None or v == 0:
                                continue
                            if last.get(s2, -1) < v:
                                eng.wait_ge(sems[s2], v)
                                last[s2] = v
                        if fn is None:
                            continue
                        inst = fn(eng)
                        if inc is not None:
                            inst.then_inc(sems[inc], amt)
                return emitter
            block.sync(make("sync"))
            block.tensor(make("tensor"))
            block.vector(make("vector"))
            block.scalar(make("scalar"))
            block.gpsimd(make("gpsimd"))
        stack.close()
        return nc


    def ln_panels(c0, pw):
        x1 = E.op("vector", lambda e: e.tensor_scalar(
            st_[:, c0 + 16:c0 + 24], st_[:, c0 + 0:c0 + 8], 1.0 / D, None, ALU.mult),
            waits=pw, inc="DV")
        x2 = E.op("vector", lambda e: e.tensor_scalar(
            st_[:, c0 + 24:c0 + 32], st_[:, c0 + 8:c0 + 16], 1.0 / D, None, ALU.mult),
            inc="DV")
        x3 = E.op("vector", lambda e: e.tensor_tensor(
            st_[:, c0 + 32:c0 + 40], st_[:, c0 + 16:c0 + 24], st_[:, c0 + 16:c0 + 24],
            ALU.mult), waits=[("DV", x1)], inc="DV")
        x4 = E.op("vector", lambda e: e.tensor_tensor(
            st_[:, c0 + 40:c0 + 48], st_[:, c0 + 24:c0 + 32], st_[:, c0 + 32:c0 + 40],
            ALU.subtract), waits=[("DV", x3), ("DV", x2)], inc="DV")
        x5 = E.op("vector", lambda e: e.tensor_scalar(
            st_[:, c0 + 40:c0 + 48], st_[:, c0 + 40:c0 + 48], LN_EPS, None, ALU.add),
            waits=[("DV", x4)], inc="DV")
        x6 = E.op("scalar", lambda e: e.activation(
            st_[:, c0 + 48:c0 + 56], st_[:, c0 + 40:c0 + 48], AF.Sqrt),
            waits=[("DV", x5)], inc="AC")
        x7 = E.op("vector", lambda e: e.reciprocal(
            st_[:, c0 + 32:c0 + 40], st_[:, c0 + 48:c0 + 56]),
            waits=[("AC", x6)], inc="DV")
        x8 = E.op("vector", lambda e: e.tensor_tensor(
            st_[:, c0 + 56:c0 + 64], st_[:, c0 + 16:c0 + 24], st_[:, c0 + 32:c0 + 40],
            ALU.mult), waits=[("DV", x7)], inc="DV")
        x9 = E.op("vector", lambda e: e.tensor_scalar(
            st_[:, c0 + 56:c0 + 64], st_[:, c0 + 56:c0 + 64], -1.0, None, ALU.mult),
            waits=[("DV", x8)], inc="DV")
        return x9

    p4_hist = []

    def emit_p4a(j):
        wts = [("CC", rs_cc[j])]
        if p4_hist:
            wts += [("DL", p4_hist[-1][2]), ("AC", p4_hist[-1][3])]
        c1 = E.dma("sync", lambda e, j=j: e.dma_start(out=p4a, in_=rs_out[j][:, :]),
                   waits=wts)
        c2 = E.dma("sync", lambda e, j=j: e.dma_start(
            out=p4b, in_=xmid_sp[128 * j:128 * (j + 1), :]),
            waits=([("DV", p4_hist[-1][4])] if p4_hist else []) + [("DL", ap_hist[j][1])])
        fp4 = fence()
        cadd = E.op("vector", lambda e: e.tensor_tensor(p4a, p4a, p4b, ALU.add),
                    waits=[("FN", fp4)], inc="DV")
        cs = E.op("vector", lambda e, j=j: e.tensor_reduce(
            stats.ap()[:, 64 + j:65 + j], p4a, axis=mybir.AxisListType.X, op=ALU.add),
            waits=[("DV", cadd)], inc="DV")
        cq = E.op("scalar", lambda e, j=j: e.activation(
            eB, p4a, AF.Square, accum_out=stats.ap()[:, 72 + j:73 + j]),
            waits=[("DV", cadd)], inc="AC")
        c3 = E.dma("sync", lambda e, j=j: e.dma_start(
            out=fsum_sp[128 * j:128 * (j + 1), :], in_=p4a),
            waits=[("DV", cs), ("AC", cq)])
        p4_hist.append((c1, c2, c3, cq, cs))

    gate_pe = {}
    up_pe = {}
    t1_pl = {}
    inter_dv = {}
    down_pe = {}
    rs_dl = {}
    rs_cc = []
    rs_fence = {}
    rs_chunk_fence = {}
    wg_dl = {}
    wu_dl = {}
    wd_dl = {}
    xm_dl = {}

    for j in range(NCHUNK):
        if j >= 2:
            emit_p4a(j - 2)
        for r in range(W):
            wts = []
            if r == 0:
                wts.append(("CC", ag_cc[j]))
                if j > 0:
                    wts.append(("PE", up_pe[(j * NT - 1, 1)]))
            src = ag_out[j].rearrange("(r dc d) t -> r d dc t", r=W, d=128)[r]
            dst = xm_v.rearrange("p (dc t) -> p dc t", t=1024)[:, :, 128 * r:128 * (r + 1)]
            for q4 in range(4):
                xm_dl[(j, r)] = E.dma("sync", lambda e, q4=q4, s=src, d=dst: e.dma_start(
                    out=d[:, 4 * q4:4 * (q4 + 1), :], in_=s[:, 4 * q4:4 * (q4 + 1), :]),
                    waits=wts if q4 == 0 else [])
        fxm = fence()
        fgu = {}
        fwd = {}

        def emit_gu_load(nt2):
            gi2 = j * NT + nt2
            bi2 = gi2 % 2
            wts = [("DL", wg_ready)] if j == 0 else []
            if gi2 >= 2:
                wts.append(("PE", gate_pe[(gi2 - 2, 1)]))
            wg_dl[gi2] = E.dma(
                "sync",
                lambda e, o=wgs_v[:, bi2 * 2048:(bi2 + 1) * 2048].rearrange("p (dc j2) -> p dc j2", j2=128),
                s=wg_dec[nt2].rearrange("dc d j2 -> d dc j2"):
                e.dma_start(out=o, in_=s), waits=wts)
            wts = [("DL", wu_ready)] if j == 0 else []
            if gi2 >= 2:
                wts.append(("PE", up_pe[(gi2 - 2, 1)]))
            wu_dl[gi2] = E.dma(
                "sync",
                lambda e, o=wus_v[:, bi2 * 2048:(bi2 + 1) * 2048].rearrange("p (dc j2) -> p dc j2", j2=128),
                s=wu_dec[nt2].rearrange("dc d j2 -> d dc j2"):
                e.dma_start(out=o, in_=s), waits=wts)
            fgu[gi2] = fence()

        def emit_wd_loads(hh):
            for ntk in range(NT):
                wts = [("DL", wd_ready)] if j == 0 else []
                if ntk == 0:
                    if hh == 1:
                        wts.append(("PE", down_pe[(j, 0, 7)]))
                    elif j > 0:
                        wts.append(("PE", down_pe[(j - 1, 1, 7)]))
                wd_dl[(j, hh, ntk)] = E.dma(
                    "sync", lambda e, o=wd_v[:, ntk * 1024:(ntk + 1) * 1024],
                    s=wd_dec[ntk][:, hh * 1024:(hh + 1) * 1024]:
                    e.dma_start(out=o, in_=s), waits=wts)
            fwd[hh] = fence()

        emit_gu_load(0)
        emit_gu_load(1)
        for nt in range(NT):
            gi = j * NT + nt
            bi = gi % 2
            for tq in range(2):
                gbank, ubank = tq, 2 + tq
                prev = (gi - 1, tq)
                wts = [("FN", fgu[gi])]
                if nt == 0 and tq == 0:
                    wts.append(("FN", fxm))
                if bank_free[gbank]:
                    wts.append(bank_free[gbank])
                for dcx in range(DC):
                    c = mm(gbank, wgs_v[:, bi * 2048 + dcx * 128: bi * 2048 + (dcx + 1) * 128],
                           xm_v[:, dcx * 1024 + tq * 512: dcx * 1024 + tq * 512 + 512],
                           dcx == 0, dcx == DC - 1,
                           waits=wts if dcx == 0 else (), inc=(dcx == DC - 1))
                gate_pe[(gi, tq)] = c
                ggv = gg_v[:, tq * 512:(tq + 1) * 512]
                g2v = g2_v[:, tq * 512:(tq + 1) * 512]
                tv = t1_v[:, tq * 512:(tq + 1) * 512]
                ggw = [("PE", c)]
                if prev in t1_pl:
                    ggw.append(("PL", t1_pl[prev]))
                cgg = E.op("scalar", lambda e, o=ggv, b=gbank: e.activation(
                    o, PS[b].ap()[:, :], AF.Copy), waits=ggw, inc="AC")
                bank_free[gbank] = ("AC", cgg)
                cg2 = E.op("scalar", lambda e, o=g2v, i2=ggv: e.activation(
                    o, i2, AF.Square), inc="AC")
                cw1 = E.op("gpsimd", lambda e, o=g2v: e.tensor_scalar(
                    o, o, GC1, GC2, ALU.mult, ALU.add), waits=[("AC", cg2)], inc="PL")
                car = E.op("gpsimd", lambda e, o=g2v, g=ggv: e.tensor_tensor(
                    o, g, o, ALU.mult), inc="PL")
                csg = E.op("scalar", lambda e, o=g2v: e.activation(
                    o, o, AF.Sigmoid), waits=[("PL", car)], inc="AC")
                t1w = [("AC", csg)]
                if j == 0 and nt == 0 and tq == 0 and BUILD_PHASE == "full":
                    t1w.append(("FN", ap_hist[7][2]))
                if prev in inter_dv:
                    t1w.append(("DV", inter_dv[prev]))
                ct1 = E.op("gpsimd", lambda e, o=tv, g=ggv, s2=g2v: e.tensor_tensor(
                    o, g, s2, ALU.mult), waits=t1w, inc="PL")
                t1_pl[(gi, tq)] = ct1
                wts = [("FN", fgu[gi])]
                if bank_free[ubank]:
                    wts.append(bank_free[ubank])
                for dcx in range(DC):
                    c2 = mm(ubank, wus_v[:, bi * 2048 + dcx * 128: bi * 2048 + (dcx + 1) * 128],
                            xm_v[:, dcx * 1024 + tq * 512: dcx * 1024 + tq * 512 + 512],
                            dcx == 0, dcx == DC - 1,
                            waits=wts if dcx == 0 else (), inc=(dcx == DC - 1))
                up_pe[(gi, tq)] = c2
                ivw = [("PL", ct1), ("PE", c2)]
                if j > 0 and nt == 0 and tq == 0:
                    ivw.append(("PE", down_pe[(j - 1, 1, 7)]))
                cin = E.op("vector", lambda e, nt=nt, tq=tq, b=ubank, t=tv: e.tensor_tensor(
                    inter_v[:, nt * 1024 + tq * 512: nt * 1024 + tq * 512 + 512],
                    t, PS[b].ap()[:, :], ALU.mult), waits=ivw, inc="DV")
                inter_dv[(gi, tq)] = cin
                bank_free[ubank] = ("DV", cin)
            if nt + 2 < NT:
                emit_gu_load(nt + 2)
            if nt == 12:
                emit_wd_loads(0)
        inter_done = inter_dv[(j * NT + NT - 1, 1)]
        for hh in range(2):
            if hh == 1:
                emit_wd_loads(1)
            for tt in range(8):
                bk = (tt % 2) * 2
                wts0 = [("DV", inter_done), ("FN", fwd[hh])]
                for hc in range(2):
                    if bank_free[bk + hc]:
                        wts0.append(bank_free[bk + hc])
                first = True
                c = None
                for ntk in range(NT):
                    lhs = inter_v[:, ntk * 1024 + tt * 128: ntk * 1024 + (tt + 1) * 128]
                    for hc in range(2):
                        rhs = wd_v[:, ntk * 1024 + hc * 512: ntk * 1024 + hc * 512 + 512]
                        c = mm(bk + hc, lhs, rhs, ntk == 0, ntk == NT - 1,
                               waits=wts0 if first else (), inc=(ntk == NT - 1))
                        first = False
                down_pe[(j, hh, tt)] = c
                step = hh * 8 + tt
                bi = step % 4
                rv = rsst_v[:, bi * 1024:(bi + 1) * 1024]
                evw = [("PE", c)]
                if rs_fence.get((j, hh, tt)):
                    evw.append(("FN", rs_fence[(j, hh, tt)]))
                E.op("scalar", lambda e, o=rv, b=bk: e.activation(
                    o[:, 0:512], PS[b].ap()[:, :], AF.Copy), waits=evw, inc="AC")
                ca2 = E.op("scalar", lambda e, o=rv, b=bk: e.activation(
                    o[:, 512:1024], PS[b + 1].ap()[:, :], AF.Copy), inc="AC")
                bank_free[bk] = ("AC", ca2)
                bank_free[bk + 1] = ("AC", ca2)
                rs_dl[(j, hh, tt)] = E.dma(
                    "sync", lambda e, j=j, hh=hh, tt=tt, s=rv:
                    e.dma_start(out=rs_in[j][128 * tt:128 * (tt + 1),
                                             hh * 1024:(hh + 1) * 1024], in_=s),
                    waits=[("AC", ca2)])
                if step % 4 == 3:
                    fr = fence()
                    for ahead in range(4):
                        st2 = step + 1 + ahead
                        jj, hh2, tt2 = (j, st2 // 8, st2 % 8) if st2 < 16 else                             (j + 1, (st2 - 16) // 8, (st2 - 16) % 8)
                        rs_fence[(jj, hh2, tt2)] = fr
                    if step == 15:
                        rs_chunk_fence[j] = fr
        rs_cc.append(E.op("gpsimd", lambda e, j=j: e.collective_compute(
            "ReduceScatter", ALU.add, replica_groups=[list(range(W))],
            ins=[rs_in[j].opt()], outs=[rs_out[j].opt()]),
            waits=[("FN", rs_chunk_fence[j])], inc="CC"))

    emit_p4a(NCHUNK - 2)
    emit_p4a(NCHUNK - 1)

    ln2_dv = ln_panels(64, [("AC", p4_hist[-1][3]), ("DV", p4_hist[-1][4])])

    app_prev = None
    out_last = None
    for j in range(NCHUNK):
        wts = [("DL", p4_hist[-1][2]), ("AC", p4_hist[-1][3])] if app_prev is None \
            else [("AC", app_prev[1])]
        c1 = E.dma("sync", lambda e, j=j: e.dma_start(
            out=p4a, in_=fsum_sp[128 * j:128 * (j + 1), :]), waits=wts)
        fp4b = fence()
        aw = [("FN", fp4b), ("DV", ln2_dv)]
        ca = E.op("scalar", lambda e, j=j: e.activation(
            p4b, p4a, AF.Identity, scale=stats.ap()[:, 96 + j:97 + j],
            bias=stats.ap()[:, 120 + j:121 + j]), waits=aw, inc="AC")
        out_last = E.dma("sync", lambda e, j=j: e.dma_start(
            out=y_out[128 * j:128 * (j + 1), :], in_=p4b), waits=[("AC", ca)])
        app_prev = (c1, ca, out_last)

    if BUILD_DBG:
        dbg1 = nc.dram_tensor("dbg_xmid", [TOWN, D], F32, kind="ExternalOutput").ap()
        dbg2 = nc.dram_tensor("dbg_rs", [TOWN, D], F32, kind="ExternalOutput").ap()
        dbg3 = nc.dram_tensor("dbg_fsum", [TOWN, D], F32, kind="ExternalOutput").ap()
        for j in range(NCHUNK):
            out_last = E.dma("sync", lambda e, j=j: e.dma_start(
                out=dbg1[128 * j:128 * (j + 1), :], in_=xmid_sp[128 * j:128 * (j + 1), :]),
                waits=[("DL", out_last)])
            out_last = E.dma("sync", lambda e, j=j: e.dma_start(
                out=dbg2[128 * j:128 * (j + 1), :], in_=rs_out[j][:, :]))
            out_last = E.dma("sync", lambda e, j=j: e.dma_start(
                out=dbg3[128 * j:128 * (j + 1), :], in_=fsum_sp[128 * j:128 * (j + 1), :]))
    E.ops["sync"].append((None, (("DL", out_last),), None, 0))

    if DEBUG_HOOK is not None:
        DEBUG_HOOK(dict(nc=nc, E=E, sems=sems, loc=locals()))

    with nc.Block() as block:
        def make(engname):
            def emitter(eng):
                last = {}
                for fn, waits, inc, amt in E.ops[engname]:
                    for s2, v in waits:
                        if v is None or v == 0:
                            continue
                        if last.get(s2, -1) < v:
                            eng.wait_ge(sems[s2], v)
                            last[s2] = v
                    if fn is None:
                        continue
                    inst = fn(eng)
                    if inc is not None:
                        inst.then_inc(sems[inc], amt)
            return emitter

        block.sync(make("sync"))
        block.tensor(make("tensor"))
        block.vector(make("vector"))
        block.scalar(make("scalar"))
        block.gpsimd(make("gpsimd"))

    stack.close()
    return nc


_PROG_CACHE = {}


def _get_prog():
    if "nc" not in _PROG_CACHE:
        _PROG_CACHE["nc"] = build_program()
    return _PROG_CACHE["nc"]


def _host_inputs(inputs):
    x = np.asarray(inputs["L0_attn_normalized_resid_pre"])
    wq = np.ascontiguousarray(np.asarray(inputs["L0_attn_W_Q"], dtype=np.float32))
    wk = np.ascontiguousarray(np.asarray(inputs["L0_attn_W_K"], dtype=np.float32))
    wv = np.ascontiguousarray(np.asarray(inputs["L0_attn_W_V"], dtype=np.float32))
    wo = np.ascontiguousarray(
        np.asarray(inputs["L0_attn_W_O"], dtype=np.float32).reshape(NKV * DH, D))
    gw = np.asarray(inputs["L0_ffn_gate_W_packed"], dtype=np.int32)
    gs = np.asarray(inputs["L0_ffn_gate_scales"], dtype=np.float32)
    uw = np.asarray(inputs["L0_ffn_up_W_packed"], dtype=np.int32)
    us = np.asarray(inputs["L0_ffn_up_scales"], dtype=np.float32)
    dw = np.asarray(inputs["L0_ffn_down_W_packed"], dtype=np.int32)
    ds = np.asarray(inputs["L0_ffn_down_scales"], dtype=np.float32)
    ident = np.eye(128, dtype=np.float32).astype(ml_dtypes.bfloat16)

    in_maps = []
    for c in range(W):
        b, p = c // 2, c % 2
        qcs = (0, 3) if p == 0 else (1, 2)
        x_full_c = np.ascontiguousarray(x[b], dtype=np.float32)
        x_own_c = np.ascontiguousarray(
            np.concatenate([x[b, 512 * qc:512 * (qc + 1)] for qc in qcs], axis=0),
            dtype=np.float32)
        sl = np.arange(NSL * c, NSL * (c + 1))
        rows = np.concatenate([sl[0::2], sl[1::2]])
        m = np.zeros((16, 128, 512), dtype=np.float32)
        kk = np.arange(128)[:, None]
        qq = np.arange(512)[None, :]
        for kt in range(8):
            m[kt] = (128 * kt + kk <= 512 * qcs[0] + qq)
        for kt in range(8, 16):
            m[kt] = (128 * kt + kk <= 512 * qcs[1] + qq)
        in_maps.append({
            "x_full": x_full_c, "x_own": x_own_c,
            "wq": wq, "wk": wk, "wv": wv, "wo": wo,
            "gpk": np.ascontiguousarray(gw[rows]),
            "gsc": np.ascontiguousarray(gs[rows]),
            "upk": np.ascontiguousarray(uw[rows]),
            "usc": np.ascontiguousarray(us[rows]),
            "dpk": np.ascontiguousarray(dw[:, 1024 * c:1024 * (c + 1)]),
            "dsc": np.ascontiguousarray(ds[:, 64 * c:64 * (c + 1)]),
            "masks": m.astype(ml_dtypes.bfloat16), "ident": ident,
        })
    return in_maps


def kernel(**inputs):
    nc = _get_prog()
    in_maps = _host_inputs(inputs)
    res = run_bass_kernel_spmd(nc, in_maps, core_ids=list(range(W)))
    out = np.empty((B, S, D), dtype=np.float32)
    for c in range(W):
        b, p = c // 2, c % 2
        qcs = (0, 3) if p == 0 else (1, 2)
        yc = res.results[c]["y"]
        out[b, 512 * qcs[0]:512 * (qcs[0] + 1)] = yc[0:512]
        out[b, 512 * qcs[1]:512 * (qcs[1] + 1)] = yc[512:1024]
    return out


if __name__ == "__main__":
    build_program()
    print("program built ok")


# version 08 (final)
# speedup vs baseline: 1.0985x; 1.0628x over previous
import numpy as np
import ml_dtypes
from contextlib import ExitStack

import concourse.bass as bass
import concourse.mybir as mybir
from concourse.bass_utils import run_bass_kernel_spmd

F32 = mybir.dt.float32
BF16 = mybir.dt.bfloat16
I32 = mybir.dt.int32
I16 = mybir.dt.int16
AF = mybir.ActivationFunctionType
ALU = mybir.AluOpType

B, S, D = 4, 2048, 2048
NKV, DH, NPAIR = 8, 64, 4
MLP = 16384
NSL = 2048
NT = 16
DC = 16
TOWN = 1024
NCHUNK = 8
CT = 1024
W = 8
EXT = (8, 16)
LN_EPS = 1e-5
EXP_SHIFT = -16.0
GC2 = 2.0 * 0.7978845608028654
GC1 = GC2 * 0.044715
P126 = float(2.0 ** 126)

ENGINES = ("sync", "tensor", "vector", "scalar", "gpsimd")
DEBUG_HOOK = None
BUILD_PHASE = "full"
BUILD_DECODE = True
BUILD_DBG = False


class Sched:
    def __init__(self):
        self.ops = {e: [] for e in ENGINES}
        self.cnt = {}

    def op(self, eng, fn, waits=(), inc=None, amt=1):
        waits = [w for w in waits if w is not None]
        self.ops[eng].append((fn, tuple(waits), inc, amt))
        if inc is not None:
            self.cnt[inc] = self.cnt.get(inc, 0) + amt
            return self.cnt[inc]
        return None

    def dma(self, eng, fn, waits=()):
        sem = {"sync": "DL", "gpsimd": "DG"}[eng]
        return self.op(eng, fn, waits, inc=sem, amt=16)

    def n(self, sem):
        return self.cnt.get(sem, 0)


def build_program():
    nc = bass.Bass("TRN2", target_bir_lowering=False)

    def din(name, shape, dt):
        return nc.dram_tensor(name, shape, dt, kind="ExternalInput").ap()

    x_full = din("x_full", [S, D], F32)
    x_own = din("x_own", [TOWN, D], F32)
    wq_in = din("wq", [32, D, DH], F32)
    wk_in = din("wk", [NKV, D, DH], F32)
    wv_in = din("wv", [NKV, D, DH], F32)
    wo_in = din("wo", [NKV * DH, D], F32)
    gpk = din("gpk", [NSL, 1024], I32)
    gsc = din("gsc", [NSL, 64], F32)
    upk = din("upk", [NSL, 1024], I32)
    usc = din("usc", [NSL, 64], F32)
    dpk_in = din("dpk", [D, 1024], I32)
    dsc_in = din("dsc", [D, 64], F32)
    masks_in = din("masks", [16, 128, 512], BF16)
    ident_in = din("ident", [128, 128], BF16)
    y_out = nc.dram_tensor("y", [TOWN, D], F32, kind="ExternalOutput").ap()

    wg_raw = nc.dram_tensor("wg_raw", [NSL, D], BF16).ap()
    wu_raw = nc.dram_tensor("wu_raw", [NSL, D], BF16).ap()
    wd_raw = nc.dram_tensor("wd_raw", [D, NSL], BF16).ap()
    wg_dec = nc.dram_tensor("wg_dec", [NT, DC, 128, 128], BF16).ap()
    wu_dec = nc.dram_tensor("wu_dec", [NT, DC, 128, 128], BF16).ap()
    wd_dec = nc.dram_tensor("wd_dec", [NT, 128, D], BF16).ap()
    ag_in = [nc.dram_tensor(f"ag_in{j}", [D, 128], BF16).ap() for j in range(NCHUNK)]
    ag_out = [nc.dram_tensor(f"ag_out{j}", [W * D, 128], BF16,
                             addr_space="Shared").ap() for j in range(NCHUNK)]
    rs_in = [nc.dram_tensor(f"rs_in{j}", [CT, D], F32).ap() for j in range(NCHUNK)]
    rs_out = [nc.dram_tensor(f"rs_out{j}", [128, D], F32).ap() for j in range(NCHUNK)]
    sent_in = nc.dram_tensor("sent_in", [8, 8], F32).ap()
    sent_out = nc.dram_tensor("sent_out", [64, 8], F32).ap()
    sent_in2 = nc.dram_tensor("sent_in2", [8, 8], F32).ap()
    sent_out2 = nc.dram_tensor("sent_out2", [64, 8], F32).ap()
    dummy_d = nc.dram_tensor("dummy_d", [8, 8], F32).ap()
    xmid_sp = nc.dram_tensor("xmid_sp", [TOWN, D], F32).ap()
    fsum_sp = nc.dram_tensor("fsum_sp", [TOWN, D], F32).ap()

    E = Sched()
    stack = ExitStack()

    def sb(name, shape, dt):
        return stack.enter_context(nc.sbuf_tensor(name, shape, dt))

    def psum_t(name, shape, dt):
        return stack.enter_context(nc.psum_tensor(name, shape, dt))

    idn = sb("idn", [128, 128], BF16)
    AR2 = sb("ar2", [128, 30848], BF16)
    xtbuf = sb("xtbuf", [128, 16384], BF16)
    AR3 = sb("ar3", [128, 16384], BF16)
    wkvp = sb("wkvp", [128, 16384], BF16)
    estage = sb("estage", [128, 4096], F32)
    xbfst = sb("xbfst", [128, 4096], BF16)
    dpk_sb = sb("dpkb", [128, 2048], I32)
    dec2 = sb("dec2", [128, 4096], I16)
    dw_sb = sb("dw", [128, 2048], BF16)
    dscl = sb("dscl", [128, 128], F32)
    dsx = sb("dsx", [128, 128], F32)
    bc_sb = sb("bc", [64, 512], F32)
    rcp_sb = sb("rcp", [1, 512], F32)
    ones1 = sb("ones1", [1, 64], F32)
    biasS = sb("biasS", [128, 1], F32)
    stats = sb("stats", [128, 128], F32)

    masks_v = AR2.ap()[:, 0:8192]
    qt_v = AR2.ap()[:, 8192:12288]
    kt_v = AR2.ap()[:, 12288:20480]
    v_v = AR2.ap()[:, 20480:28800]
    pt_v = AR2.ap()[:, 28800:30848]
    wqp = AR3.ap()[:, 0:8192]
    zt_v = AR3.ap()[:, 8192:12288]
    xmb_v = AR3.ap()[:, 12288:16384]
    AR3f = AR3.ap().bitcast(F32)
    osq_scr = AR3f[:, 6144:8192]
    kp_v = wkvp.ap()[:, 0:8192]
    vp_v = wkvp.ap()[:, 8192:16384]
    eA = estage.ap()[:, 0:2048]
    eB = estage.ap()[:, 2048:4096]
    wst = [estage.ap()[:, 1024 * i:1024 * (i + 1)] for i in range(4)]
    wst3 = [w.rearrange("p (c k) -> p c k", k=64) for w in wst]
    xbff = xbfst.ap().bitcast(F32)
    dpkf = dpk_sb.ap().bitcast(F32)
    dec2bf = dec2.ap().bitcast(BF16)
    dec2f = dec2.ap().bitcast(F32)
    xm_v = xtbuf.ap()
    inter_v = wkvp.ap()
    wd_v = AR2.ap()[:, 0:16384]
    wgs_v = AR3.ap()[:, 0:4096]
    wus_v = AR3.ap()[:, 4096:8192]
    rsst_v = AR3f[:, 4096:8192]
    gg_v = xbff[:, 0:1024]
    g2_v = xbff[:, 1024:2048]
    t1_v = estage.ap()[:, 0:1024]
    p4a = dpkf[:, 0:2048]
    p4b = dec2f[:, 0:2048]

    PS = [psum_t(f"ps{i}", [128, 512], F32) for i in range(5)]
    PTB = [psum_t("pst0", [128, 1024], BF16), psum_t("pst1", [128, 1024], BF16)]
    BC = psum_t("psb", [64, 512], F32)

    sems = {}
    for s in ("DL", "DG", "CC", "PE", "AC", "DV", "PL", "FN", "FG"):
        sems[s] = stack.enter_context(nc.semaphore(s))

    def fence(eng="sync"):
        dsem = "DL" if eng == "sync" else "DG"
        fsem = "FN" if eng == "sync" else "FG"
        v = E.n(dsem)
        E.cnt[fsem] = E.cnt.get(fsem, 0) + 1
        fv = E.cnt[fsem]
        E.ops[eng].append((lambda e, h=sems[fsem]: e.sem_inc(h, 1), ((dsem, v),), None, 0))
        return fv

    E.op("vector", lambda e: e.memset(biasS[:, :], EXP_SHIFT), inc="DV")
    E.op("vector", lambda e: e.memset(ones1[:, :], 1.0), inc="DV")
    E.op("vector", lambda e: e.memset(
        v_v.rearrange("p (t g c) -> p t g c", t=16, g=8)[:, :, :, 64:65], 1.0), inc="DV")
    E.op("vector", lambda e: e.memset(stats.ap()[:, 8:16], 0.0), inc="DV")
    E.op("vector", lambda e: e.memset(stats.ap()[:, 72:80], 0.0), inc="DV")
    E.dma("sync", lambda e: e.dma_start(out=idn[:], in_=ident_in[:, :]))
    for i in range(16):
        E.dma("sync", lambda e, i=i: e.dma_start(
            out=masks_v[:, i * 512:(i + 1) * 512], in_=masks_in[i, :, :]))
    f_init = fence()

    dec_pl = {}
    dec_raw = {}
    dec_nib = {}
    dec_state = {"k": 0}
    dec_srcs = (
        [(gpk[128 * t:128 * (t + 1), :], gsc[128 * t:128 * (t + 1), :],
          wg_raw[128 * t:128 * (t + 1), :]) for t in range(16)]
        + [(upk[128 * t:128 * (t + 1), :], usc[128 * t:128 * (t + 1), :],
            wu_raw[128 * t:128 * (t + 1), :]) for t in range(16)]
        + [(dpk_in[128 * t:128 * (t + 1), :], dsc_in[128 * t:128 * (t + 1), :],
            wd_raw[128 * t:128 * (t + 1), :]) for t in range(16)]
    )

    def decode_next(n):
        if not BUILD_DECODE:
            return
        for _ in range(n):
            k = dec_state["k"]
            if k >= 48:
                return
            dec_state["k"] = k + 1
            pk_src, sc_src, raw_dst = dec_srcs[k]
            bi = k % 2
            pk_v = dpk_sb.ap()[:, bi * 1024:(bi + 1) * 1024]
            sc_v = dscl.ap()[:, bi * 64:(bi + 1) * 64]
            sx_v = dsx.ap()[:, bi * 64:(bi + 1) * 64]
            if k >= 1:
                _, _, prev_dst = dec_srcs[k - 1]
                dec_raw[k - 1] = E.dma("sync", lambda e, d=prev_dst: e.dma_start(
                    out=d, in_=dw_sb[:]), waits=[("PL", dec_pl[k - 1])])
            war = [("DV", dec_nib[k - 2])] if k >= 2 else []
            E.dma("sync", lambda e, o=pk_v, s=pk_src: e.dma_start(out=o, in_=s), waits=war)
            csc = E.dma("sync", lambda e, o=sc_v, s=sc_src: e.dma_start(out=o, in_=s))
            fdec = fence()
            w0 = [("FN", fdec)]
            if k >= 1:
                w0.append(("PL", dec_pl[k - 1]))
            cnib = E.op("vector", lambda e, i2=pk_v: e.tensor_copy(
                dec2[:, 2048:3072], i2), waits=w0, inc="DV")
            dec_nib[k] = cnib
            d1 = E.op("vector", lambda e: e.tensor_scalar(
                dec2[:, 0:1024], dec2[:, 2048:3072], 15, None, ALU.bitwise_and),
                waits=[("DV", cnib)], inc="DV")
            d2 = E.op("vector", lambda e: e.tensor_scalar(
                dec2[:, 1024:2048], dec2[:, 2048:3072], 4, None, ALU.logical_shift_right),
                inc="DV")
            d3 = E.op("vector", lambda e: e.tensor_scalar(
                dec2[:, 2048:4096], dec2[:, 0:2048], 8, 12, ALU.bitwise_and, ALU.logical_shift_left),
                waits=[("DV", d2)], inc="DV")
            d4 = E.op("vector", lambda e: e.tensor_scalar(
                dec2[:, 0:2048], dec2[:, 0:2048], 7, 6, ALU.bitwise_and, ALU.logical_shift_left),
                waits=[("DV", d3)], inc="DV")
            E.op("vector", lambda e, sv=sc_v, xv=sx_v: e.tensor_scalar(
                xv, sv, P126, None, ALU.mult))
            cdv = E.op("vector", lambda e: e.tensor_tensor(
                dec2[:, 0:2048], dec2[:, 0:2048], dec2[:, 2048:4096], ALU.bitwise_or),
                waits=[("DV", d4)], inc="DV")
            mb4 = dec2bf[:, 0:2048].rearrange("p (h a b) -> p h a b", h=2, b=16)
            wv4 = dw_sb.ap().rearrange("p (h a b) -> p h a b", h=2, b=16)
            sx4 = sx_v.unsqueeze(1).unsqueeze(-1).broadcast_to([128, 2, 64, 16])
            plw = [("DV", cdv), ("FN", fdec)]
            dec_pl[k] = E.op("gpsimd", lambda e, a=wv4, b=mb4, c=sx4: e.tensor_tensor(
                a, b, c, ALU.mult), waits=plw, inc="PL")

    def decode_flush():
        if not BUILD_DECODE:
            return
        decode_next(48 - dec_state["k"])
        k = 47
        _, _, dst = dec_srcs[k]
        dec_raw[k] = E.dma("sync", lambda e, d=dst: e.dma_start(
            out=d, in_=dw_sb[:]), waits=[("PL", dec_pl[k])])

    pt_state = {"i": 0}
    pt_hist = []

    def transpose_batch(items, pe_waits, ev_waits=()):
        b = pt_state["i"]
        pt_state["i"] += 1
        T = PTB[b % 2].ap()
        w = list(pe_waits)
        if b >= 2:
            w.append(pt_hist[b - 2])
        cpes = []
        for si, (in_ap, out_ap) in enumerate(items):
            pv = T[:, si * 128:(si + 1) * 128]
            cpe = E.op("tensor", lambda e, p=pv, i2=in_ap: e.transpose(p, i2, idn[:]),
                       waits=w if si == 0 else (), inc="PE")
            cpes.append(cpe)
        last_ev = None
        for si, (in_ap, out_ap) in enumerate(items):
            pv = T[:, si * 128:(si + 1) * 128]
            last_ev = E.op("scalar", lambda e, p=pv, o=out_ap: e.activation(o, p, AF.Copy),
                           waits=[("PE", cpes[-1])] + (list(ev_waits) if si == 0 else []),
                           inc="AC")
        pt_hist.append(("AC", last_ev))
        return cpes[-1], ("AC", last_ev)

    xt_hist = []

    def build_xt_tile(src_rows, seq, xt_col0):
        bi = seq % 2
        xb = xbfst.ap()[:, bi * 2048:(bi + 1) * 2048]
        war = [("PE", xt_hist[-2])] if len(xt_hist) >= 2 else []
        cl = E.dma("gpsimd", lambda e, o=xb, s=src_rows: e.dma_start(out=o, in_=s),
                   waits=war)
        fxt = fence("gpsimd")
        last_pe = None
        last_dv = None
        for g in range(DC // 4):
            items = [(xb[:, (4 * g + u) * 128:(4 * g + u + 1) * 128],
                      xtbuf.ap()[:, (4 * g + u) * 1024 + xt_col0:
                                 (4 * g + u) * 1024 + xt_col0 + 128]) for u in range(4)]
            last_pe, last_dv = transpose_batch(
                items, [("FG", fxt), ("FN", f_init)] if g == 0 else [])
        xt_hist.append(last_pe)
        return last_dv

    bank_free = {i: None for i in range(6)}

    def mm(bank, lhsT, rhs, start, stop, waits=(), inc=False, out_ap=None):
        out = out_ap if out_ap is not None else PS[bank].ap()[:, :]
        return E.op("tensor",
                    lambda e, o=out, l=lhsT, r=rhs, st=start, sp=stop:
                    e.matmul(o, l, r, start=st, stop=sp, skip_group_check=True),
                    waits=waits, inc="PE" if inc else None)

    xt_seq = 0
    xtq_done = None
    for i in range(8):
        xtq_done = build_xt_tile(x_own[128 * i:128 * (i + 1), :], xt_seq, i * 128)
        xt_seq += 1

    def strided_copy(dst4, src3, waits, inc=None):
        return E.op("vector", lambda e, d=dst4, s2=src3: e.tensor_copy(d, s2),
                    waits=waits, inc=inc)


    def wk_ap(a, dc):
        return kp_v[:, a * 2048 + dc * 128: a * 2048 + (dc + 1) * 128]

    def wv_ap(dc):
        return vp_v[:, dc * 512:(dc + 1) * 512]

    dv_prev = None
    decode_next(2)
    for a in range(NPAIR):
        srcs = [wq_in[8 * a + qh].rearrange("(c p) k -> p c k", p=128) for qh in range(8)]
        l0 = [None] * 8
        for qh in range(4):
            wts = []
            if qh == 0 and dv_prev:
                wts = [("DV", dv_prev)]
            l0[qh] = E.dma("sync", lambda e, s=srcs[qh], d=wst3[qh]: e.dma_start(out=d, in_=s),
                           waits=wts)
        fq1 = fence()
        c01 = E.op("vector", lambda e: e.tensor_tensor(wst[0], wst[0], wst[1], ALU.add),
                   waits=[("FN", fq1)], inc="DV")
        c23 = E.op("vector", lambda e: e.tensor_tensor(wst[2], wst[2], wst[3], ALU.add),
                   inc="DV")
        c02 = E.op("vector", lambda e: e.tensor_tensor(wst[0], wst[0], wst[2], ALU.add),
                   waits=[("DV", c23)], inc="DV")
        l4 = E.dma("sync", lambda e, s=srcs[4], d=wst3[1]: e.dma_start(out=d, in_=s),
                   waits=[("DV", c01)])
        l5 = E.dma("sync", lambda e, s=srcs[5], d=wst3[2]: e.dma_start(out=d, in_=s),
                   waits=[("DV", c02)])
        l6 = E.dma("sync", lambda e, s=srcs[6], d=wst3[3]: e.dma_start(out=d, in_=s))
        fq2 = fence()
        c45 = E.op("vector", lambda e: e.tensor_tensor(wst[1], wst[1], wst[2], ALU.add),
                   waits=[("FN", fq2)], inc="DV")
        l7 = E.dma("sync", lambda e, s=srcs[7], d=wst3[2]: e.dma_start(out=d, in_=s),
                   waits=[("DV", c45)])
        fq3 = fence()
        c46 = E.op("vector", lambda e: e.tensor_tensor(wst[1], wst[1], wst[3], ALU.add),
                   waits=[("DV", c45)], inc="DV")
        c47 = E.op("vector", lambda e: e.tensor_tensor(wst[1], wst[1], wst[2], ALU.add),
                   waits=[("FN", fq3), ("DV", c46)], inc="DV")
        dst = wqp.rearrange("p (a c two k) -> p a c two k", a=4, two=2, k=64)[:, a]
        strided_copy(dst[:, :, 0, :], wst3[0], [("DV", c02)])
        dv_prev = strided_copy(dst[:, :, 1, :], wst3[1], [("DV", c47)], inc="DV")
        decode_next(2)
    wq_done = dv_prev

    for a in range(NPAIR):
        lds = []
        for h in range(2):
            src = wk_in[2 * a + h].rearrange("(c p) k -> p c k", p=128)
            lds.append(E.dma("sync", lambda e, s=src, d=wst3[h]: e.dma_start(out=d, in_=s),
                             waits=[("DV", dv_prev)] if h == 0 else []))
        fkv = fence()
        dst = kp_v[:, a * 2048:(a + 1) * 2048].rearrange("p (c two k) -> p c two k", two=2, k=64)
        strided_copy(dst[:, :, 0, :], wst3[0], [("FN", fkv)])
        dv_prev = strided_copy(dst[:, :, 1, :], wst3[1], [], inc="DV")
    decode_next(2)
    for a in range(NPAIR):
        lds = []
        for h in range(2):
            src = wv_in[2 * a + h].rearrange("(c p) k -> p c k", p=128)
            lds.append(E.dma("sync", lambda e, s=src, d=wst3[h]: e.dma_start(out=d, in_=s),
                             waits=[("DV", dv_prev)] if h == 0 else []))
        fkv = fence()
        dst = vp_v.rearrange("p (c a two k) -> p c a two k", a=4, two=2, k=64)[:, :, a]
        strided_copy(dst[:, :, 0, :], wst3[0], [("FN", fkv)])
        dv_prev = strided_copy(dst[:, :, 1, :], wst3[1], [], inc="DV")
    wkv_done = dv_prev
    decode_next(2)

    q_last_ac = None
    for a in range(NPAIR):
        for sl in range(2):
            wts = []
            if a == 0 and sl == 0:
                wts = [xtq_done, ("DV", wq_done)]
            if bank_free[sl]:
                wts.append(bank_free[sl])
            for dcx in range(DC):
                c = mm(sl, wqp[:, a * 2048 + dcx * 128: a * 2048 + (dcx + 1) * 128],
                       xtbuf.ap()[:, dcx * 1024 + sl * 512: dcx * 1024 + sl * 512 + 512],
                       dcx == 0, dcx == DC - 1,
                       waits=wts if dcx == 0 else (), inc=(dcx == DC - 1))
            q_last_ac = E.op("scalar", lambda e, a=a, sl=sl, b=sl: e.activation(
                qt_v[:, a * 1024 + sl * 512: a * 1024 + sl * 512 + 512],
                PS[b].ap()[:, :], AF.Copy), waits=[("PE", c)], inc="AC")
            bank_free[sl] = ("AC", q_last_ac)
    decode_next(10)

    kv_last_ac = None
    for half in range(2):
        xt_done = None
        for i in range(8):
            t = 8 * half + i
            xt_done = build_xt_tile(x_full[128 * t:128 * (t + 1), :], xt_seq, i * 128)
            xt_seq += 1
        for a in range(NPAIR):
            for kc2 in range(2):
                kc = 2 * half + kc2
                bank = kc2
                wts = [xt_done]
                if half == 0 and a == 0 and kc2 == 0:
                    wts.append(("DV", wkv_done))
                if bank_free[bank]:
                    wts.append(bank_free[bank])
                for dcx in range(DC):
                    c = mm(bank, wk_ap(a, dcx),
                           xtbuf.ap()[:, dcx * 1024 + kc2 * 512: dcx * 1024 + kc2 * 512 + 512],
                           dcx == 0, dcx == DC - 1,
                           waits=wts if dcx == 0 else (), inc=(dcx == DC - 1))
                kv_last_ac = E.op("scalar", lambda e, a=a, kc=kc, b=bank: e.activation(
                    kt_v[:, a * 2048 + kc * 512: a * 2048 + kc * 512 + 512],
                    PS[b].ap()[:, :], AF.Copy), waits=[("PE", c)], inc="AC")
                bank_free[bank] = ("AC", kv_last_ac)
        for i in range(8):
            tt = 8 * half + i
            bank = 2 + (i % 2)
            wts = [xt_done]
            if bank_free[bank]:
                wts.append(bank_free[bank])
            for dcx in range(DC):
                c = mm(bank, xtbuf.ap()[:, dcx * 1024 + i * 128: dcx * 1024 + i * 128 + 128],
                       wv_ap(dcx), dcx == 0, dcx == DC - 1,
                       waits=wts if dcx == 0 else (), inc=(dcx == DC - 1))
            vout = v_v.rearrange("p (t g c) -> p t g c", t=16, g=8)[:, tt, :, 0:64]
            vin = PS[bank].ap().rearrange("p (g c) -> p g c", g=8)
            kv_last_ac = E.op("scalar", lambda e, o=vout, i2=vin: e.activation(o, i2, AF.Copy),
                              waits=[("PE", c)], inc="AC")
            bank_free[bank] = ("AC", kv_last_ac)
        decode_next(12)

    wo_dv = None
    for a in range(NPAIR):
        wts = [("DV", wo_dv)] if wo_dv else [("AC", q_last_ac), ("DV", wkv_done)]
        c = E.dma("sync", lambda e, s=wo_in[128 * a:128 * (a + 1), :]:
                  e.dma_start(out=eA, in_=s), waits=wts)
        fwo = fence()
        wo_dv = E.op("vector", lambda e, a=a: e.tensor_copy(
            wqp[:, a * 2048:(a + 1) * 2048], eA), waits=[("FN", fwo)], inc="DV")
    wo_done = wo_dv
    decode_flush()

    xbar_hist = []
    pe_snapshot = E.n("PE")

    def emit_xbar(src_ap, dst_ap, wait_dl):
        k = len(xbar_hist)
        bi = k % 2
        tv = xbfst.ap()[:, bi * 2048:(bi + 1) * 2048]
        war = [("DL", xbar_hist[k - 2])] if k >= 2 else [("PE", pe_snapshot)]
        ct = E.dma("sync", lambda e, o=tv, s=src_ap: e.dma_start_transpose(o, s),
                   waits=[("DL", wait_dl)] + war)
        cw = E.dma("sync", lambda e, d=dst_ap, s=tv: e.dma_start(out=d, in_=s),
                   waits=[("DL", ct)])
        xbar_hist.append(cw)
        return cw

    if BUILD_DECODE:
        for dcx in range(DC):
            emit_xbar(wg_raw[:, 128 * dcx:128 * (dcx + 1)],
                      wg_dec.rearrange("nt dc d j -> dc d nt j")[dcx], dec_raw[15])
        for dcx in range(DC):
            emit_xbar(wu_raw[:, 128 * dcx:128 * (dcx + 1)],
                      wu_dec.rearrange("nt dc d j -> dc d nt j")[dcx], dec_raw[31])
    if BUILD_DECODE:
        for ntk in range(NT):
            emit_xbar(wd_raw[:, 128 * ntk:128 * (ntk + 1)], wd_dec[ntk], dec_raw[47])
        wg_ready = xbar_hist[DC - 1]
        wu_ready = xbar_hist[2 * DC - 1]
        wd_ready = xbar_hist[-1]
    else:
        wg_ready = wu_ready = wd_ready = 0

    pt_war = {0: None, 1: None, 2: None, 3: None}
    bc_cbc = None
    bc_cev = None
    zn_last = None
    for a in range(NPAIR):
        for sl in range(2):
            ext = EXT[sl]
            for h in range(2):
                last_av = None
                for kt in range(ext):
                    sbank = kt % 4
                    wts = []
                    if a == 0 and sl == 0 and kt == 0 and h == 0:
                        wts.append(("AC", kv_last_ac))
                        wts.append(("DV", 5))
                    if bank_free[sbank]:
                        wts.append(bank_free[sbank])
                    c = mm(sbank,
                           kt_v[64 * h:64 * (h + 1),
                                a * 2048 + kt * 128: a * 2048 + (kt + 1) * 128],
                           qt_v[64 * h:64 * (h + 1),
                                a * 1024 + sl * 512: a * 1024 + sl * 512 + 512],
                           True, True, waits=wts, inc=True)
                    slot = kt % 4
                    pv = pt_v[:, slot * 512:(slot + 1) * 512]
                    ew = [("PE", c)]
                    if pt_war[slot]:
                        ew.append(("PE", pt_war[slot]))
                    ce = E.op("scalar", lambda e, o=pv, b=sbank: e.activation(
                        o, PS[b].ap()[:, :], AF.Exp, scale=0.125, bias=biasS[:, :]),
                        waits=ew, inc="AC")
                    bank_free[sbank] = ("AC", ce)
                    if (sl == 0) or (kt >= 8):
                        mv = masks_v[:, kt * 512:(kt + 1) * 512]
                        cm = E.op("vector", lambda e, p=pv, m=mv: e.tensor_tensor(
                            p, p, m, ALU.mult),
                            waits=[("AC", ce), ("FN", f_init)], inc="DV")
                        pwait = ("DV", cm)
                    else:
                        pwait = ("AC", ce)
                    vslice = v_v[:, kt * 520 + (2 * a + h) * 65:
                                 kt * 520 + (2 * a + h) * 65 + 65]
                    avw = [pwait]
                    if kt == 0 and bank_free.get(4):
                        avw.append(bank_free[4])
                    cav = mm(None, vslice, pv, kt == 0, kt == ext - 1,
                             waits=avw, inc=True, out_ap=PS[4].ap()[0:65, :])
                    pt_war[slot] = cav
                    last_av = cav
                crw = [("PE", last_av)]
                if bc_cbc:
                    crw.append(("PE", bc_cbc))
                crc = E.op("vector", lambda e: e.reciprocal(
                    rcp_sb[:, :], PS[4].ap()[64:65, :]), waits=crw, inc="DV")
                bcw = [("DV", crc)]
                if bc_cev:
                    bcw.append(("AC", bc_cev))
                bc_cbc = E.op("tensor", lambda e: e.matmul(
                    BC[:, :], ones1[:, :], rcp_sb[:, :], start=True, stop=True,
                    skip_group_check=True), waits=bcw, inc="PE")
                bc_cev = E.op("scalar", lambda e: e.activation(bc_sb[:, :], BC[:, :], AF.Copy),
                              waits=[("PE", bc_cbc)] + ([("DV", zn_last)] if zn_last else []),
                              inc="AC")
                zn_last = E.op("vector", lambda e, a=a, sl=sl, h=h: e.tensor_tensor(
                    zt_v[64 * h:64 * (h + 1),
                         a * 1024 + sl * 512: a * 1024 + sl * 512 + 512],
                    PS[4].ap()[0:64, :], bc_sb[:, :], ALU.mult),
                    waits=[("AC", bc_cev)], inc="DV")
                bank_free[4] = ("DV", zn_last)
    z_done = zn_last

    sum_hist = []
    sq_hist = []
    spill_hist = []
    ag_cc = []
    ag_cag = []
    ap_hist = []
    st_ = stats.ap()
    for i in range(8):
        bi = i % 2
        xo = eA if bi == 0 else eB
        xb16 = xmb_v[:, bi * 2048:(bi + 1) * 2048]
        xmt = dec2bf[:, bi * 2048:(bi + 1) * 2048]
        war = [("DV", wo_done)]
        if i >= 2:
            war = [("DL", spill_hist[i - 2]), ("AC", ap_hist[i - 2][0])]
        cl = E.dma("sync", lambda e, o=xo, s=x_own[128 * i:128 * (i + 1), :]:
                   e.dma_start(out=o, in_=s), waits=war)
        fxo = fence()
        cadd_last = None
        for dq in range(4):
            wts = []
            if i == 0 and dq == 0:
                wts = [("DV", z_done), ("DV", wo_done)]
            if bank_free[dq]:
                wts.append(bank_free[dq])
            for a in range(NPAIR):
                c = mm(dq, zt_v[:, a * 1024 + i * 128: a * 1024 + (i + 1) * 128],
                       wqp[:, a * 2048 + dq * 512: a * 2048 + dq * 512 + 512],
                       a == 0, a == NPAIR - 1,
                       waits=wts if a == 0 else (), inc=(a == NPAIR - 1))
            cadd_last = E.op("vector", lambda e, o=xo, dq=dq, b=dq: e.tensor_tensor(
                o[:, dq * 512:(dq + 1) * 512], PS[b].ap()[:, :],
                o[:, dq * 512:(dq + 1) * 512], ALU.add),
                waits=[("PE", c), ("FN", fxo)], inc="DV")
            bank_free[dq] = ("DV", cadd_last)
        cs = E.op("vector", lambda e, x=xo, i=i: e.tensor_reduce(
            st_[:, i:i + 1], x, axis=mybir.AxisListType.X, op=ALU.add),
            waits=[("DV", cadd_last)], inc="DV")
        sum_hist.append(cs)
        cq = E.op("scalar", lambda e, x=xo, i=i: e.activation(
            osq_scr, x, AF.Square, accum_out=st_[:, 8 + i:9 + i]),
            waits=[("DV", cadd_last)], inc="AC")
        sq_hist.append(cq)
        p1 = E.op("vector", lambda e, i=i: e.tensor_scalar(
            st_[:, 16 + i:17 + i], st_[:, i:i + 1], 1.0 / D, None, ALU.mult),
            waits=[("DV", cs), ("AC", cq)], inc="DV")
        p3 = E.op("vector", lambda e, i=i: e.tensor_tensor(
            st_[:, 32 + i:33 + i], st_[:, 16 + i:17 + i], st_[:, 16 + i:17 + i],
            ALU.mult), waits=[("DV", p1)], inc="DV")
        p4 = E.op("vector", lambda e, i=i: e.scalar_tensor_tensor(
            st_[:, 40 + i:41 + i], st_[:, 8 + i:9 + i], 1.0 / D,
            st_[:, 32 + i:33 + i], ALU.mult, ALU.subtract),
            waits=[("DV", p3)], inc="DV")
        p5 = E.op("vector", lambda e, i=i: e.tensor_scalar(
            st_[:, 40 + i:41 + i], st_[:, 40 + i:41 + i], LN_EPS, None, ALU.add),
            waits=[("DV", p4)], inc="DV")
        p6 = E.op("scalar", lambda e, i=i: e.activation(
            st_[:, 48 + i:49 + i], st_[:, 40 + i:41 + i], AF.Sqrt),
            waits=[("DV", p5)], inc="AC")
        p7 = E.op("vector", lambda e, i=i: e.reciprocal(
            st_[:, 32 + i:33 + i], st_[:, 48 + i:49 + i]), waits=[("AC", p6)], inc="DV")
        p8 = E.op("vector", lambda e, i=i: e.tensor_tensor(
            st_[:, 56 + i:57 + i], st_[:, 16 + i:17 + i], st_[:, 32 + i:33 + i],
            ALU.mult), waits=[("DV", p7)], inc="DV")
        p9 = E.op("vector", lambda e, i=i: e.tensor_scalar(
            st_[:, 56 + i:57 + i], st_[:, 56 + i:57 + i], -1.0, None, ALU.mult),
            waits=[("DV", p8)], inc="DV")
        aw = [("DV", p9)]
        if i >= 2:
            aw.append(("PE", ap_hist[i - 2][3]))
        cbf = E.op("scalar", lambda e, o=xb16, x=xo, i=i: e.activation(
            o, x, AF.Identity, scale=st_[:, 32 + i:33 + i],
            bias=st_[:, 56 + i:57 + i]), waits=aw, inc="AC")
        cf32 = E.op("scalar", lambda e, x=xo, i=i: e.activation(
            x, x, AF.Identity, scale=st_[:, 32 + i:33 + i],
            bias=st_[:, 56 + i:57 + i]), inc="AC")
        csp = E.dma("sync", lambda e, x=xo, i=i: e.dma_start(
            out=xmid_sp[128 * i:128 * (i + 1), :], in_=x), waits=[("AC", cf32)])
        spill_hist.append(csp)
        last_pe = None
        last_ev = None
        for g in range(4):
            items = []
            for u in range(4):
                pc = 4 * g + u
                if pc < 8:
                    src = xb16[:, 256 * pc: 256 * pc + 256: 2]
                else:
                    src = xb16[:, 256 * (pc - 8) + 1: 256 * (pc - 8) + 256: 2]
                items.append((src, xmt[:, pc * 128:(pc + 1) * 128]))
            evw = []
            if g == 0:
                evw = [("PL", dec_pl[47])] if (BUILD_DECODE and i == 0) else []
                if i >= 2:
                    evw.append(("FN", ap_hist[i - 2][2]))
            last_pe, last_ev = transpose_batch(
                items, [("AC", cbf)] if g == 0 else [], ev_waits=evw)
        for q4 in range(4):
            E.dma("sync", lambda e, i=i, q4=q4, s=xmt: e.dma_start(
                out=ag_in[i].rearrange("(pc d) t -> d pc t", d=128)[:, 4 * q4:4 * (q4 + 1), :],
                in_=s.rearrange("p (pc t) -> p pc t", t=128)[:, 4 * q4:4 * (q4 + 1), :]),
                waits=[last_ev] if q4 == 0 else [])
        fcag = fence()
        ag_cag.append(fcag)
        if BUILD_PHASE == "full":
            ag_cc.append(E.op("gpsimd", lambda e, i=i: e.collective_compute(
                "AllGather", ALU.bypass, replica_groups=[list(range(W))],
                ins=[ag_in[i].opt()], outs=[ag_out[i].opt()]),
                waits=[("FN", fcag)], inc="CC"))
        ap_hist.append((cf32, csp, fcag, last_pe))

    if BUILD_PHASE == "attn":
        outl = None
        for i in range(8):
            outl = E.dma("sync", lambda e, i=i: e.dma_start(
                out=y_out[128 * i:128 * (i + 1), :],
                in_=xmid_sp[128 * i:128 * (i + 1), :]),
                waits=[("DL", ap_hist[i][1])])
        E.ops["sync"].append((None, (("DL", outl),), None, 0))
        with nc.Block() as block:
            def make(engname):
                def emitter(eng):
                    last = {}
                    for fn, waits, inc, amt in E.ops[engname]:
                        for s2, v in waits:
                            if v is None or v == 0:
                                continue
                            if last.get(s2, -1) < v:
                                eng.wait_ge(sems[s2], v)
                                last[s2] = v
                        if fn is None:
                            continue
                        inst = fn(eng)
                        if inc is not None:
                            inst.then_inc(sems[inc], amt)
                return emitter
            block.sync(make("sync"))
            block.tensor(make("tensor"))
            block.vector(make("vector"))
            block.scalar(make("scalar"))
            block.gpsimd(make("gpsimd"))
        stack.close()
        return nc


    def ln_panels(c0, pw):
        x1 = E.op("vector", lambda e: e.tensor_scalar(
            st_[:, c0 + 16:c0 + 24], st_[:, c0 + 0:c0 + 8], 1.0 / D, None, ALU.mult),
            waits=pw, inc="DV")
        x2 = E.op("vector", lambda e: e.tensor_scalar(
            st_[:, c0 + 24:c0 + 32], st_[:, c0 + 8:c0 + 16], 1.0 / D, None, ALU.mult),
            inc="DV")
        x3 = E.op("vector", lambda e: e.tensor_tensor(
            st_[:, c0 + 32:c0 + 40], st_[:, c0 + 16:c0 + 24], st_[:, c0 + 16:c0 + 24],
            ALU.mult), waits=[("DV", x1)], inc="DV")
        x4 = E.op("vector", lambda e: e.tensor_tensor(
            st_[:, c0 + 40:c0 + 48], st_[:, c0 + 24:c0 + 32], st_[:, c0 + 32:c0 + 40],
            ALU.subtract), waits=[("DV", x3), ("DV", x2)], inc="DV")
        x5 = E.op("vector", lambda e: e.tensor_scalar(
            st_[:, c0 + 40:c0 + 48], st_[:, c0 + 40:c0 + 48], LN_EPS, None, ALU.add),
            waits=[("DV", x4)], inc="DV")
        x6 = E.op("scalar", lambda e: e.activation(
            st_[:, c0 + 48:c0 + 56], st_[:, c0 + 40:c0 + 48], AF.Sqrt),
            waits=[("DV", x5)], inc="AC")
        x7 = E.op("vector", lambda e: e.reciprocal(
            st_[:, c0 + 32:c0 + 40], st_[:, c0 + 48:c0 + 56]),
            waits=[("AC", x6)], inc="DV")
        x8 = E.op("vector", lambda e: e.tensor_tensor(
            st_[:, c0 + 56:c0 + 64], st_[:, c0 + 16:c0 + 24], st_[:, c0 + 32:c0 + 40],
            ALU.mult), waits=[("DV", x7)], inc="DV")
        x9 = E.op("vector", lambda e: e.tensor_scalar(
            st_[:, c0 + 56:c0 + 64], st_[:, c0 + 56:c0 + 64], -1.0, None, ALU.mult),
            waits=[("DV", x8)], inc="DV")
        return x9

    p4_hist = []

    def emit_p4a(j):
        wts = [("CC", rs_cc[j])]
        if p4_hist:
            wts += [("DL", p4_hist[-1][2]), ("AC", p4_hist[-1][3])]
        c1 = E.dma("sync", lambda e, j=j: e.dma_start(out=p4a, in_=rs_out[j][:, :]),
                   waits=wts)
        c2 = E.dma("sync", lambda e, j=j: e.dma_start(
            out=p4b, in_=xmid_sp[128 * j:128 * (j + 1), :]),
            waits=([("DV", p4_hist[-1][4])] if p4_hist else []) + [("DL", ap_hist[j][1])])
        fp4 = fence()
        cadd = E.op("vector", lambda e: e.tensor_tensor(p4a, p4a, p4b, ALU.add),
                    waits=[("FN", fp4)], inc="DV")
        cs = E.op("vector", lambda e, j=j: e.tensor_reduce(
            stats.ap()[:, 64 + j:65 + j], p4a, axis=mybir.AxisListType.X, op=ALU.add),
            waits=[("DV", cadd)], inc="DV")
        cq = E.op("scalar", lambda e, j=j: e.activation(
            eB, p4a, AF.Square, accum_out=stats.ap()[:, 72 + j:73 + j]),
            waits=[("DV", cadd)], inc="AC")
        c3 = E.dma("sync", lambda e, j=j: e.dma_start(
            out=fsum_sp[128 * j:128 * (j + 1), :], in_=p4a),
            waits=[("DV", cs), ("AC", cq)])
        p4_hist.append((c1, c2, c3, cq, cs))

    gate_pe = {}
    up_pe = {}
    t1_pl = {}
    inter_dv = {}
    down_pe = {}
    rs_dl = {}
    rs_cc = []
    rs_fence = {}
    rs_chunk_fence = {}
    wg_dl = {}
    wu_dl = {}
    wd_dl = {}
    xm_dl = {}

    fxm_d = {}
    fgu = {}

    def emit_xm_loads(j):
        for r in range(W):
            wts = []
            if r == 0:
                wts.append(("CC", ag_cc[j]))
                if j > 0:
                    wts.append(("PE", up_pe[(j * NT - 1, 1)]))
            src = ag_out[j].rearrange("(r dc d) t -> r d dc t", r=W, d=128)[r]
            dst = xm_v.rearrange("p (dc t) -> p dc t", t=1024)[:, :, 128 * r:128 * (r + 1)]
            for q4 in range(4):
                xm_dl[(j, r)] = E.dma("sync", lambda e, q4=q4, s=src, d=dst: e.dma_start(
                    out=d[:, 4 * q4:4 * (q4 + 1), :], in_=s[:, 4 * q4:4 * (q4 + 1), :]),
                    waits=wts if q4 == 0 else [])
        fxm_d[j] = fence()

    def emit_gu_load(j, nt2):
        gi2 = j * NT + nt2
        bi2 = gi2 % 2
        wts = [("DL", wg_ready)] if j == 0 else []
        if gi2 >= 2:
            wts.append(("PE", gate_pe[(gi2 - 2, 1)]))
        wg_dl[gi2] = E.dma(
            "sync",
            lambda e, o=wgs_v[:, bi2 * 2048:(bi2 + 1) * 2048].rearrange("p (dc j2) -> p dc j2", j2=128),
            s=wg_dec[nt2].rearrange("dc d j2 -> d dc j2"):
            e.dma_start(out=o, in_=s), waits=wts)
        wts = [("DL", wu_ready)] if j == 0 else []
        if gi2 >= 2:
            wts.append(("PE", up_pe[(gi2 - 2, 1)]))
        wu_dl[gi2] = E.dma(
            "sync",
            lambda e, o=wus_v[:, bi2 * 2048:(bi2 + 1) * 2048].rearrange("p (dc j2) -> p dc j2", j2=128),
            s=wu_dec[nt2].rearrange("dc d j2 -> d dc j2"):
            e.dma_start(out=o, in_=s), waits=wts)
        fgu[gi2] = fence()

    emit_xm_loads(0)
    emit_gu_load(0, 0)
    emit_gu_load(0, 1)
    for j in range(NCHUNK):
        if j >= 2:
            emit_p4a(j - 2)
        fxm = fxm_d[j]
        fwd = {}

        def emit_wd_loads(hh):
            for ntk in range(NT):
                wts = [("DL", wd_ready)] if j == 0 else []
                if ntk == 0:
                    if hh == 1:
                        wts.append(("PE", down_pe[(j, 0, 7)]))
                    elif j > 0:
                        wts.append(("PE", down_pe[(j - 1, 1, 7)]))
                wd_dl[(j, hh, ntk)] = E.dma(
                    "sync", lambda e, o=wd_v[:, ntk * 1024:(ntk + 1) * 1024],
                    s=wd_dec[ntk][:, hh * 1024:(hh + 1) * 1024]:
                    e.dma_start(out=o, in_=s), waits=wts)
            fwd[hh] = fence()

        for nt in range(NT):
            gi = j * NT + nt
            bi = gi % 2
            for tq in range(2):
                gbank, ubank = tq, 2 + tq
                prev = (gi - 1, tq)
                wts = [("FN", fgu[gi])]
                if nt == 0 and tq == 0:
                    wts.append(("FN", fxm))
                if bank_free[gbank]:
                    wts.append(bank_free[gbank])
                for dcx in range(DC):
                    c = mm(gbank, wgs_v[:, bi * 2048 + dcx * 128: bi * 2048 + (dcx + 1) * 128],
                           xm_v[:, dcx * 1024 + tq * 512: dcx * 1024 + tq * 512 + 512],
                           dcx == 0, dcx == DC - 1,
                           waits=wts if dcx == 0 else (), inc=(dcx == DC - 1))
                gate_pe[(gi, tq)] = c
                ggv = gg_v[:, tq * 512:(tq + 1) * 512]
                g2v = g2_v[:, tq * 512:(tq + 1) * 512]
                tv = t1_v[:, tq * 512:(tq + 1) * 512]
                ggw = [("PE", c)]
                if prev in t1_pl:
                    ggw.append(("PL", t1_pl[prev]))
                cgg = E.op("scalar", lambda e, o=ggv, b=gbank: e.activation(
                    o, PS[b].ap()[:, :], AF.Copy), waits=ggw, inc="AC")
                bank_free[gbank] = ("AC", cgg)
                cg2 = E.op("scalar", lambda e, o=g2v, i2=ggv: e.activation(
                    o, i2, AF.Square), inc="AC")
                cw1 = E.op("gpsimd", lambda e, o=g2v: e.tensor_scalar(
                    o, o, GC1, GC2, ALU.mult, ALU.add), waits=[("AC", cg2)], inc="PL")
                car = E.op("gpsimd", lambda e, o=g2v, g=ggv: e.tensor_tensor(
                    o, g, o, ALU.mult), inc="PL")
                csg = E.op("scalar", lambda e, o=g2v: e.activation(
                    o, o, AF.Sigmoid), waits=[("PL", car)], inc="AC")
                t1w = [("AC", csg)]
                if j == 0 and nt == 0 and tq == 0 and BUILD_PHASE == "full":
                    t1w.append(("FN", ap_hist[7][2]))
                if prev in inter_dv:
                    t1w.append(("DV", inter_dv[prev]))
                ct1 = E.op("gpsimd", lambda e, o=tv, g=ggv, s2=g2v: e.tensor_tensor(
                    o, g, s2, ALU.mult), waits=t1w, inc="PL")
                t1_pl[(gi, tq)] = ct1
                wts = [("FN", fgu[gi])]
                if bank_free[ubank]:
                    wts.append(bank_free[ubank])
                for dcx in range(DC):
                    c2 = mm(ubank, wus_v[:, bi * 2048 + dcx * 128: bi * 2048 + (dcx + 1) * 128],
                            xm_v[:, dcx * 1024 + tq * 512: dcx * 1024 + tq * 512 + 512],
                            dcx == 0, dcx == DC - 1,
                            waits=wts if dcx == 0 else (), inc=(dcx == DC - 1))
                up_pe[(gi, tq)] = c2
                ivw = [("PL", ct1), ("PE", c2)]
                if j > 0 and nt == 0 and tq == 0:
                    ivw.append(("PE", down_pe[(j - 1, 1, 7)]))
                cin = E.op("vector", lambda e, nt=nt, tq=tq, b=ubank, t=tv: e.tensor_tensor(
                    inter_v[:, nt * 1024 + tq * 512: nt * 1024 + tq * 512 + 512],
                    t, PS[b].ap()[:, :], ALU.mult), waits=ivw, inc="DV")
                inter_dv[(gi, tq)] = cin
                bank_free[ubank] = ("DV", cin)
            if nt + 2 < NT:
                emit_gu_load(j, nt + 2)
            if nt == 12:
                emit_wd_loads(0)
        if j + 1 < NCHUNK:
            emit_xm_loads(j + 1)
            emit_gu_load(j + 1, 0)
            emit_gu_load(j + 1, 1)
        inter_done = inter_dv[(j * NT + NT - 1, 1)]
        for hh in range(2):
            if hh == 1:
                emit_wd_loads(1)
            for tt in range(8):
                bk = (tt % 2) * 2
                wts0 = [("DV", inter_done), ("FN", fwd[hh])]
                for hc in range(2):
                    if bank_free[bk + hc]:
                        wts0.append(bank_free[bk + hc])
                first = True
                c = None
                for ntk in range(NT):
                    lhs = inter_v[:, ntk * 1024 + tt * 128: ntk * 1024 + (tt + 1) * 128]
                    for hc in range(2):
                        rhs = wd_v[:, ntk * 1024 + hc * 512: ntk * 1024 + hc * 512 + 512]
                        c = mm(bk + hc, lhs, rhs, ntk == 0, ntk == NT - 1,
                               waits=wts0 if first else (), inc=(ntk == NT - 1))
                        first = False
                down_pe[(j, hh, tt)] = c
                step = hh * 8 + tt
                bi = step % 4
                rv = rsst_v[:, bi * 1024:(bi + 1) * 1024]
                evw = [("PE", c)]
                if rs_fence.get((j, hh, tt)):
                    evw.append(("FN", rs_fence[(j, hh, tt)]))
                E.op("scalar", lambda e, o=rv, b=bk: e.activation(
                    o[:, 0:512], PS[b].ap()[:, :], AF.Copy), waits=evw, inc="AC")
                ca2 = E.op("scalar", lambda e, o=rv, b=bk: e.activation(
                    o[:, 512:1024], PS[b + 1].ap()[:, :], AF.Copy), inc="AC")
                bank_free[bk] = ("AC", ca2)
                bank_free[bk + 1] = ("AC", ca2)
                rs_dl[(j, hh, tt)] = E.dma(
                    "sync", lambda e, j=j, hh=hh, tt=tt, s=rv:
                    e.dma_start(out=rs_in[j][128 * tt:128 * (tt + 1),
                                             hh * 1024:(hh + 1) * 1024], in_=s),
                    waits=[("AC", ca2)])
                if step % 4 == 3:
                    fr = fence()
                    for ahead in range(4):
                        st2 = step + 1 + ahead
                        jj, hh2, tt2 = (j, st2 // 8, st2 % 8) if st2 < 16 else                             (j + 1, (st2 - 16) // 8, (st2 - 16) % 8)
                        rs_fence[(jj, hh2, tt2)] = fr
                    if step == 15:
                        rs_chunk_fence[j] = fr
        rs_cc.append(E.op("gpsimd", lambda e, j=j: e.collective_compute(
            "ReduceScatter", ALU.add, replica_groups=[list(range(W))],
            ins=[rs_in[j].opt()], outs=[rs_out[j].opt()]),
            waits=[("FN", rs_chunk_fence[j])], inc="CC"))

    emit_p4a(NCHUNK - 2)
    emit_p4a(NCHUNK - 1)

    ln2_dv = ln_panels(64, [("AC", p4_hist[-1][3]), ("DV", p4_hist[-1][4])])

    app_prev = None
    out_last = None
    for j in range(NCHUNK):
        wts = [("DL", p4_hist[-1][2]), ("AC", p4_hist[-1][3])] if app_prev is None \
            else [("AC", app_prev[1])]
        c1 = E.dma("sync", lambda e, j=j: e.dma_start(
            out=p4a, in_=fsum_sp[128 * j:128 * (j + 1), :]), waits=wts)
        fp4b = fence()
        aw = [("FN", fp4b), ("DV", ln2_dv)]
        ca = E.op("scalar", lambda e, j=j: e.activation(
            p4b, p4a, AF.Identity, scale=stats.ap()[:, 96 + j:97 + j],
            bias=stats.ap()[:, 120 + j:121 + j]), waits=aw, inc="AC")
        out_last = E.dma("sync", lambda e, j=j: e.dma_start(
            out=y_out[128 * j:128 * (j + 1), :], in_=p4b), waits=[("AC", ca)])
        app_prev = (c1, ca, out_last)

    if BUILD_DBG:
        dbg1 = nc.dram_tensor("dbg_xmid", [TOWN, D], F32, kind="ExternalOutput").ap()
        dbg2 = nc.dram_tensor("dbg_rs", [TOWN, D], F32, kind="ExternalOutput").ap()
        dbg3 = nc.dram_tensor("dbg_fsum", [TOWN, D], F32, kind="ExternalOutput").ap()
        for j in range(NCHUNK):
            out_last = E.dma("sync", lambda e, j=j: e.dma_start(
                out=dbg1[128 * j:128 * (j + 1), :], in_=xmid_sp[128 * j:128 * (j + 1), :]),
                waits=[("DL", out_last)])
            out_last = E.dma("sync", lambda e, j=j: e.dma_start(
                out=dbg2[128 * j:128 * (j + 1), :], in_=rs_out[j][:, :]))
            out_last = E.dma("sync", lambda e, j=j: e.dma_start(
                out=dbg3[128 * j:128 * (j + 1), :], in_=fsum_sp[128 * j:128 * (j + 1), :]))
    E.ops["sync"].append((None, (("DL", out_last),), None, 0))

    if DEBUG_HOOK is not None:
        DEBUG_HOOK(dict(nc=nc, E=E, sems=sems, loc=locals()))

    with nc.Block() as block:
        def make(engname):
            def emitter(eng):
                last = {}
                for fn, waits, inc, amt in E.ops[engname]:
                    for s2, v in waits:
                        if v is None or v == 0:
                            continue
                        if last.get(s2, -1) < v:
                            eng.wait_ge(sems[s2], v)
                            last[s2] = v
                    if fn is None:
                        continue
                    inst = fn(eng)
                    if inc is not None:
                        inst.then_inc(sems[inc], amt)
            return emitter

        block.sync(make("sync"))
        block.tensor(make("tensor"))
        block.vector(make("vector"))
        block.scalar(make("scalar"))
        block.gpsimd(make("gpsimd"))

    stack.close()
    return nc


_PROG_CACHE = {}


def _get_prog():
    if "nc" not in _PROG_CACHE:
        _PROG_CACHE["nc"] = build_program()
    return _PROG_CACHE["nc"]


def _host_inputs(inputs):
    x = np.asarray(inputs["L0_attn_normalized_resid_pre"])
    wq = np.ascontiguousarray(np.asarray(inputs["L0_attn_W_Q"], dtype=np.float32))
    wk = np.ascontiguousarray(np.asarray(inputs["L0_attn_W_K"], dtype=np.float32))
    wv = np.ascontiguousarray(np.asarray(inputs["L0_attn_W_V"], dtype=np.float32))
    wo = np.ascontiguousarray(
        np.asarray(inputs["L0_attn_W_O"], dtype=np.float32).reshape(NKV * DH, D))
    gw = np.asarray(inputs["L0_ffn_gate_W_packed"], dtype=np.int32)
    gs = np.asarray(inputs["L0_ffn_gate_scales"], dtype=np.float32)
    uw = np.asarray(inputs["L0_ffn_up_W_packed"], dtype=np.int32)
    us = np.asarray(inputs["L0_ffn_up_scales"], dtype=np.float32)
    dw = np.asarray(inputs["L0_ffn_down_W_packed"], dtype=np.int32)
    ds = np.asarray(inputs["L0_ffn_down_scales"], dtype=np.float32)
    ident = np.eye(128, dtype=np.float32).astype(ml_dtypes.bfloat16)

    in_maps = []
    for c in range(W):
        b, p = c // 2, c % 2
        qcs = (0, 3) if p == 0 else (1, 2)
        x_full_c = np.ascontiguousarray(x[b], dtype=np.float32)
        x_own_c = np.ascontiguousarray(
            np.concatenate([x[b, 512 * qc:512 * (qc + 1)] for qc in qcs], axis=0),
            dtype=np.float32)
        sl = np.arange(NSL * c, NSL * (c + 1))
        rows = np.concatenate([sl[0::2], sl[1::2]])
        m = np.zeros((16, 128, 512), dtype=np.float32)
        kk = np.arange(128)[:, None]
        qq = np.arange(512)[None, :]
        for kt in range(8):
            m[kt] = (128 * kt + kk <= 512 * qcs[0] + qq)
        for kt in range(8, 16):
            m[kt] = (128 * kt + kk <= 512 * qcs[1] + qq)
        in_maps.append({
            "x_full": x_full_c, "x_own": x_own_c,
            "wq": wq, "wk": wk, "wv": wv, "wo": wo,
            "gpk": np.ascontiguousarray(gw[rows]),
            "gsc": np.ascontiguousarray(gs[rows]),
            "upk": np.ascontiguousarray(uw[rows]),
            "usc": np.ascontiguousarray(us[rows]),
            "dpk": np.ascontiguousarray(dw[:, 1024 * c:1024 * (c + 1)]),
            "dsc": np.ascontiguousarray(ds[:, 64 * c:64 * (c + 1)]),
            "masks": m.astype(ml_dtypes.bfloat16), "ident": ident,
        })
    return in_maps


def kernel(**inputs):
    nc = _get_prog()
    in_maps = _host_inputs(inputs)
    res = run_bass_kernel_spmd(nc, in_maps, core_ids=list(range(W)))
    out = np.empty((B, S, D), dtype=np.float32)
    for c in range(W):
        b, p = c // 2, c % 2
        qcs = (0, 3) if p == 0 else (1, 2)
        yc = res.results[c]["y"]
        out[b, 512 * qcs[0]:512 * (qcs[0] + 1)] = yc[0:512]
        out[b, 512 * qcs[1]:512 * (qcs[1] + 1)] = yc[512:1024]
    return out


if __name__ == "__main__":
    build_program()
    print("program built ok")
